# version 02
# baseline (speedup 1.0000x reference)
import numpy as np
import ml_dtypes
import concourse.bass as bass
import concourse.mybir as mybir
from concourse.bass_utils import run_bass_kernel_spmd
from contextlib import ExitStack

F32 = mybir.dt.float32
BF16 = mybir.dt.bfloat16
AF = mybir.ActivationFunctionType
ALU = mybir.AluOpType
AX = mybir.AxisListType
BF = ml_dtypes.bfloat16

NCORES = 8
D = 1024
NP = 8
SEQ = 2048
RPC = 2
T = RPC * SEQ
TC = 512
NTC = T // TC
CPR = SEQ // TC
BIG_NEG = -1e30
DEBUG_TAPS = False
SAME_ENGINE = True
L0F_TRUNC = None
ENGINES = ("sync", "tensor", "vector", "scalar", "gpsimd")
DMA_Q = {"sync": "qS", "scalar": "qC", "gpsimd": "qG"}


NDMASEM = 72
NSW = 20


class Prog:
    def __init__(self, nc, stack):
        self.nc = nc
        self.ops = {e: [] for e in ENGINES}
        self.hdl = {}
        for name in ("eT", "eV", "eS", "eG", "eY", "PH"):
            self.hdl[name] = stack.enter_context(nc.semaphore(name))
        for i in range(NDMASEM):
            self.hdl[f"d{i}"] = stack.enter_context(nc.semaphore(f"d{i}"))
        self.esem = {"tensor": "eT", "vector": "eV", "scalar": "eS",
                     "gpsimd": "eG", "sync": "eY"}
        self.last_w = {}
        self.readers = {}
        self.hw_free = [f"d{i}" for i in range(NDMASEM - NSW)]
        self.sw_free = [f"d{i}" for i in range(NDMASEM - NSW, NDMASEM)]
        self.dma_map = {}
        self.dma_cnt = {f"d{i}": 0 for i in range(NDMASEM)}
        self.dma_last = {}
        self.dma_sem_last = {}
        self.sw_used = set()
        self.ph = 0

    def op(self, eng, build, r=(), w=(), deps=(), dma=False):
        idx = len(self.ops[eng])
        dep = set(deps)
        for t in r:
            lw = self.last_w.get(t)
            if lw is not None:
                dep.add(lw)
            self.readers.setdefault(t, []).append((eng, idx))
        for t in w:
            lw = self.last_w.get(t)
            if lw is not None:
                dep.add(lw)
            for rd in self.readers.get(t, ()):
                dep.add(rd)
            self.last_w[t] = (eng, idx)
            self.readers[t] = []
        dep.discard((eng, idx))
        kept = set()
        for e2, i2 in dep:
            if self.ops[e2][i2]["dma"]:
                kept.add((e2, i2))
            elif e2 != eng or (SAME_ENGINE and eng in ("vector", "scalar", "gpsimd")):
                kept.add((e2, i2))
        byeng = {}
        dmadep = set()
        for e2, i2 in kept:
            if self.ops[e2][i2]["dma"]:
                dmadep.add((e2, i2))
            else:
                byeng[e2] = max(byeng.get(e2, -1), i2)
        self.ops[eng].append({"b": build, "dep": byeng, "ddep": dmadep, "dma": dma})
        return (eng, idx)

    def dma(self, queue, build, r=(), w=(), deps=()):
        assert w, "dma op needs a primary write token"
        key = (queue == "gpsimd", w[0])
        deps = list(deps)
        sem = self.dma_map.get(key)
        if sem is None:
            pool = self.sw_free if queue == "gpsimd" else self.hw_free
            assert pool, "out of dma semaphores (add a barrier)"
            sem = pool.pop()
            self.dma_map[key] = sem
            prev = self.dma_sem_last.get(sem)
            if prev is not None:
                deps.append(prev)
        if queue == "gpsimd":
            self.sw_used.add(sem)
        eng, idx = self.op(queue, build, r=r, w=w, deps=deps, dma=True)
        self.dma_cnt[sem] += 16
        o = self.ops[eng][idx]
        o["dsem"] = sem
        o["dcnt"] = self.dma_cnt[sem]
        self.dma_last[key] = (eng, idx)
        self.dma_sem_last[sem] = (eng, idx)
        return (eng, idx)

    def last_real(self, eng):
        ops = self.ops[eng]
        for i in range(len(ops) - 1, -1, -1):
            if ops[i]["b"] is not None:
                return i
        return -1

    def barrier(self):
        snap = {e: self.last_real(e) for e in ENGINES}
        dsnap = set(self.dma_last.values())
        for e in ENGINES:
            deps = {(e2, i2) for e2, i2 in snap.items() if e2 != e and i2 >= 0}
            deps |= set(dsnap)
            self.op(e, None, deps=deps)
        self.dma_map.clear()
        self.dma_last.clear()
        self.hw_free = [f"d{i}" for i in range(NDMASEM - NSW)]
        self.sw_free = [f"d{i}" for i in range(NDMASEM - NSW, NDMASEM)]
        self.last_w.clear()
        self.readers.clear()

    def finalize_and_emit(self, block):
        needed = {e: set() for e in ENGINES}
        for e in ENGINES:
            for o in self.ops[e]:
                for e2, i2 in o["dep"].items():
                    assert self.ops[e2][i2]["b"] is not None
                    assert not self.ops[e2][i2]["dma"]
                    needed[e2].add(i2)
        cum = {}
        for e in ENGINES:
            cn, c = [], 0
            for i, o in enumerate(self.ops[e]):
                if (not o["dma"]) and i in needed[e]:
                    c += 1
                cn.append(c)
            cum[e] = cn
            assert c < 60000, (e, c)
        for s, v in self.dma_cnt.items():
            assert v < 900000, (s, v)

        def runner(e):
            ops, need, esem = self.ops[e], needed[e], self.hdl[self.esem[e]]
            hw = {}

            def f(eng):
                for i, o in enumerate(ops):
                    for s in o.get("hwreset", ()):
                        hw.pop(s, None)
                    waits = [(self.esem[e2], cum[e2][i2])
                             for e2, i2 in o["dep"].items()]
                    waits += [(self.ops[e2][i2]["dsem"], self.ops[e2][i2]["dcnt"])
                              for e2, i2 in o["ddep"]]
                    for s, v in sorted(waits):
                        if v > 0 and hw.get(s, 0) < v:
                            eng.wait_ge(self.hdl[s], v)
                            hw[s] = v
                    if o["b"] is not None:
                        ins = o["b"](eng)
                        if o["dma"]:
                            ins.then_inc(self.hdl[o["dsem"]], 16)
                        elif i in need:
                            ins.then_inc(esem, 1)
            return f

        block.sync(runner("sync"))
        block.tensor(runner("tensor"))
        block.vector(runner("vector"))
        block.scalar(runner("scalar"))
        block.gpsimd(runner("gpsimd"))


class Carver:
    def __init__(self, ap, size):
        self.ap, self.size, self.off = ap, size, 0

    def bf(self, nelem):
        nelem = (nelem + 31) & ~31
        assert self.off + nelem <= self.size, (self.off, nelem)
        ap = self.ap[:, self.off:self.off + nelem]
        self.off += nelem
        return ap

    def f32(self, nelem):
        return self.bf(2 * nelem).bitcast(F32)

    def reset(self):
        self.off = 0


def _cc(a):
    return np.ascontiguousarray(a)


def _col_tiles(b, np_=NP):
    return _cc(np.asarray(b, np.float32).reshape(np_, 128).T)


def prep_inputs(inputs):
    g = {}
    g["ones_col_bf"] = np.ones((128, 1), dtype=BF)
    g["ones_row_f32"] = np.ones((1, 128), np.float32)
    g["ones_row_bf"] = np.ones((1, 128), dtype=BF)
    g["ident_bf"] = np.eye(128, dtype=np.float32).astype(BF)
    g["ident_f32"] = np.eye(128, dtype=np.float32)
    g["l0a_wqT"] = _cc(np.asarray(inputs["L0_attn_Wq"], np.float32).T)
    g["l0a_wkT"] = _cc(np.asarray(inputs["L0_attn_Wk"], np.float32).T)
    g["l0a_wvT"] = _cc(np.asarray(inputs["L0_attn_Wv"], np.float32).T)
    g["l0a_woT"] = _cc(np.asarray(inputs["L0_attn_Wo"], np.float32).T)
    g["l0a_bq"] = _col_tiles(inputs["L0_attn_bq"])
    g["l0a_bo"] = _col_tiles(inputs["L0_attn_bo"])
    g["l0a_bk_row"] = _cc(np.asarray(inputs["L0_attn_bk"], np.float32)[None, :])
    g["l0a_bv_row"] = _cc(np.asarray(inputs["L0_attn_bv"], np.float32)[None, :])
    g["l1a_wgT"] = _cc(np.asarray(inputs["L1_attn_Wg"], np.float32).T)
    g["l1a_wcT"] = _cc(np.asarray(inputs["L1_attn_Wc"], np.float32).T)
    g["l1a_w1T"] = _cc(np.asarray(inputs["L1_attn_w1"], np.float32).T)
    g["l1a_w2T"] = _cc(np.asarray(inputs["L1_attn_w2"], np.float32).T)
    g["l1a_w3T"] = _cc(np.asarray(inputs["L1_attn_w3"], np.float32).T)
    g["l1a_lnw"] = _col_tiles(inputs["L1_attn_ln_w"])
    g["l1a_ffnw"] = _col_tiles(inputs["L1_attn_ffn_w"])
    DH, W0 = 64, 512
    perm = np.empty(DH, dtype=np.int64)
    perm[0::2] = np.arange(32)
    perm[1::2] = np.arange(32, 64)
    qk_perm = np.concatenate([h * DH + perm for h in range(8)])
    half = np.arange(0, DH, 2, dtype=np.float64)
    inv_freq = 1.0 / 10000.0 ** (half / DH)
    t_ = np.arange(SEQ, dtype=np.float64)
    freqs = np.concatenate([t_[:, None] * inv_freq[None, :]] * 2, axis=1)
    base = (half + 0.4 * DH) / (1.4 * DH)
    power = (t_ - SEQ // 2) / (W0 // 2)
    scale = np.concatenate([base[None, :] ** power[:, None]] * 2, axis=1)
    sign = np.ones(DH)
    sign[:32] = -1.0
    A_q = (np.cos(freqs) * scale) * 8.0
    B_q = (np.sin(freqs) * scale) * sign[None, :] * 8.0
    A_k = np.cos(freqs) / scale
    B_k = (np.sin(freqs) / scale) * sign[None, :]
    for nm, M in (("Aq", A_q), ("Bq", B_q), ("Ak", A_k), ("Bk", B_k)):
        g["l0f_" + nm] = _cc(np.tile(M[:, perm].T.astype(np.float32), (2, 1)))
    for qt in range(4):
        qi = (qt * 128 + np.arange(128))[:, None]
        kj = np.arange(512)[None, :]
        mprev = np.where(kj >= qi, 0.0, BIG_NEG)
        mown = np.where(kj <= qi, 0.0, BIG_NEG)
        g[f"l0f_mask{qt}"] = np.concatenate([mprev, mown], axis=1).astype(BF)
    g["l0f_mask_all"] = np.full((128, 512), BIG_NEG, np.float32).astype(BF)
    hm2 = np.zeros((128, 2), np.float32)
    hm2[0:64, 0] = 1.0
    hm2[64:128, 1] = 1.0
    g["l0f_hm2"] = hm2.astype(BF)
    g["l0f_hbc"] = _cc(hm2.T)
    wqkv = np.asarray(inputs["L0_ffn_wqkv"], np.float32)
    colperm = np.concatenate([qk_perm, 512 + qk_perm, np.arange(1024, 1536)])
    HIDP = 2816
    for l in range(2):
        g[f"l0f_wqkv{l}"] = _cc(wqkv[l][:, colperm])
        g[f"l0f_wo{l}"] = _cc(np.asarray(inputs["L0_ffn_wo"], np.float32)[l])
        qs = np.asarray(inputs["L0_ffn_q_scale"], np.float32)[l][perm]
        ks = np.asarray(inputs["L0_ffn_k_scale"], np.float32)[l][perm]
        g[f"l0f_qs{l}"] = _cc(np.tile(qs, 2)[:, None])
        g[f"l0f_ks{l}"] = _cc(np.tile(ks, 2)[:, None])
        g[f"l0f_lng{l}"] = _col_tiles(np.asarray(inputs["L0_ffn_ln_g"], np.float32)[l])
        g[f"l0f_lnb{l}"] = _col_tiles(np.asarray(inputs["L0_ffn_ln_b"], np.float32)[l])
        w1 = np.asarray(inputs["L0_ffn_w1"], np.float32)[l]
        b1 = np.asarray(inputs["L0_ffn_b1"], np.float32)[l]
        w1a = np.zeros((1024, HIDP), np.float32)
        w1g = np.zeros((1024, HIDP), np.float32)
        w1a[:, :2730] = w1[:, :2730]
        w1g[:, :2730] = w1[:, 2730:]
        b1a = np.zeros(HIDP, np.float32)
        b1g = np.zeros(HIDP, np.float32)
        b1a[:2730] = b1[:2730]
        b1g[:2730] = b1[2730:]
        g[f"l0f_w1a{l}"] = w1a
        g[f"l0f_w1g{l}"] = w1g
        g[f"l0f_b1a{l}"] = _col_tiles(b1a, 22)
        g[f"l0f_b1g{l}"] = _col_tiles(b1g, 22)
        w2 = np.asarray(inputs["L0_ffn_w2"], np.float32)[l]
        w2p = np.zeros((HIDP, 1024), np.float32)
        w2p[:2730] = w2
        g[f"l0f_w2{l}"] = w2p
        g[f"l0f_b2{l}"] = _col_tiles(np.asarray(inputs["L0_ffn_b2"], np.float32)[l])
    g["l1f_rwT"] = _cc(np.asarray(inputs["L1_ffn_router_w"], np.float32).T)
    g["l1f_rb"] = _cc(np.asarray(inputs["L1_ffn_router_b"], np.float32)[None, :])
    gw = np.asarray(inputs["L1_ffn_gate_w"], np.float32)
    uw = np.asarray(inputs["L1_ffn_up_w"], np.float32)
    dw = np.asarray(inputs["L1_ffn_down_w"], np.float32)
    g["l1f_gwT"] = _cc(gw.transpose(0, 2, 1).reshape(8 * 1024, 4096))
    g["l1f_uwT"] = _cc(uw.transpose(0, 2, 1).reshape(8 * 1024, 4096))
    g["l1f_dwT"] = _cc(dw.transpose(0, 2, 1).reshape(8 * 4096, 1024))
    x = np.asarray(inputs["L0_attn_x"], np.float32)
    shards = [_cc(x[RPC * c: RPC * c + RPC].transpose(2, 0, 1).reshape(D, T))
              for c in range(NCORES)]
    return g, shards


SLOT_INPUTS = {
    "l0a": ("l0a_wqT", "l0a_wkT", "l0a_wvT", "l0a_woT",
            "l0a_bq", "l0a_bo", "l0a_bk_row", "l0a_bv_row"),
    "l1a": ("l1a_wgT", "l1a_wcT", "l1a_w1T", "l1a_w2T", "l1a_w3T",
            "l1a_lnw", "l1a_ffnw"),
    "l1f": ("l1f_rwT", "l1f_rb", "l1f_gwT", "l1f_uwT", "l1f_dwT"),
    "l0f": tuple(["l0f_Aq", "l0f_Bq", "l0f_Ak", "l0f_Bk", "l0f_mask_all",
                  "l0f_hm2", "l0f_hbc"]
                 + [f"l0f_mask{qt}" for qt in range(4)]
                 + [f"l0f_{nm}{l}" for l in range(2)
                    for nm in ("wqkv", "wo", "qs", "ks", "lng", "lnb",
                               "w1a", "w1g", "b1a", "b1g", "w2", "b2")]),
}
CONST_INPUTS = ("ones_col_bf", "ones_row_f32", "ones_row_bf", "ident_bf", "ident_f32")


class Ctx:
    pass


def setup_ctx(P, nc, stack, io):
    C = Ctx()
    C.P, C.nc, C.io = P, nc, io
    sb = lambda n, s, d: stack.enter_context(nc.sbuf_tensor(n, s, d))
    ps = lambda n, s, d: stack.enter_context(nc.psum_tensor(n, s, d))
    C.pb = [ps(f"PB{i}", [128, 512], F32)[:] for i in range(7)]
    C.pt = ps("PT", [128, 1024], BF16)[:]
    ar = sb("ARENA", [128, 82944], BF16)
    C.cv = Carver(ar[:], 82944)
    C.rbuf = [sb(f"RBUF{i}", [128, NP * TC], F32)[:] for i in range(2)]
    C.rows = {n: sb("row_" + n, [1, TC], F32)[:]
              for n in ("mu", "ms", "tmp", "rstd", "nmr", "x1")}
    C.ones_col_bf = sb("c_ones_col", [128, 1], BF16)[:]
    C.ones_row_f32 = sb("c_ones_row", [1, 128], F32)[:]
    C.ones_row_bf = sb("c_ones_row_bf", [1, 128], BF16)[:]
    C.ident_bf = sb("c_ident_bf", [128, 128], BF16)[:]
    C.ident_f32 = sb("c_ident_f32", [128, 128], F32)[:]
    C.smalls = sb("SMALLS", [128, 64], F32)[:]
    C.smalls_used = 0
    for name, dst in (("ones_col_bf", C.ones_col_bf), ("ones_row_f32", C.ones_row_f32),
                      ("ones_row_bf", C.ones_row_bf),
                      ("ident_bf", C.ident_bf), ("ident_f32", C.ident_f32)):
        P.dma("sync", lambda e, dst=dst, name=name: e.dma_start(out=dst, in_=io[name]),
              w=[("const", name)])
    C.OCB = ("const", "ones_col_bf")
    C.ORF = ("const", "ones_row_f32")
    C.ORB = ("const", "ones_row_bf")
    return C


def small_cols(C, name, ncols, tok):
    o = C.smalls_used
    C.smalls_used += ncols
    assert C.smalls_used <= 64
    dst = C.smalls[:, o:o + ncols]
    C.P.dma("sync", lambda e, dst=dst, name=name: e.dma_start(out=dst, in_=C.io[name]),
            w=[tok])
    return dst


def ln_stats(C, stat_bf, eps, rms_only=False, double_rms=None):
    P, rows = C.P, C.rows
    bsum, bsq = C.pb[6], C.pb[3]
    if not rms_only:
        for p in range(NP):
            P.op("tensor", lambda e, p=p: e.matmul(
                bsum[0:1, 0:TC], C.ones_col_bf, stat_bf[p],
                start=(p == 0), stop=(p == NP - 1), skip_group_check=True),
                r=[("lnsrc", p), C.OCB], w=[("ps", 6)] if p in (0, NP - 1) else [])
    for p in range(NP):
        P.op("gpsimd", lambda e, p=p: e.tensor_mul(stat_bf[p], stat_bf[p], stat_bf[p]),
             r=[("lnsrc", p)], w=[("lnsrc", p)])
        P.op("tensor", lambda e, p=p: e.matmul(
            bsq[0:1, 0:TC], C.ones_col_bf, stat_bf[p],
            start=(p == 0), stop=(p == NP - 1), skip_group_check=True),
            r=[("lnsrc", p), C.OCB],
            w=[("ps", 3)] if p in (0, NP - 1) else [])
    if not rms_only:
        P.op("vector", lambda e: e.tensor_scalar_mul(rows["mu"], bsum[0:1, 0:TC], 1.0 / D),
             r=[("ps", 6)], w=[("row", "mu")])
        P.op("vector", lambda e: e.tensor_scalar_mul(rows["ms"], bsq[0:1, 0:TC], 1.0 / D),
             r=[("ps", 3)], w=[("row", "ms")])
        P.op("vector", lambda e: e.tensor_mul(rows["tmp"], rows["mu"], rows["mu"]),
             r=[("row", "mu")], w=[("row", "tmp")])
        P.op("vector", lambda e: e.tensor_sub(rows["ms"], rows["ms"], rows["tmp"]),
             r=[("row", "tmp")], w=[("row", "ms")])
    else:
        P.op("vector", lambda e: e.tensor_scalar_mul(rows["ms"], bsq[0:1, 0:TC], 1.0 / D),
             r=[("ps", 3)], w=[("row", "ms")])
    if double_rms is not None:
        eo, ei = double_rms
        P.op("vector", lambda e: e.tensor_scalar(
            rows["tmp"], rows["ms"], 1.0 + ei, ei * eo, ALU.mult, ALU.add),
            r=[("row", "ms")], w=[("row", "tmp")])
    else:
        P.op("vector", lambda e: e.tensor_scalar_add(rows["tmp"], rows["ms"], float(eps)),
             r=[("row", "ms")], w=[("row", "tmp")])
    P.op("scalar", lambda e: e.activation(rows["tmp"], rows["tmp"], AF.Sqrt),
         r=[("row", "tmp")], w=[("row", "tmp")])
    P.op("vector", lambda e: e.reciprocal(rows["rstd"], rows["tmp"]),
         r=[("row", "tmp")], w=[("row", "rstd")])
    if not rms_only:
        P.op("vector", lambda e: e.tensor_mul(rows["nmr"], rows["mu"], rows["rstd"]),
             r=[("row", "mu"), ("row", "rstd")], w=[("row", "nmr")])
        P.op("vector", lambda e: e.tensor_scalar_mul(rows["nmr"], rows["nmr"], -1.0),
             w=[("row", "nmr")])


def ln_bcast(C, bankA, bankB=None, rowA="rstd", rowB="nmr"):
    P, rows = C.P, C.rows
    P.op("tensor", lambda e, bankA=bankA, rowA=rowA: e.matmul(
        C.pb[bankA][:, 0:TC], C.ones_row_f32, rows[rowA], start=True, stop=True),
        r=[("row", rowA), C.ORF], w=[("ps", bankA)])
    if rowB is not None:
        P.op("tensor", lambda e, bankB=bankB, rowB=rowB: e.matmul(
            C.pb[bankB][:, 0:TC], C.ones_row_f32, rows[rowB], start=True, stop=True),
            r=[("row", rowB), C.ORF], w=[("ps", bankB)])


def emit_l0a(C, R_in, R_out, sidx):
    P, io, cv = C.P, C.io, C.cv
    cv.reset()
    zb = [cv.bf(T) for _ in range(NP)]
    w0, w1 = cv.bf(8192), cv.bf(8192)
    kvsb = [cv.bf(512) for _ in range(2)]
    ktm = [cv.bf(1024) for _ in range(2)]
    vtm = [cv.bf(1024) for _ in range(2)]
    qbf = [cv.bf(TC) for _ in range(NP)]
    obf = [cv.bf(TC) for _ in range(NP)]
    sqb = [cv.bf(TC) for _ in range(NP)]
    bias_bc = cv.f32(2048)
    scr = [cv.f32(TC) for _ in range(4)]
    sS = [cv.f32(TC) for _ in range(NP)]

    def S(i):
        return scr[i], ("scr", i)

    def wload(dst, name, tok):
        nk = io[name].shape[0] // 128
        src = io[name].rearrange("(k p) f -> p k f", p=128)
        dst3 = dst.rearrange("p (k f) -> p k f", k=nk)
        P.dma("gpsimd", lambda e, dst3=dst3, src=src: e.dma_start(out=dst3, in_=src),
              w=[tok])

    wload(w0, "l0a_wkT", "wk")
    wload(w1, "l0a_wvT", "wv")
    bq_col = small_cols(C, "l0a_bq", NP, "bqcol")
    bo_col = small_cols(C, "l0a_bo", NP, "bocol")
    P.dma("sync", lambda e: e.dma_start(out=bias_bc[0:1, 0:1024], in_=io["l0a_bk_row"]),
          w=["biasbc"])
    P.dma("sync", lambda e: e.dma_start(out=bias_bc[0:1, 1024:2048], in_=io["l0a_bv_row"]),
          w=["biasbc"])
    for h in range(4):
        P.op("tensor", lambda e, h=h: e.matmul(
            C.pb[5][:, 0:512], C.ones_row_f32, bias_bc[0:1, h * 512:(h + 1) * 512],
            start=True, stop=True),
            r=["biasbc", C.ORF], w=[("ps", 5)])
        P.op("vector", lambda e, h=h: e.tensor_copy(
            bias_bc[:, h * 512:(h + 1) * 512], C.pb[5][:, 0:512]),
            r=[("ps", 5)], w=["biasbc"])

    for c in range(NTC):
        rb = C.rbuf[c % 2]
        for p in range(NP):
            P.dma("sync", lambda e, p=p, c=c, rb=rb: e.dma_start(
                out=rb[:, p * TC:(p + 1) * TC],
                in_=R_in[p * 128:(p + 1) * 128, c * TC:(c + 1) * TC]),
                r=[("R", sidx, c)], w=[("rbuf", c % 2, p)])
        for p in range(NP):
            P.op("vector", lambda e, p=p, rb=rb: e.tensor_copy(
                sqb[p], rb[:, p * TC:(p + 1) * TC]),
                r=[("rbuf", c % 2, p)], w=[("lnsrc", p)])
        ln_stats(C, sqb, 1e-5)
        ln_bcast(C, 4, 5)
        rsb, trs = S(2)
        nsb, tns = S(3)
        P.op("vector", lambda e, rsb=rsb: e.tensor_copy(rsb, C.pb[4][:, 0:TC]),
             r=[("ps", 4)], w=[trs])
        P.op("vector", lambda e, nsb=nsb: e.tensor_copy(nsb, C.pb[5][:, 0:TC]),
             r=[("ps", 5)], w=[tns])
        for p in range(NP):
            a, ta = S(p % 2)
            P.op("gpsimd", lambda e, p=p, rb=rb, a=a, rsb=rsb: e.tensor_mul(
                a, rb[:, p * TC:(p + 1) * TC], rsb),
                r=[("rbuf", c % 2, p), trs], w=[ta])
            P.op("vector", lambda e, p=p, c=c, a=a, nsb=nsb: e.tensor_add(
                zb[p][:, c * TC:(c + 1) * TC], a, nsb),
                r=[ta, tns], w=[("zb", p)])

    scnt = 0
    for r in range(RPC):
        kvbank = 2 + r
        nts = SEQ // 128
        for ts in range(nts):
            tok0 = r * SEQ + ts * 128
            kbuf, vbuf = ktm[ts % 2], vtm[ts % 2]
            for half in range(2):
                for which in ("k", "v"):
                    wsb = w0 if which == "k" else w1
                    wtok = "wk" if which == "k" else "wv"
                    bank = (ts * 2 + half) % 2
                    fo = half * 512
                    for k in range(NP):
                        P.op("tensor", lambda e, k=k, fo=fo, wsb=wsb, bank=bank, tok0=tok0: e.matmul(
                            C.pb[bank][:, 0:512],
                            zb[k][:, tok0:tok0 + 128],
                            wsb[:, k * 1024 + fo: k * 1024 + fo + 512],
                            start=(k == 0), stop=(k == NP - 1), skip_group_check=True),
                            r=[("zb", k), wtok],
                            w=[("ps", bank)] if k in (0, NP - 1) else [])
                    if which == "v":
                        P.op("vector", lambda e, vbuf=vbuf, fo=fo, bank=bank, half=half: e.tensor_add(
                            vbuf[:, fo:fo + 512], C.pb[bank][:, 0:512],
                            bias_bc[:, 1024 + fo:1024 + fo + 512]),
                            r=[("ps", bank), "biasbc"],
                            w=[("vtm", ts % 2, half)])
                    else:
                        si = (scnt % 2) * 2
                        scnt += 1
                        a, ta = S(si)
                        b, tb = S(si + 1)
                        P.op("vector", lambda e, a=a, fo=fo, bank=bank: e.tensor_add(
                            a, C.pb[bank][:, 0:512], bias_bc[:, fo:fo + 512]),
                            r=[("ps", bank), "biasbc"], w=[ta])
                        P.op("gpsimd", lambda e, a=a, b=b: e.tensor_scalar_min(b, a, 0.0),
                             r=[ta], w=[tb])
                        P.op("scalar", lambda e, b=b: e.activation(b, b, AF.Exp),
                             r=[tb], w=[tb])
                        P.op("gpsimd", lambda e, a=a: e.tensor_scalar_max(a, a, 0.0),
                             r=[ta], w=[ta])
                        P.op("vector", lambda e, kbuf=kbuf, fo=fo, a=a, b=b: e.tensor_add(
                            kbuf[:, fo:fo + 512], a, b),
                            r=[ta, tb], w=[("ktm", ts % 2, half)])
            for h in range(16):
                pb_ = h % 2
                fs = 64 * (h // 2)
                first = ts == 0 and h in (0, 1)
                last = ts == nts - 1 and h in (14, 15)
                P.op("tensor", lambda e, pb_=pb_, fs=fs, h=h, kbuf=kbuf, vbuf=vbuf, kvbank=kvbank, first=first, last=last: e.matmul(
                    C.pb[kvbank][64 * pb_:64 * pb_ + 64, fs:fs + 64],
                    kbuf[:, 64 * h:64 * h + 64],
                    vbuf[:, 64 * h:64 * h + 64],
                    start=first, stop=last, skip_group_check=True),
                    r=[("ktm", ts % 2, h // 8), ("vtm", ts % 2, h // 8)],
                    w=[("ps", kvbank)] if (first or last) else [])
        P.op("vector", lambda e, r=r, kvbank=kvbank: e.tensor_copy(
            kvsb[r], C.pb[kvbank][:, 0:512]),
            r=[("ps", kvbank)], w=[("kvsb", r)])

    wload(w0, "l0a_wqT", "wk")
    wload(w1, "l0a_woT", "wv")
    for c in range(NTC):
        r = c // CPR
        t0 = c * TC
        rb = C.rbuf[c % 2]
        for p in range(NP):
            P.dma("sync", lambda e, p=p, t0=t0, rb=rb: e.dma_start(
                out=rb[:, p * TC:(p + 1) * TC],
                in_=R_in[p * 128:(p + 1) * 128, t0:t0 + TC]),
                r=[("R", sidx, c)], w=[("rbuf", c % 2, p)])
        for fp in range(NP):
            bank = fp % 2
            for k in range(NP):
                P.op("tensor", lambda e, k=k, fp=fp, bank=bank, t0=t0: e.matmul(
                    C.pb[bank][:, 0:TC],
                    w0[:, k * 1024 + fp * 128: k * 1024 + fp * 128 + 128],
                    zb[k][:, t0:t0 + TC],
                    start=(k == 0), stop=(k == NP - 1), skip_group_check=True),
                    r=[("zb", k), "wk"], w=[("ps", bank)] if k in (0, NP - 1) else [])
            si = (fp % 2) * 2
            a, ta = S(si)
            b, tb = S(si + 1)
            P.op("vector", lambda e, a=a, bank=bank, fp=fp: e.tensor_scalar(
                a, C.pb[bank][:, 0:TC], bq_col[:, fp:fp + 1], 0.0, ALU.add, ALU.min),
                r=[("ps", bank), "bqcol"], w=[ta])
            P.op("scalar", lambda e, a=a: e.activation(a, a, AF.Exp),
                 r=[ta], w=[ta])
            P.op("vector", lambda e, b=b, bank=bank, fp=fp: e.tensor_scalar(
                b, C.pb[bank][:, 0:TC], bq_col[:, fp:fp + 1], 0.0, ALU.add, ALU.max),
                r=[("ps", bank), "bqcol"], w=[tb])
            P.op("vector", lambda e, a=a, b=b, fp=fp: e.tensor_add(qbf[fp], a, b),
                 r=[ta, tb], w=[("qbf", fp)])
        for pair in range(8):
            bank = 4 + pair % 2
            for sub in range(2):
                h = 2 * pair + sub
                P.op("tensor", lambda e, h=h, bank=bank, r=r, sub=sub: e.matmul(
                    C.pb[bank][64 * (h % 2):64 * (h % 2) + 64, 0:TC],
                    kvsb[r][64 * (h % 2):64 * (h % 2) + 64,
                            64 * (h // 2):64 * (h // 2) + 64],
                    qbf[h // 2][64 * (h % 2):64 * (h % 2) + 64, :],
                    start=True, stop=True, skip_group_check=True),
                    r=[("kvsb", r), ("qbf", h // 2)], w=[("ps", bank)])
            P.op("vector", lambda e, pair=pair, bank=bank: e.tensor_copy(
                obf[pair], C.pb[bank][:, 0:TC]),
                r=[("ps", bank)], w=[("obf", pair)])
        for fp in range(NP):
            bank = fp % 2
            for k in range(NP):
                P.op("tensor", lambda e, k=k, fp=fp, bank=bank: e.matmul(
                    C.pb[bank][:, 0:TC],
                    w1[:, k * 1024 + fp * 128: k * 1024 + fp * 128 + 128],
                    obf[k], start=(k == 0), stop=(k == NP - 1), skip_group_check=True),
                    r=[("obf", k), "wv"], w=[("ps", bank)] if k in (0, NP - 1) else [])
            P.op("vector", lambda e, fp=fp, bank=bank: e.tensor_scalar_add(
                sS[fp], C.pb[bank][:, 0:TC], bo_col[:, fp:fp + 1]),
                r=[("ps", bank), "bocol"], w=[("sS", fp)])
        for p in range(NP):
            P.op("gpsimd", lambda e, p=p: e.tensor_copy(sqb[p], sS[p]),
                 r=[("sS", p)], w=[("lnsrc", p)])
        ln_stats(C, sqb, 1e-5)
        ln_bcast(C, 4, 5)
        rsb, trs = S(2)
        nsb, tns = S(3)
        P.op("vector", lambda e, rsb=rsb: e.tensor_copy(rsb, C.pb[4][:, 0:TC]),
             r=[("ps", 4)], w=[trs])
        P.op("vector", lambda e, nsb=nsb: e.tensor_copy(nsb, C.pb[5][:, 0:TC]),
             r=[("ps", 5)], w=[tns])
        for p in range(NP):
            a, ta = S(p % 2)
            P.op("gpsimd", lambda e, p=p, a=a, rsb=rsb: e.tensor_mul(a, sS[p], rsb),
                 r=[("sS", p), trs], w=[ta])
            P.op("gpsimd", lambda e, a=a, nsb=nsb: e.tensor_add(a, a, nsb),
                 r=[ta, tns], w=[ta])
            P.op("vector", lambda e, p=p, rb=rb, a=a: e.tensor_add(
                rb[:, p * TC:(p + 1) * TC], a, rb[:, p * TC:(p + 1) * TC]),
                r=[ta], w=[("rbuf", c % 2, p)])
            P.dma("sync", lambda e, p=p, t0=t0, rb=rb: e.dma_start(
                out=R_out[p * 128:(p + 1) * 128, t0:t0 + TC],
                in_=rb[:, p * TC:(p + 1) * TC]),
                r=[("rbuf", c % 2, p)], w=[("R", sidx + 1, c)])
    if DEBUG_TAPS:
        for p in range(NP):
            P.dma("gpsimd", lambda e, p=p: e.dma_start(
                out=io["dbg_z"][p * 128:(p + 1) * 128, :], in_=zb[p]),
                r=[("zb", p)])
        for r_ in range(2):
            P.dma("gpsimd", lambda e, r_=r_: e.dma_start(
                out=io["dbg_kv"][r_ * 128:(r_ + 1) * 128, :], in_=kvsb[r_]),
                r=[("kvsb", r_)])
        for p in range(NP):
            P.dma("gpsimd", lambda e, p=p: e.dma_start(
                out=io["dbg_q"][p * 128:(p + 1) * 128, :], in_=qbf[p]),
                r=[("qbf", p)])
            P.dma("gpsimd", lambda e, p=p: e.dma_start(
                out=io["dbg_o"][p * 128:(p + 1) * 128, :], in_=obf[p]),
                r=[("obf", p)])
            P.dma("sync", lambda e, p=p: e.dma_start(
                out=io["dbg_s"][p * 128:(p + 1) * 128, :], in_=sS[p]),
                r=[("sS", p)])


def emit_l1a(C, R_in, R_out, sidx):
    P, io, cv = C.P, C.io, C.cv
    Z = io["Zscratch"]
    lnw_col = small_cols(C, "l1a_lnw", NP, "lnwcol")
    ffnw_col = small_cols(C, "l1a_ffnw", NP, "ffnwcol")

    def wload(dst, name, tok):
        nk = io[name].shape[0] // 128
        src_ = io[name].rearrange("(k p) f -> p k f", p=128)
        dst3 = dst.rearrange("p (k f) -> p k f", k=nk)
        P.dma("gpsimd", lambda e, dst3=dst3, src_=src_: e.dma_start(out=dst3, in_=src_),
              w=[tok])

    def S(i):
        return scr[i], ("scr", i)

    cv.reset()
    zst = cv.bf(4096)
    sqb = [cv.bf(TC) for _ in range(NP)]
    scr = [cv.f32(TC) for _ in range(4)]
    Zv = Z.rearrange("(k p) t -> p k t", p=128)
    for c in range(NTC):
        rb = C.rbuf[c % 2]
        for p in range(NP):
            P.dma("sync", lambda e, p=p, c=c, rb=rb: e.dma_start(
                out=rb[:, p * TC:(p + 1) * TC],
                in_=R_in[p * 128:(p + 1) * 128, c * TC:(c + 1) * TC]),
                r=[("R", sidx, c)], w=[("rbuf", c % 2, p)])
        for p in range(NP):
            P.op("vector", lambda e, p=p, rb=rb: e.tensor_copy(
                sqb[p], rb[:, p * TC:(p + 1) * TC]),
                r=[("rbuf", c % 2, p)], w=[("lnsrc", p)])
        ln_stats(C, sqb, 1e-5, double_rms=(1e-5, 1e-6))
        ln_bcast(C, 4, 5)
        rsb, trs = S(2)
        nsb, tns = S(3)
        P.op("vector", lambda e, rsb=rsb: e.tensor_copy(rsb, C.pb[4][:, 0:TC]),
             r=[("ps", 4)], w=[trs])
        P.op("vector", lambda e, nsb=nsb: e.tensor_copy(nsb, C.pb[5][:, 0:TC]),
             r=[("ps", 5)], w=[tns])
        for p in range(NP):
            a, ta = S(p % 2)
            P.op("gpsimd", lambda e, p=p, rb=rb, a=a, rsb=rsb: e.tensor_mul(
                a, rb[:, p * TC:(p + 1) * TC], rsb),
                r=[("rbuf", c % 2, p), trs], w=[ta])
            P.op("gpsimd", lambda e, a=a, nsb=nsb: e.tensor_add(a, a, nsb),
                 r=[ta, tns], w=[ta])
            P.op("vector", lambda e, p=p, a=a: e.tensor_scalar(
                zst[:, p * TC:(p + 1) * TC], a, lnw_col[:, p:p + 1], None, ALU.mult),
                r=[ta, "lnwcol"], w=["zst"])
        P.dma("sync", lambda e, c=c: e.dma_start(
            out=Zv[:, :, c * TC:(c + 1) * TC],
            in_=zst.rearrange("p (k t) -> p k t", k=NP)),
            r=["zst"], w=[("Z", c)])
    P.barrier()

    HX = io["HXscratch"]
    HXv = HX.rearrange("(k p) t -> p k t", p=128)
    cv.reset()
    hxc = [cv.f32(NP * TC) for _ in range(2)]
    wg = cv.bf(8192)
    wc = cv.bf(8192)
    zck = [cv.bf(4096) for _ in range(2)]
    abuf = [cv.f32(TC) for _ in range(2)]
    xbuf = [cv.f32(TC) for _ in range(2)]
    scr1 = [cv.f32(TC) for _ in range(2)]
    wload(wg, "l1a_wgT", "wg")
    wload(wc, "l1a_wcT", "wc")
    for c in range(NTC):
        zc_ = zck[c % 2]
        P.dma("sync", lambda e, c=c, zc_=zc_: e.dma_start(
            out=zc_.rearrange("p (k t) -> p k t", k=NP),
            in_=Zv[:, :, c * TC:(c + 1) * TC]),
            r=[("Z", c)], w=[("zck", c % 2)])
        for p in range(NP):
            bg, bh = p % 2, 2 + p % 2
            for k in range(NP):
                P.op("tensor", lambda e, k=k, p=p, bg=bg, zc_=zc_: e.matmul(
                    C.pb[bg][:, 0:TC],
                    wg[:, k * 1024 + p * 128: k * 1024 + p * 128 + 128],
                    zc_[:, k * TC:(k + 1) * TC],
                    start=(k == 0), stop=(k == NP - 1), skip_group_check=True),
                    r=[("zck", c % 2), "wg"], w=[("ps", bg)] if k in (0, NP - 1) else [])
            for k in range(NP):
                P.op("tensor", lambda e, k=k, p=p, bh=bh, zc_=zc_: e.matmul(
                    C.pb[bh][:, 0:TC],
                    wc[:, k * 1024 + p * 128: k * 1024 + p * 128 + 128],
                    zc_[:, k * TC:(k + 1) * TC],
                    start=(k == 0), stop=(k == NP - 1), skip_group_check=True),
                    r=[("zck", c % 2), "wc"], w=[("ps", bh)] if k in (0, NP - 1) else [])
            ab, tab = abuf[p % 2], ("ab", p % 2)
            xb_, txb = xbuf[p % 2], ("xb", p % 2)
            sc_, tsc = scr1[p % 2], ("scr1", p % 2)
            P.op("scalar", lambda e, ab=ab, bg=bg: e.activation(
                ab, C.pb[bg][:, 0:TC], AF.Sigmoid, scale=-1.0),
                r=[("ps", bg)], w=[tab])
            P.op("scalar", lambda e, sc_=sc_, bg=bg: e.activation(
                sc_, C.pb[bg][:, 0:TC], AF.Sigmoid),
                r=[("ps", bg)], w=[tsc])
            P.op("vector", lambda e, xb_=xb_, sc_=sc_, bh=bh: e.tensor_mul(
                xb_, sc_, C.pb[bh][:, 0:TC]),
                r=[tsc, ("ps", bh)], w=[txb])
            hxt = hxc[c % 2][:, p * TC:(p + 1) * TC]
            if c % CPR == 0:
                P.op("vector", lambda e, hxt=hxt, ab=ab, xb_=xb_: e.tensor_tensor_scan(
                    hxt, ab, xb_, 0.0, ALU.mult, ALU.add),
                    r=[tab, txb], w=[("hxc", c % 2, p)])
            else:
                carry = hxc[(c - 1) % 2][:, p * TC + TC - 1: p * TC + TC]
                P.op("vector", lambda e, hxt=hxt, ab=ab, xb_=xb_, carry=carry: e.tensor_tensor_scan(
                    hxt, ab, xb_, carry, ALU.mult, ALU.add),
                    r=[tab, txb, ("hxc", (c - 1) % 2, p)], w=[("hxc", c % 2, p)])
        P.dma("sync", lambda e, c=c: e.dma_start(
            out=HXv[:, :, c * TC:(c + 1) * TC],
            in_=hxc[c % 2].rearrange("p (k t) -> p k t", k=NP)),
            r=[("hxc", c % 2, p) for p in range(NP)], w=[("HX", c)])
    P.barrier()

    cv.reset()
    hxk = [cv.f32(NP * TC) for _ in range(2)]
    w1 = cv.bf(8192)
    w3 = cv.bf(8192)
    w2 = cv.bf(8192)
    nhb = [cv.bf(TC) for _ in range(NP)]
    hhb = [cv.bf(TC) for _ in range(NP)]
    sqb2 = [cv.bf(TC) for _ in range(NP)]
    scr2 = [cv.f32(TC) for _ in range(4)]

    def S2(i):
        return scr2[i], ("scr2", i)
    wload(w1, "l1a_w1T", "w1")
    wload(w3, "l1a_w3T", "w3")
    wload(w2, "l1a_w2T", "w2")
    for c in range(NTC):
        t0 = c * TC
        rb = C.rbuf[c % 2]
        hx_ = hxk[c % 2]
        P.dma("scalar", lambda e, c=c, hx_=hx_: e.dma_start(
            out=hx_.rearrange("p (k t) -> p k t", k=NP),
            in_=HXv[:, :, c * TC:(c + 1) * TC]),
            r=[("HX", c)], w=[("hxk", c % 2)])
        for p in range(NP):
            P.dma("sync", lambda e, p=p, t0=t0, rb=rb: e.dma_start(
                out=rb[:, p * TC:(p + 1) * TC],
                in_=R_in[p * 128:(p + 1) * 128, t0:t0 + TC]),
                r=[("R", sidx, c)], w=[("rbuf", c % 2, p)])
        for p in range(NP):
            P.op("vector", lambda e, p=p, hx_=hx_: e.tensor_copy(
                sqb2[p], hx_[:, p * TC:(p + 1) * TC]),
                r=[("hxk", c % 2)], w=[("lnsrc", p)])
        ln_stats(C, sqb2, 1e-6, rms_only=True)
        ln_bcast(C, 4, None, rowB=None)
        rsb, trs = S2(2)
        P.op("vector", lambda e, rsb=rsb: e.tensor_copy(rsb, C.pb[4][:, 0:TC]),
             r=[("ps", 4)], w=[trs])
        for p in range(NP):
            a, ta = S2(p % 2)
            P.op("gpsimd", lambda e, p=p, a=a, rsb=rsb, hx_=hx_: e.tensor_mul(
                a, hx_[:, p * TC:(p + 1) * TC], rsb),
                r=[("hxk", c % 2), trs], w=[ta])
            P.op("vector", lambda e, p=p, a=a: e.tensor_scalar(
                nhb[p], a, ffnw_col[:, p:p + 1], None, ALU.mult),
                r=[ta, "ffnwcol"], w=[("nh", p)])
        for p in range(NP):
            b1, b3 = p % 2, 2 + p % 2
            for k in range(NP):
                P.op("tensor", lambda e, k=k, p=p, b1=b1: e.matmul(
                    C.pb[b1][:, 0:TC],
                    w1[:, k * 1024 + p * 128: k * 1024 + p * 128 + 128],
                    nhb[k], start=(k == 0), stop=(k == NP - 1), skip_group_check=True),
                    r=[("nh", k), "w1"], w=[("ps", b1)] if k in (0, NP - 1) else [])
            for k in range(NP):
                P.op("tensor", lambda e, k=k, p=p, b3=b3: e.matmul(
                    C.pb[b3][:, 0:TC],
                    w3[:, k * 1024 + p * 128: k * 1024 + p * 128 + 128],
                    nhb[k], start=(k == 0), stop=(k == NP - 1), skip_group_check=True),
                    r=[("nh", k), "w3"], w=[("ps", b3)] if k in (0, NP - 1) else [])
            a, ta = S2(p % 2)
            P.op("scalar", lambda e, a=a, b1=b1: e.activation(
                a, C.pb[b1][:, 0:TC], AF.Silu),
                r=[("ps", b1)], w=[ta])
            P.op("vector", lambda e, p=p, a=a, b3=b3: e.tensor_mul(
                hhb[p], a, C.pb[b3][:, 0:TC]),
                r=[ta, ("ps", b3)], w=[("hh", p)])
        for p in range(NP):
            b2 = p % 2
            for k in range(NP):
                P.op("tensor", lambda e, k=k, p=p, b2=b2: e.matmul(
                    C.pb[b2][:, 0:TC],
                    w2[:, k * 1024 + p * 128: k * 1024 + p * 128 + 128],
                    hhb[k], start=(k == 0), stop=(k == NP - 1), skip_group_check=True),
                    r=[("hh", k), "w2"], w=[("ps", b2)] if k in (0, NP - 1) else [])
            P.op("vector", lambda e, p=p, b2=b2, hx_=hx_: e.tensor_add(
                hx_[:, p * TC:(p + 1) * TC], hx_[:, p * TC:(p + 1) * TC],
                C.pb[b2][:, 0:TC]),
                r=[("ps", b2), ("hxk", c % 2)], w=[("hxk", c % 2)])
        for p in range(NP):
            P.op("gpsimd", lambda e, p=p, hx_=hx_: e.tensor_copy(
                sqb2[p], hx_[:, p * TC:(p + 1) * TC]),
                r=[("hxk", c % 2)], w=[("lnsrc", p)])
        ln_stats(C, sqb2, 1e-5)
        ln_bcast(C, 4, 5)
        rsb, trs = S2(2)
        nsb, tns = S2(3)
        P.op("vector", lambda e, rsb=rsb: e.tensor_copy(rsb, C.pb[4][:, 0:TC]),
             r=[("ps", 4)], w=[trs])
        P.op("vector", lambda e, nsb=nsb: e.tensor_copy(nsb, C.pb[5][:, 0:TC]),
             r=[("ps", 5)], w=[tns])
        for p in range(NP):
            a, ta = S2(p % 2)
            P.op("gpsimd", lambda e, p=p, a=a, rsb=rsb, hx_=hx_: e.tensor_mul(
                a, hx_[:, p * TC:(p + 1) * TC], rsb),
                r=[("hxk", c % 2), trs], w=[ta])
            P.op("gpsimd", lambda e, a=a, nsb=nsb: e.tensor_add(a, a, nsb),
                 r=[ta, tns], w=[ta])
            P.op("vector", lambda e, p=p, rb=rb, a=a: e.tensor_add(
                rb[:, p * TC:(p + 1) * TC], a, rb[:, p * TC:(p + 1) * TC]),
                r=[ta], w=[("rbuf", c % 2, p)])
            P.dma("sync", lambda e, p=p, t0=t0, rb=rb: e.dma_start(
                out=R_out[p * 128:(p + 1) * 128, t0:t0 + TC],
                in_=rb[:, p * TC:(p + 1) * TC]),
                r=[("rbuf", c % 2, p)], w=[("R", sidx + 1, c)])


def emit_l1f(C, R_in, R_out, sidx):
    P, io, cv = C.P, C.io, C.cv
    Z = io["Zscratch"]
    Zv = Z.rearrange("(k p) t -> p k t", p=128)
    ACC = io["ACCscratch"]
    WG = io["WGATES"]
    NE, F, FH = 8, 4096, 2048

    cv.reset()
    zst = cv.bf(4096)
    zf = cv.f32(4096)
    sqb = [cv.bf(TC) for _ in range(NP)]
    scr = [cv.f32(TC) for _ in range(4)]
    w_em = cv.f32(T)[0:8, :]
    rw_sb = cv.f32(64)
    rbrow = cv.f32(32)[0:1, 0:8]
    rb_sb = cv.f32(32)[0:128, 0:8]
    gm = [cv.f32(32)[0:128, 0:8] for _ in range(2)]
    col = [cv.f32(16)[0:128, 0:1] for _ in range(4)]

    def S(i):
        return scr[i], ("scr", i)

    P.dma("sync", lambda e: e.dma_start(
        out=rw_sb.rearrange("p (k f) -> p k f", k=8),
        in_=io["l1f_rwT"].rearrange("(k p) f -> p k f", p=128)), w=["rw"])
    P.dma("sync", lambda e: e.dma_start(out=rbrow, in_=io["l1f_rb"]), w=["rb"])
    P.op("tensor", lambda e: e.matmul(C.pb[2][:, 0:8], C.ones_row_f32, rbrow,
                                      start=True, stop=True),
         r=["rb", C.ORF], w=[("ps", 2)])
    P.op("vector", lambda e: e.tensor_copy(rb_sb, C.pb[2][:, 0:8]),
         r=[("ps", 2)], w=["rbsb"])

    for c in range(NTC):
        rb = C.rbuf[c % 2]
        for p in range(NP):
            P.dma("sync", lambda e, p=p, c=c, rb=rb: e.dma_start(
                out=rb[:, p * TC:(p + 1) * TC],
                in_=R_in[p * 128:(p + 1) * 128, c * TC:(c + 1) * TC]),
                r=[("R", sidx, c)], w=[("rbuf", c % 2, p)])
        for p in range(NP):
            P.op("vector", lambda e, p=p, rb=rb: e.tensor_copy(
                sqb[p], rb[:, p * TC:(p + 1) * TC]),
                r=[("rbuf", c % 2, p)], w=[("lnsrc", p)])
        ln_stats(C, sqb, 1e-5)
        ln_bcast(C, 4, 5)
        rsb, trs = S(2)
        nsb, tns = S(3)
        P.op("vector", lambda e, rsb=rsb: e.tensor_copy(rsb, C.pb[4][:, 0:TC]),
             r=[("ps", 4)], w=[trs])
        P.op("vector", lambda e, nsb=nsb: e.tensor_copy(nsb, C.pb[5][:, 0:TC]),
             r=[("ps", 5)], w=[tns])
        for p in range(NP):
            a, ta = S(p % 2)
            P.op("gpsimd", lambda e, p=p, rb=rb, a=a, rsb=rsb: e.tensor_mul(
                a, rb[:, p * TC:(p + 1) * TC], rsb),
                r=[("rbuf", c % 2, p), trs], w=[ta])
            P.op("vector", lambda e, p=p, a=a, nsb=nsb: e.tensor_add(
                zf[:, p * TC:(p + 1) * TC], a, nsb),
                r=[ta, tns], w=["zf"])
            P.op("gpsimd", lambda e, p=p: e.tensor_copy(
                zst[:, p * TC:(p + 1) * TC], zf[:, p * TC:(p + 1) * TC]),
                r=["zf"], w=["zst"])
        P.dma("sync", lambda e, c=c: e.dma_start(
            out=Zv[:, :, c * TC:(c + 1) * TC],
            in_=zst.rearrange("p (k t) -> p k t", k=NP)),
            r=["zst"], w=[("Z", c)])
        for tpl in range(4):
            tp = c * 4 + tpl
            for k in range(NP):
                P.op("tensor", lambda e, k=k, tpl=tpl: e.matmul(
                    C.pb[0][:, 0:8],
                    zf[:, k * TC + tpl * 128: k * TC + tpl * 128 + 128],
                    rw_sb[:, k * 8:(k + 1) * 8],
                    start=(k == 0), stop=(k == NP - 1), skip_group_check=True),
                    r=["zf", "rw"], w=[("ps", 0)] if k in (0, NP - 1) else [])
            l_sb, tl = gm[0], ("gm", 0)
            l2, tl2 = gm[1], ("gm", 1)
            nm1, m2c, dcol, ecol = col
            P.op("vector", lambda e, l_sb=l_sb: e.tensor_add(l_sb, C.pb[0][:, 0:8], rb_sb),
                 r=[("ps", 0), "rbsb"], w=[tl])
            P.op("vector", lambda e, nm1=nm1, l_sb=l_sb: e.reduce_max(
                nm1, l_sb, AX.X, negate=True), r=[tl], w=[("col", 0)])
            P.op("vector", lambda e, ecol=ecol, nm1=nm1: e.tensor_scalar_mul(
                ecol, nm1, -1.0), r=[("col", 0)], w=[("col", 3)])
            P.op("vector", lambda e, l2=l2, l_sb=l_sb, ecol=ecol: e.tensor_single_scalar(
                l2, l_sb, ecol, ALU.is_equal), r=[tl, ("col", 3)], w=[tl2])
            P.op("vector", lambda e, l2=l2, l_sb=l_sb: e.tensor_scalar(
                l2, l2, -BIG_NEG, None, ALU.mult), r=[tl2], w=[tl2])
            P.op("vector", lambda e, l2=l2, l_sb=l_sb: e.tensor_sub(l2, l_sb, l2),
                 r=[tl, tl2], w=[tl2])
            P.op("vector", lambda e, m2c=m2c, l2=l2: e.reduce_max(m2c, l2, AX.X),
                 r=[tl2], w=[("col", 1)])
            P.op("scalar", lambda e, dcol=dcol, m2c=m2c, nm1=nm1: e.activation(
                dcol, m2c, AF.Exp, bias=nm1), r=[("col", 0), ("col", 1)], w=[("col", 2)])
            P.op("vector", lambda e, dcol=dcol: e.tensor_scalar_add(dcol, dcol, 1.0),
                 r=[("col", 2)], w=[("col", 2)])
            P.op("vector", lambda e, dcol=dcol: e.reciprocal(dcol, dcol),
                 r=[("col", 2)], w=[("col", 2)])
            P.op("scalar", lambda e, l2=l2, l_sb=l_sb, nm1=nm1: e.activation(
                l2, l_sb, AF.Exp, bias=nm1), r=[tl, ("col", 0)], w=[tl2])
            P.op("vector", lambda e, l_sb=l_sb, m2c=m2c: e.tensor_single_scalar(
                l_sb, l_sb, m2c, ALU.is_ge), r=[tl, ("col", 1)], w=[tl])
            P.op("vector", lambda e, l_sb=l_sb, l2=l2: e.tensor_mul(l_sb, l_sb, l2),
                 r=[tl, tl2], w=[tl])
            P.op("vector", lambda e, l_sb=l_sb, dcol=dcol: e.tensor_single_scalar(
                l_sb, l_sb, dcol, ALU.mult), r=[tl, ("col", 2)], w=[tl])
            P.op("tensor", lambda e, l_sb=l_sb: e.transpose(
                C.pb[1][0:8, 0:128], l_sb, C.ident_f32),
                r=[tl, ("const", "ident_f32")], w=[("ps", 1)])
            P.op("vector", lambda e, tp=tp: e.tensor_copy(
                w_em[:, tp * 128:(tp + 1) * 128], C.pb[1][0:8, 0:128]),
                r=[("ps", 1)], w=["wem"])
    P.dma("sync", lambda e: e.dma_start(out=WG, in_=w_em), r=["wem"], w=["WGd"])
    P.barrier()

    cv.reset()
    gwb = cv.bf(16384)
    uwb = cv.bf(16384)
    dwb = cv.bf(16384)
    hact = cv.bf(8192)
    zck = cv.bf(4096)
    wst = cv.bf(T)[0:1, :]
    yout = [cv.f32(TC) for _ in range(2)]
    wev = cv.f32(TC)
    sil = cv.f32(TC)
    for e_ in range(NE):
        P.dma("gpsimd", lambda e, e_=e_: e.dma_start(out=wst, in_=WG[e_:e_ + 1, :]),
              r=["WGd"], w=["wst"])
        for fh in range(2):
            r0 = e_ * 1024
            f0 = fh * FH
            P.dma("gpsimd", lambda e, r0=r0, f0=f0: e.dma_start(
                out=gwb.rearrange("p (k f) -> p k f", k=8),
                in_=io["l1f_gwT"][r0:r0 + 1024, f0:f0 + FH].rearrange(
                    "(k p) f -> p k f", p=128)), w=["gw"])
            P.dma("gpsimd", lambda e, r0=r0, f0=f0: e.dma_start(
                out=uwb.rearrange("p (k f) -> p k f", k=8),
                in_=io["l1f_uwT"][r0:r0 + 1024, f0:f0 + FH].rearrange(
                    "(k p) f -> p k f", p=128)), w=["uw"])
            d0 = e_ * 4096 + fh * FH
            P.dma("gpsimd", lambda e, d0=d0: e.dma_start(
                out=dwb.rearrange("p (k f) -> p k f", k=16),
                in_=io["l1f_dwT"][d0:d0 + FH, :].rearrange(
                    "(k p) f -> p k f", p=128)), w=["dw"])
            for tc_ in range(NTC):
                t0 = tc_ * TC
                P.dma("sync", lambda e, t0=t0: e.dma_start(
                    out=zck.rearrange("p (k t) -> p k t", k=NP),
                    in_=Zv[:, :, t0:t0 + TC]),
                    r=[("Z", tc_)], w=["zck"])
                P.op("tensor", lambda e, t0=t0: e.matmul(
                    C.pb[4][:, 0:TC], C.ones_row_bf, wst[0:1, t0:t0 + TC],
                    start=True, stop=True),
                    r=["wst", C.ORB], w=[("ps", 4)])
                P.op("vector", lambda e: e.tensor_copy(wev, C.pb[4][:, 0:TC]),
                     r=[("ps", 4)], w=["wev"])
                for fp in range(16):
                    bg, bu = fp % 2, 2 + fp % 2
                    for k in range(NP):
                        P.op("tensor", lambda e, k=k, fp=fp, bg=bg: e.matmul(
                            C.pb[bg][:, 0:TC],
                            gwb[:, k * FH + fp * 128: k * FH + fp * 128 + 128],
                            zck[:, k * TC:(k + 1) * TC],
                            start=(k == 0), stop=(k == NP - 1), skip_group_check=True),
                            r=["zck", "gw"], w=[("ps", bg)] if k in (0, NP - 1) else [])
                    for k in range(NP):
                        P.op("tensor", lambda e, k=k, fp=fp, bu=bu: e.matmul(
                            C.pb[bu][:, 0:TC],
                            uwb[:, k * FH + fp * 128: k * FH + fp * 128 + 128],
                            zck[:, k * TC:(k + 1) * TC],
                            start=(k == 0), stop=(k == NP - 1), skip_group_check=True),
                            r=["zck", "uw"], w=[("ps", bu)] if k in (0, NP - 1) else [])
                    P.op("scalar", lambda e, bg=bg: e.activation(
                        sil, C.pb[bg][:, 0:TC], AF.Silu),
                        r=[("ps", bg)], w=["sil"])
                    P.op("vector", lambda e, fp=fp, bu=bu: e.tensor_mul(
                        hact[:, fp * TC:(fp + 1) * TC], sil, C.pb[bu][:, 0:TC]),
                        r=["sil", ("ps", bu)], w=[("ha", fp)])
                for op in range(NP):
                    bo = 5 if op % 2 else 6
                    for k in range(16):
                        P.op("tensor", lambda e, k=k, op=op, bo=bo: e.matmul(
                            C.pb[bo][:, 0:TC],
                            dwb[:, k * 1024 + op * 128: k * 1024 + op * 128 + 128],
                            hact[:, k * TC:(k + 1) * TC],
                            start=(k == 0), stop=(k == 15), skip_group_check=True),
                            r=[("ha", k), "dw"], w=[("ps", bo)] if k in (0, 15) else [])
                    yo = yout[op % 2]
                    P.op("vector", lambda e, yo=yo, bo=bo: e.tensor_mul(
                        yo, C.pb[bo][:, 0:TC], wev),
                        r=[("ps", bo), "wev"], w=[("yo", op % 2)])
                    first_w = e_ == 0 and fh == 0
                    P.dma("gpsimd", lambda e, yo=yo, op=op, t0=t0, first_w=first_w: e.dma_start(
                        out=ACC[op * 128:(op + 1) * 128, t0:t0 + TC], in_=yo,
                        accum_op=(ALU.bypass if first_w else ALU.add)),
                        r=[("yo", op % 2)], w=[("ACC", tc_)])
    P.barrier()

    cv.reset()
    sqb3 = [cv.bf(TC) for _ in range(NP)]
    scr3 = [cv.f32(TC) for _ in range(4)]
    for c in range(NTC):
        rb = C.rbuf[c % 2]
        ab = C.rbuf[1 - c % 2]
        for p in range(NP):
            P.dma("sync", lambda e, p=p, c=c, rb=rb: e.dma_start(
                out=rb[:, p * TC:(p + 1) * TC],
                in_=R_in[p * 128:(p + 1) * 128, c * TC:(c + 1) * TC]),
                r=[("R", sidx, c)], w=[("rbuf", c % 2, p)])
            P.dma("scalar", lambda e, p=p, c=c, ab=ab: e.dma_start(
                out=ab[:, p * TC:(p + 1) * TC],
                in_=ACC[p * 128:(p + 1) * 128, c * TC:(c + 1) * TC]),
                r=[("ACC", c)], w=[("rbuf", 1 - c % 2, p)])
        for p in range(NP):
            P.op("vector", lambda e, p=p, ab=ab: e.tensor_copy(
                sqb3[p], ab[:, p * TC:(p + 1) * TC]),
                r=[("rbuf", 1 - c % 2, p)], w=[("lnsrc", p)])
        ln_stats(C, sqb3, 1e-5)
        ln_bcast(C, 4, 5)
        rsb, tr3 = scr3[2], ("scr3", 2)
        nsb, tn3 = scr3[3], ("scr3", 3)
        P.op("vector", lambda e, rsb=rsb: e.tensor_copy(rsb, C.pb[4][:, 0:TC]),
             r=[("ps", 4)], w=[tr3])
        P.op("vector", lambda e, nsb=nsb: e.tensor_copy(nsb, C.pb[5][:, 0:TC]),
             r=[("ps", 5)], w=[tn3])
        for p in range(NP):
            a, ta = scr3[p % 2], ("scr3", p % 2)
            P.op("gpsimd", lambda e, p=p, a=a, ab=ab, rsb=rsb: e.tensor_mul(
                a, ab[:, p * TC:(p + 1) * TC], rsb),
                r=[("rbuf", 1 - c % 2, p), tr3], w=[ta])
            P.op("gpsimd", lambda e, a=a, nsb=nsb: e.tensor_add(a, a, nsb),
                 r=[ta, tn3], w=[ta])
            P.op("vector", lambda e, p=p, rb=rb, a=a: e.tensor_add(
                rb[:, p * TC:(p + 1) * TC], a, rb[:, p * TC:(p + 1) * TC]),
                r=[ta], w=[("rbuf", c % 2, p)])
            P.dma("sync", lambda e, p=p, c=c, rb=rb: e.dma_start(
                out=R_out[p * 128:(p + 1) * 128, c * TC:(c + 1) * TC],
                in_=rb[:, p * TC:(p + 1) * TC]),
                r=[("rbuf", c % 2, p)], w=[("R", sidx + 1, c)])


def emit_l0f(C, R_in, R_out, sidx):
    P, io, cv = C.P, C.io, C.cv
    XLH = io["HXscratch"]
    NW = SEQ // 512
    HIDP = 2816
    NK2 = HIDP // 128

    cv.reset()
    sqb = [cv.bf(TC) for _ in range(NP)]
    scr = [cv.f32(TC) for _ in range(4)]
    xf = cv.f32(NP * TC)
    for c in range(NTC):
        rb = C.rbuf[c % 2]
        for p in range(NP):
            P.dma("sync", lambda e, p=p, c=c, rb=rb: e.dma_start(
                out=rb[:, p * TC:(p + 1) * TC],
                in_=R_in[p * 128:(p + 1) * 128, c * TC:(c + 1) * TC]),
                r=[("R", sidx, c)], w=[("rbuf", c % 2, p)])
        for p in range(NP):
            P.op("vector", lambda e, p=p, rb=rb: e.tensor_copy(
                sqb[p], rb[:, p * TC:(p + 1) * TC]),
                r=[("rbuf", c % 2, p)], w=[("lnsrc", p)])
        ln_stats(C, sqb, 1e-5)
        ln_bcast(C, 4, 5)
        rsb, trs = scr[2], ("scr", 2)
        nsb, tns = scr[3], ("scr", 3)
        P.op("vector", lambda e, rsb=rsb: e.tensor_copy(rsb, C.pb[4][:, 0:TC]),
             r=[("ps", 4)], w=[trs])
        P.op("vector", lambda e, nsb=nsb: e.tensor_copy(nsb, C.pb[5][:, 0:TC]),
             r=[("ps", 5)], w=[tns])
        for p in range(NP):
            a, ta = scr[p % 2], ("scr", p % 2)
            P.op("gpsimd", lambda e, p=p, rb=rb, a=a, rsb=rsb: e.tensor_mul(
                a, rb[:, p * TC:(p + 1) * TC], rsb),
                r=[("rbuf", c % 2, p), trs], w=[ta])
            P.op("vector", lambda e, p=p, a=a, nsb=nsb: e.tensor_add(
                xf[:, p * TC:(p + 1) * TC], a, nsb),
                r=[ta, tns], w=["xf"])
        P.dma("sync", lambda e, c=c: e.dma_start(
            out=XLH.rearrange("(k p) t -> p k t", p=128)[:, :, c * TC:(c + 1) * TC],
            in_=xf.rearrange("p (k t) -> p k t", k=NP)),
            r=["xf"], w=[("XLH", c)])
    P.barrier()
    if L0F_TRUNC == "xl":
        return

    for l in range(2):
        cv.reset()
        wqkv = cv.bf(NP * 1536)
        wo_sb = cv.bf(4096)
        tabs = {nm: cv.f32(2048) for nm in ("Aq", "Bq", "Ak", "Bk")}
        mq = [cv.bf(1024) for _ in range(4)]
        mall = cv.bf(512)
        hm2_sb = cv.bf(32)[:, 0:2]
        hbc_sb = cv.f32(128)[0:2, :]
        kb = cv.bf(4096)
        vb = cv.bf(4096)
        qb = cv.bf(2048)
        pn = cv.bf(4096)
        pt_sb = cv.bf(4096)
        ob = cv.bf(2048)
        hb = cv.bf(4096)
        sqf = cv.bf(512)
        scr = [cv.f32(TC) for _ in range(4)]
        r2a = cv.f32(TC)[0:2, :]
        colc = cv.f32(32)
        halo = cv.f32(32)[:, 0:8]
        qs_col = cv.f32(32)[:, 0:1]
        ks_col = cv.f32(32)[:, 0:1]

        def wload3(dst, name, tok, nk):
            P.dma("gpsimd", lambda e, dst=dst, name=name, nk=nk: e.dma_start(
                out=dst.rearrange("p (k f) -> p k f", k=nk),
                in_=io[name].rearrange("(k p) f -> p k f", p=128)), w=[tok])

        wload3(wqkv, f"l0f_wqkv{l}", "wqkv", NP)
        wload3(wo_sb, f"l0f_wo{l}", "wo", 4)
        for nm in ("Aq", "Bq", "Ak", "Bk"):
            P.dma("sync", lambda e, nm=nm: e.dma_start(out=tabs[nm][:, 0:SEQ],
                                                       in_=io["l0f_" + nm]),
                  w=[("tab", nm)])
        for qt in range(4):
            P.dma("sync", lambda e, qt=qt: e.dma_start(out=mq[qt], in_=io[f"l0f_mask{qt}"]),
                  w=[("mq", qt)])
        P.dma("sync", lambda e: e.dma_start(out=mall, in_=io["l0f_mask_all"]), w=["mall"])
        P.dma("sync", lambda e: e.dma_start(out=hm2_sb, in_=io["l0f_hm2"]), w=["hm2"])
        P.dma("sync", lambda e: e.dma_start(out=hbc_sb, in_=io["l0f_hbc"]), w=["hbc"])
        P.dma("sync", lambda e, l=l: e.dma_start(out=qs_col, in_=io[f"l0f_qs{l}"]), w=["qsc"])
        P.dma("sync", lambda e, l=l: e.dma_start(out=ks_col, in_=io[f"l0f_ks{l}"]), w=["ksc"])
        P.op("gpsimd", lambda e: e.memset(kb, 0.0), w=[("kb", 0, 0), ("kb", 0, 1),
                                                       ("kb", 0, 2), ("kb", 0, 3),
                                                       ("kb", 1, 0), ("kb", 1, 1),
                                                       ("kb", 1, 2), ("kb", 1, 3)])
        P.op("gpsimd", lambda e: e.memset(vb, 0.0), w=[("vb", 0, 0), ("vb", 0, 1),
                                                       ("vb", 0, 2), ("vb", 0, 3),
                                                       ("vb", 1, 0), ("vb", 1, 1),
                                                       ("vb", 1, 2), ("vb", 1, 3)])

        def qk_proc(which, fp, bank, slot, woff, tc):
            A = tabs["Aq" if which == "q" else "Ak"]
            B = tabs["Bq" if which == "q" else "Bk"]
            sc_col = qs_col if which == "q" else ks_col
            sct = "qsc" if which == "q" else "ksc"
            if which == "q":
                dst, dtok = qb[:, fp * 512:(fp + 1) * 512], ("qb", fp)
            else:
                dst = kb[:, (slot * 4 + fp) * 512:(slot * 4 + fp + 1) * 512]
                dtok = ("kb", slot, fp)
            P.op("scalar", lambda e, bank=bank: e.activation(sqf, C.pb[bank][:, 0:TC],
                                                             AF.Square),
                 r=[("ps", bank)], w=["sqf"])
            P.op("tensor", lambda e: e.matmul(C.pb[6][0:2, 0:TC], hm2_sb, sqf,
                                              start=True, stop=True),
                 r=["sqf", "hm2"], w=[("ps", 6)])
            P.op("scalar", lambda e: e.activation(r2a, C.pb[6][0:2, 0:TC], AF.Sqrt),
                 r=[("ps", 6)], w=["r2a"])
            P.op("vector", lambda e: e.tensor_scalar_max(r2a, r2a, 1e-12),
                 r=["r2a"], w=["r2a"])
            P.op("vector", lambda e: e.reciprocal(r2a, r2a), r=["r2a"], w=["r2a"])
            P.op("tensor", lambda e: e.matmul(C.pb[5][:, 0:TC], hbc_sb, r2a,
                                              start=True, stop=True),
                 r=["r2a", "hbc"], w=[("ps", 5)])
            a, ta = scr[0], ("scr", 0)
            b, tb = scr[1], ("scr", 1)
            P.op("vector", lambda e, bank=bank, sc_col=sc_col, a=a: e.tensor_scalar(
                a, C.pb[bank][:, 0:TC], sc_col, None, ALU.mult),
                r=[("ps", bank), sct], w=[ta])
            mask = []
            for i in range(16):
                mask += [2 * i + 1, 2 * i]
            P.op("vector", lambda e, a=a, b=b, mask=mask: e.stream_shuffle(b, a, mask),
                 r=[ta], w=[tb])
            P.op("gpsimd", lambda e, a=a, A=A, woff=woff: e.tensor_mul(
                a, a, A[:, woff:woff + 512]),
                r=[ta, ("tab", "Aq"), ("tab", "Ak")], w=[ta])
            P.op("gpsimd", lambda e, b=b, B=B, woff=woff: e.tensor_mul(
                b, b, B[:, woff:woff + 512]),
                r=[tb, ("tab", "Bq"), ("tab", "Bk")], w=[tb])
            P.op("vector", lambda e, a=a, b=b: e.tensor_add(a, a, b),
                 r=[ta, tb], w=[ta])
            P.op("vector", lambda e, a=a, dst=dst: e.tensor_mul(dst, a, C.pb[5][:, 0:TC]),
                 r=[ta, ("ps", 5)], w=[dtok])

        for r_ in range(RPC):
            for w_ in range(NW):
                tc = r_ * NW + w_
                slot = tc % 2
                pslot = 1 - slot
                woff = w_ * 512
                tok0 = tc * TC
                rb = C.rbuf[slot]
                for p in range(NP):
                    P.dma("sync", lambda e, p=p, tok0=tok0, rb=rb: e.dma_start(
                        out=rb[:, p * TC:(p + 1) * TC],
                        in_=XLH[p * 128:(p + 1) * 128, tok0:tok0 + TC]),
                        r=[("XLH", tc)], w=[("rbuf", slot, p)])
                if w_ == 0:
                    P.op("vector", lambda e: e.memset(halo, 0.0), w=["halo"])
                for p in range(NP):
                    if p < 4:
                        P.op("gpsimd", lambda e, p=p, rb=rb: e.tensor_copy(
                            hb[:, p * 512:(p + 1) * 512], rb[:, p * TC:(p + 1) * TC]),
                            r=[("rbuf", slot, p)], w=[("hb", p)])
                    else:
                        P.op("vector", lambda e, p=p, rb=rb: e.tensor_copy(
                            hb[:, p * 512 + 1:(p + 1) * 512],
                            rb[:, p * TC:(p + 1) * TC - 1]),
                            r=[("rbuf", slot, p)], w=[("hb", p)])
                        P.op("vector", lambda e, p=p: e.tensor_copy(
                            hb[:, p * 512:p * 512 + 1], halo[:, p:p + 1]),
                            r=["halo"], w=[("hb", p)])
                for p in range(NP):
                    P.op("vector", lambda e, p=p, rb=rb: e.tensor_copy(
                        halo[:, p:p + 1], rb[:, (p + 1) * TC - 1:(p + 1) * TC]),
                        r=[("rbuf", slot, p)], w=["halo"])
                for which, cbase in (("q", 0), ("k", 512)):
                    for fp in range(4):
                        bank = fp % 2
                        for k in range(NP):
                            P.op("tensor", lambda e, k=k, fp=fp, bank=bank, cbase=cbase: e.matmul(
                                C.pb[bank][:, 0:TC],
                                wqkv[:, k * 1536 + cbase + fp * 128:
                                     k * 1536 + cbase + fp * 128 + 128],
                                hb[:, k * 512:(k + 1) * 512],
                                start=(k == 0), stop=(k == NP - 1),
                                skip_group_check=True),
                                r=[("hb", k), "wqkv"],
                                w=[("ps", bank)] if k in (0, NP - 1) else [])
                        qk_proc(which, fp, bank, slot, woff, tc)
                for ts in range(4):
                    bank = 2 + ts % 2
                    for k in range(NP):
                        P.op("tensor", lambda e, k=k, ts=ts, bank=bank: e.matmul(
                            C.pb[bank][:, 0:TC],
                            hb[:, k * 512 + ts * 128: k * 512 + ts * 128 + 128],
                            wqkv[:, k * 1536 + 1024: k * 1536 + 1536],
                            start=(k == 0), stop=(k == NP - 1), skip_group_check=True),
                            r=[("hb", k), "wqkv"],
                            w=[("ps", bank)] if k in (0, NP - 1) else [])
                    P.op("vector", lambda e, ts=ts, bank=bank, slot=slot: e.tensor_copy(
                        vb[:, (slot * 4 + ts) * 512:(slot * 4 + ts + 1) * 512],
                        C.pb[bank][:, 0:TC]),
                        r=[("ps", bank)], w=[("vb", slot, ts)])
                for h in range(8):
                    hp = h // 2
                    hb_ = 64 * (h % 2)
                    for qt in range(4):
                        bp = 2 * (qt % 2)
                        bo = bp + 1
                        prev_mask = mall if w_ == 0 else mq[qt][:, 0:512]
                        prev_mtok = "mall" if w_ == 0 else ("mq", qt)
                        P.op("tensor", lambda e, bp=bp, prev_mask=prev_mask: e.matmul(
                            C.pb[bp][:, 0:TC], C.ident_bf, prev_mask,
                            start=True, stop=False, skip_group_check=True),
                            r=[prev_mtok, ("const", "ident_bf")], w=[("ps", bp)])
                        P.op("tensor", lambda e, bo=bo, qt=qt: e.matmul(
                            C.pb[bo][:, 0:TC], C.ident_bf, mq[qt][:, 512:1024],
                            start=True, stop=False, skip_group_check=True),
                            r=[("mq", qt), ("const", "ident_bf")], w=[("ps", bo)])
                        for sl, bnk in ((pslot, bp), (slot, bo)):
                            P.op("tensor", lambda e, h=h, hp=hp, hb_=hb_, qt=qt, sl=sl, bnk=bnk: e.matmul(
                                C.pb[bnk][:, 0:TC],
                                qb[hb_:hb_ + 64, hp * 512 + qt * 128: hp * 512 + qt * 128 + 128],
                                kb[hb_:hb_ + 64, (sl * 4 + hp) * 512:(sl * 4 + hp + 1) * 512],
                                start=False, stop=True, skip_group_check=True),
                                r=[("qb", hp), ("kb", sl, hp)], w=[("ps", bnk)])
                        nm1 = colc[:, 0:1]
                        nm2 = colc[:, 1:2]
                        s0 = colc[:, 2:3]
                        s1 = colc[:, 3:4]
                        P.op("vector", lambda e, bp=bp, nm1=nm1: e.reduce_max(
                            nm1, C.pb[bp][:, 0:TC], AX.X, negate=True),
                            r=[("ps", bp)], w=[("col", 0)])
                        P.op("vector", lambda e, bo=bo, nm2=nm2: e.reduce_max(
                            nm2, C.pb[bo][:, 0:TC], AX.X, negate=True),
                            r=[("ps", bo)], w=[("col", 1)])
                        P.op("vector", lambda e, nm1=nm1, nm2=nm2: e.tensor_tensor(
                            nm1, nm1, nm2, ALU.min),
                            r=[("col", 0), ("col", 1)], w=[("col", 0)])
                        P.op("scalar", lambda e, bp=bp, qt=qt, nm1=nm1, s0=s0: e.activation(
                            pn[:, qt * 1024: qt * 1024 + 512], C.pb[bp][:, 0:TC],
                            AF.Exp, bias=nm1, accum_out=s0),
                            r=[("ps", bp), ("col", 0)], w=[("pn", qt), ("col", 2)])
                        P.op("scalar", lambda e, bo=bo, qt=qt, nm1=nm1, s1=s1: e.activation(
                            pn[:, qt * 1024 + 512: qt * 1024 + 1024], C.pb[bo][:, 0:TC],
                            AF.Exp, bias=nm1, accum_out=s1),
                            r=[("ps", bo), ("col", 0)], w=[("pn", qt), ("col", 3)])
                        P.op("vector", lambda e, s0=s0, s1=s1: e.tensor_add(s0, s0, s1),
                             r=[("col", 2), ("col", 3)], w=[("col", 2)])
                        P.op("vector", lambda e, s0=s0: e.reciprocal(s0, s0),
                             r=[("col", 2)], w=[("col", 2)])
                        P.op("vector", lambda e, qt=qt, s0=s0: e.tensor_scalar(
                            pn[:, qt * 1024:(qt + 1) * 1024],
                            pn[:, qt * 1024:(qt + 1) * 1024], s0, None, ALU.mult),
                            r=[("pn", qt), ("col", 2)], w=[("pn", qt)])
                    for j in range(8):
                        for qt in range(4):
                            P.op("tensor", lambda e, j=j, qt=qt: e.transpose(
                                C.pt[:, qt * 128:(qt + 1) * 128],
                                pn[:, qt * 1024 + j * 128: qt * 1024 + j * 128 + 128],
                                C.ident_bf),
                                r=[("pn", qt), ("const", "ident_bf")],
                                w=[("pt",)] if qt in (0, 3) else [])
                        P.op("vector", lambda e, j=j: e.tensor_copy(
                            pt_sb[:, j * 512:(j + 1) * 512], C.pt[:, 0:512]),
                            r=[("pt",)], w=[("ptsb", j)])
                    bo2 = 4 + hp % 2
                    for j in range(8):
                        sl = pslot if j < 4 else slot
                        ts = j % 4
                        P.op("tensor", lambda e, j=j, sl=sl, ts=ts, h=h, hb_=hb_, bo2=bo2: e.matmul(
                            C.pb[bo2][hb_:hb_ + 64, 0:TC],
                            vb[:, (sl * 4 + ts) * 512 + 64 * h:
                               (sl * 4 + ts) * 512 + 64 * h + 64],
                            pt_sb[:, j * 512:(j + 1) * 512],
                            start=(j == 0), stop=(j == 7), skip_group_check=True),
                            r=[("vb", sl, ts), ("ptsb", j)],
                            w=[("ps", bo2)] if j in (0, 7) else [])
                    if h % 2 == 1:
                        P.op("vector", lambda e, hp=hp, bo2=bo2: e.tensor_copy(
                            ob[:, hp * 512:(hp + 1) * 512], C.pb[bo2][:, 0:TC]),
                            r=[("ps", bo2)], w=[("ob", hp)])
                for fp in range(NP):
                    bank = fp % 2
                    for k in range(4):
                        P.op("tensor", lambda e, k=k, fp=fp, bank=bank: e.matmul(
                            C.pb[bank][:, 0:TC],
                            wo_sb[:, k * 1024 + fp * 128: k * 1024 + fp * 128 + 128],
                            ob[:, k * 512:(k + 1) * 512],
                            start=(k == 0), stop=(k == 3), skip_group_check=True),
                            r=[("ob", k), "wo"],
                            w=[("ps", bank)] if k in (0, 3) else [])
                    P.op("vector", lambda e, fp=fp, bank=bank, rb=rb: e.tensor_add(
                        rb[:, fp * TC:(fp + 1) * TC], rb[:, fp * TC:(fp + 1) * TC],
                        C.pb[bank][:, 0:TC]),
                        r=[("ps", bank)], w=[("rbuf", slot, fp)])
                    P.dma("sync", lambda e, fp=fp, tok0=tok0, rb=rb: e.dma_start(
                        out=XLH[fp * 128:(fp + 1) * 128, tok0:tok0 + TC],
                        in_=rb[:, fp * TC:(fp + 1) * TC]),
                        r=[("rbuf", slot, fp)], w=[("XLH", tc)])
        if DEBUG_TAPS and L0F_TRUNC == f"att{l}":
            P.barrier()
            for nm, buf in (("dbg_hb", hb), ("dbg_qb", qb), ("dbg_kb", kb),
                            ("dbg_vb", vb), ("dbg_pn", pn), ("dbg_ob", ob),
                            ("dbg_pt", pt_sb)):
                P.dma("gpsimd", lambda e, nm=nm, buf=buf: e.dma_start(
                    out=io[nm], in_=buf), w=[("dbg", nm)])
        P.barrier()
        if L0F_TRUNC == f"att{l}":
            return

        cv.reset()
        w1a_sb = cv.bf(NP * HIDP)
        w1g_sb = cv.bf(NP * HIDP)
        w2c = [cv.bf(HIDP) for _ in range(2)]
        hn = cv.bf(4096)
        hh = cv.bf(NK2 * 512)
        sqb2 = [cv.bf(TC) for _ in range(NP)]
        hfc = cv.f32(NP * TC)
        scr2 = [cv.f32(TC) for _ in range(4)]
        gel = scr2[3]
        lng = cv.f32(16)[:, 0:8]
        lnb = cv.f32(16)[:, 0:8]
        b1a = cv.f32(NK2)[:, 0:NK2]
        b1g = cv.f32(NK2)[:, 0:NK2]
        b2c = cv.f32(16)[:, 0:8]
        halo2 = cv.f32(32)[:, 0:8]
        wload3(w1a_sb, f"l0f_w1a{l}", "w1a", NP)
        wload3(w1g_sb, f"l0f_w1g{l}", "w1g", NP)
        for nm, dst in (("lng", lng), ("lnb", lnb), ("b1a", b1a), ("b1g", b1g),
                        ("b2", b2c)):
            P.dma("sync", lambda e, nm=nm, dst=dst, l=l: e.dma_start(
                out=dst, in_=io[f"l0f_{nm}{l}"]), w=[nm])
        for tc in range(NTC):
            t0 = tc * TC
            rb = C.rbuf[0]
            rb2 = C.rbuf[1]
            for p in range(NP):
                P.dma("sync", lambda e, p=p, t0=t0, rb=rb: e.dma_start(
                    out=rb[:, p * TC:(p + 1) * TC],
                    in_=XLH[p * 128:(p + 1) * 128, t0:t0 + TC]),
                    r=[("XLH", tc)], w=[("rbuf", 0, p)])
                if l == 1:
                    P.dma("scalar", lambda e, p=p, t0=t0, rb2=rb2: e.dma_start(
                        out=rb2[:, p * TC:(p + 1) * TC],
                        in_=R_in[p * 128:(p + 1) * 128, t0:t0 + TC]),
                        r=[("R", sidx, tc)], w=[("rbuf", 1, p)])
            if tc % CPR == 0:
                P.op("vector", lambda e: e.memset(halo2, 0.0), w=["halo2"])
            for p in range(NP):
                if p < 4:
                    P.op("gpsimd", lambda e, p=p, rb=rb: e.tensor_copy(
                        hfc[:, p * TC:(p + 1) * TC], rb[:, p * TC:(p + 1) * TC]),
                        r=[("rbuf", 0, p)], w=[("hfc", p)])
                else:
                    P.op("vector", lambda e, p=p, rb=rb: e.tensor_copy(
                        hfc[:, p * TC + 1:(p + 1) * TC],
                        rb[:, p * TC:(p + 1) * TC - 1]),
                        r=[("rbuf", 0, p)], w=[("hfc", p)])
                    P.op("vector", lambda e, p=p: e.tensor_copy(
                        hfc[:, p * TC:p * TC + 1], halo2[:, p:p + 1]),
                        r=["halo2"], w=[("hfc", p)])
            for p in range(NP):
                P.op("vector", lambda e, p=p, rb=rb: e.tensor_copy(
                    halo2[:, p:p + 1], rb[:, (p + 1) * TC - 1:(p + 1) * TC]),
                    r=[("rbuf", 0, p)], w=["halo2"])
            for p in range(NP):
                P.op("gpsimd", lambda e, p=p: e.tensor_copy(
                    sqb2[p], hfc[:, p * TC:(p + 1) * TC]),
                    r=[("hfc", p)], w=[("lnsrc", p)])
            ln_stats(C, sqb2, 1e-5)
            ln_bcast(C, 4, 5)
            rsb, trs = scr2[2], ("scr2", 2)
            nsb, tns = scr2[3], ("scr2", 3)
            P.op("vector", lambda e, rsb=rsb: e.tensor_copy(rsb, C.pb[4][:, 0:TC]),
                 r=[("ps", 4)], w=[trs])
            P.op("vector", lambda e, nsb=nsb: e.tensor_copy(nsb, C.pb[5][:, 0:TC]),
                 r=[("ps", 5)], w=[tns])
            for p in range(NP):
                a, ta = scr2[p % 2], ("scr2", p % 2)
                P.op("gpsimd", lambda e, p=p, a=a, rsb=rsb: e.tensor_mul(
                    a, hfc[:, p * TC:(p + 1) * TC], rsb),
                    r=[("hfc", p), trs], w=[ta])
                P.op("gpsimd", lambda e, a=a, nsb=nsb: e.tensor_add(a, a, nsb),
                     r=[ta, tns], w=[ta])
                P.op("vector", lambda e, p=p, a=a: e.tensor_scalar(
                    hn[:, p * TC:(p + 1) * TC], a, lng[:, p:p + 1], lnb[:, p:p + 1],
                    ALU.mult, ALU.add),
                    r=[ta, "lng", "lnb"], w=[("hn", p)])
            for fp in range(NK2):
                ba, bg = fp % 2, 2 + fp % 2
                for k in range(NP):
                    P.op("tensor", lambda e, k=k, fp=fp, ba=ba: e.matmul(
                        C.pb[ba][:, 0:TC],
                        w1a_sb[:, k * HIDP + fp * 128: k * HIDP + fp * 128 + 128],
                        hn[:, k * TC:(k + 1) * TC],
                        start=(k == 0), stop=(k == NP - 1), skip_group_check=True),
                        r=[("hn", k), "w1a"], w=[("ps", ba)] if k in (0, NP - 1) else [])
                for k in range(NP):
                    P.op("tensor", lambda e, k=k, fp=fp, bg=bg: e.matmul(
                        C.pb[bg][:, 0:TC],
                        w1g_sb[:, k * HIDP + fp * 128: k * HIDP + fp * 128 + 128],
                        hn[:, k * TC:(k + 1) * TC],
                        start=(k == 0), stop=(k == NP - 1), skip_group_check=True),
                        r=[("hn", k), "w1g"], w=[("ps", bg)] if k in (0, NP - 1) else [])
                P.op("scalar", lambda e, fp=fp, bg=bg: e.activation(
                    gel, C.pb[bg][:, 0:TC], AF.Gelu, bias=b1g[:, fp:fp + 1]),
                    r=[("ps", bg), "b1g"], w=[("scr2", 3)])
                P.op("vector", lambda e, fp=fp, ba=ba: e.scalar_tensor_tensor(
                    hh[:, fp * 512:(fp + 1) * 512], C.pb[ba][:, 0:TC],
                    b1a[:, fp:fp + 1], gel, ALU.add, ALU.mult),
                    r=[("ps", ba), "b1a", ("scr2", 3)], w=[("hh", fp)])
            for op in range(NP):
                b2k = 4 + op % 2
                wc_ = w2c[op % 2]
                P.dma("gpsimd", lambda e, op=op, wc_=wc_, l=l: e.dma_start(
                    out=wc_.rearrange("p (k f) -> p k f", k=NK2),
                    in_=io[f"l0f_w2{l}"][:, op * 128:(op + 1) * 128].rearrange(
                        "(k p) f -> p k f", p=128)),
                    w=[("w2c", op % 2)])
                for k in range(NK2):
                    P.op("tensor", lambda e, k=k, op=op, b2k=b2k, wc_=wc_: e.matmul(
                        C.pb[b2k][:, 0:TC],
                        wc_[:, k * 128:(k + 1) * 128],
                        hh[:, k * 512:(k + 1) * 512],
                        start=(k == 0), stop=(k == NK2 - 1), skip_group_check=True),
                        r=[("hh", k), ("w2c", op % 2)],
                        w=[("ps", b2k)] if k in (0, NK2 - 1) else [])
                P.op("vector", lambda e, op=op, b2k=b2k, rb=rb: e.scalar_tensor_tensor(
                    rb[:, op * TC:(op + 1) * TC], C.pb[b2k][:, 0:TC],
                    b2c[:, op:op + 1], rb[:, op * TC:(op + 1) * TC],
                    ALU.add, ALU.add),
                    r=[("ps", b2k), "b2", ("rbuf", 0, op)], w=[("rbuf", 0, op)])
                if l == 0:
                    P.dma("sync", lambda e, op=op, t0=t0, rb=rb: e.dma_start(
                        out=XLH[op * 128:(op + 1) * 128, t0:t0 + TC],
                        in_=rb[:, op * TC:(op + 1) * TC]),
                        r=[("rbuf", 0, op)], w=[("XLH", tc)])
            if l == 1:
                for p in range(NP):
                    P.op("gpsimd", lambda e, p=p, rb=rb: e.tensor_copy(
                        sqb2[p], rb[:, p * TC:(p + 1) * TC]),
                        r=[("rbuf", 0, p)], w=[("lnsrc", p)])
                ln_stats(C, sqb2, 1e-5)
                ln_bcast(C, 4, 5)
                P.op("vector", lambda e, rsb=rsb: e.tensor_copy(rsb, C.pb[4][:, 0:TC]),
                     r=[("ps", 4)], w=[trs])
                P.op("vector", lambda e, nsb=nsb: e.tensor_copy(nsb, C.pb[5][:, 0:TC]),
                     r=[("ps", 5)], w=[tns])
                for p in range(NP):
                    a, ta = scr2[p % 2], ("scr2", p % 2)
                    P.op("gpsimd", lambda e, p=p, a=a, rsb=rsb, rb=rb: e.tensor_mul(
                        a, rb[:, p * TC:(p + 1) * TC], rsb),
                        r=[("rbuf", 0, p), trs], w=[ta])
                    P.op("gpsimd", lambda e, a=a, nsb=nsb: e.tensor_add(a, a, nsb),
                         r=[ta, tns], w=[ta])
                    P.op("vector", lambda e, p=p, a=a, rb2=rb2: e.tensor_add(
                        rb2[:, p * TC:(p + 1) * TC], a, rb2[:, p * TC:(p + 1) * TC]),
                        r=[ta], w=[("rbuf", 1, p)])
                    P.dma("sync", lambda e, p=p, t0=t0, rb2=rb2: e.dma_start(
                        out=R_out[p * 128:(p + 1) * 128, t0:t0 + TC],
                        in_=rb2[:, p * TC:(p + 1) * TC]),
                        r=[("rbuf", 1, p)], w=[("R", sidx + 1, tc)])
        P.barrier()
        if L0F_TRUNC == f"ffn{l}":
            return


SLOT_EMITTERS = {"l0a": emit_l0a, "l1a": emit_l1a, "l1f": emit_l1f,
                 "l0f": emit_l0f}


def build_program(g, slots):
    nc = bass.Bass("TRN2", target_bir_lowering=False)
    stack = ExitStack()
    io = {}
    io["x"] = nc.dram_tensor("x", [D, T], F32, kind="ExternalInput").ap()
    io["out"] = nc.dram_tensor("out", [D, T], F32, kind="ExternalOutput").ap()
    if DEBUG_TAPS:
        for nm, shp in (("dbg_hb", 4096), ("dbg_qb", 2048), ("dbg_kb", 4096),
                        ("dbg_vb", 4096), ("dbg_pn", 4096), ("dbg_ob", 2048),
                        ("dbg_pt", 4096)):
            io[nm] = nc.dram_tensor(nm, [128, shp], BF16, kind="ExternalOutput").ap()
        io["dbg_z"] = nc.dram_tensor("dbg_z", [D, T], BF16, kind="ExternalOutput").ap()
        io["dbg_kv"] = nc.dram_tensor("dbg_kv", [256, 512], BF16, kind="ExternalOutput").ap()
        io["dbg_q"] = nc.dram_tensor("dbg_q", [D, TC], BF16, kind="ExternalOutput").ap()
        io["dbg_o"] = nc.dram_tensor("dbg_o", [D, TC], BF16, kind="ExternalOutput").ap()
        io["dbg_s"] = nc.dram_tensor("dbg_s", [D, TC], F32, kind="ExternalOutput").ap()
    names = list(CONST_INPUTS)
    for s in slots:
        names += list(SLOT_INPUTS[s])
    for n in names:
        a = np.asarray(g[n])
        dt = BF16 if a.dtype == BF else F32
        io[n] = nc.dram_tensor(n, list(a.shape), dt, kind="ExternalInput").ap()
    R = [io["x"]]
    for i in range(len(slots) - 1):
        R.append(nc.dram_tensor(f"Rtmp{i}", [D, T], F32, kind="Internal").ap())
    R.append(io["out"])
    io["Zscratch"] = nc.dram_tensor("Zscratch", [D, T], BF16, kind="Internal").ap()
    io["HXscratch"] = nc.dram_tensor("HXscratch", [D, T], F32, kind="Internal").ap()
    io["ACCscratch"] = nc.dram_tensor("ACCscratch", [D, T], F32, kind="Internal").ap()
    io["WGATES"] = nc.dram_tensor("WGATES", [8, T], F32, kind="Internal").ap()

    P = Prog(nc, stack)
    C = setup_ctx(P, nc, stack, io)
    for i, s in enumerate(slots):
        SLOT_EMITTERS[s](C, R[i], R[i + 1], i)
        P.barrier()
    with nc.Block() as block:
        P.finalize_and_emit(block)
    stack.close()
    return nc, names


def run_slots(g, shards, slot_groups):
    cur = shards
    for grp in slot_groups:
        nc, names = build_program(g, grp)
        in_maps = []
        for c in range(NCORES):
            m = {n: np.asarray(g[n]) for n in names}
            m["x"] = cur[c]
            in_maps.append(m)
        res = run_bass_kernel_spmd(nc, in_maps, core_ids=list(range(NCORES)))
        if DEBUG_TAPS:
            globals()["LAST_RESULTS"] = res.results
        cur = [np.asarray(res.results[c]["out"]) for c in range(NCORES)]
    return cur


def assemble(cur):
    out = np.empty((NCORES * RPC, SEQ, D), np.float32)
    for c in range(NCORES):
        out[RPC * c:RPC * c + RPC] = np.asarray(cur[c], np.float32).reshape(
            D, RPC, SEQ).transpose(1, 2, 0)
    return out


def kernel(**inputs):
    g, shards = prep_inputs(inputs)
    cur = run_slots(g, shards, [["l0a", "l0f", "l1a", "l1f"]])
    return assemble(cur)


# version 03
# speedup vs baseline: 1.0199x; 1.0199x over previous
import numpy as np
import ml_dtypes
import concourse.bass as bass
import concourse.mybir as mybir
from concourse.bass_utils import run_bass_kernel_spmd
from contextlib import ExitStack

F32 = mybir.dt.float32
BF16 = mybir.dt.bfloat16
AF = mybir.ActivationFunctionType
ALU = mybir.AluOpType
AX = mybir.AxisListType
BF = ml_dtypes.bfloat16

NCORES = 8
D = 1024
NP = 8
SEQ = 2048
RPC = 2
T = RPC * SEQ
TC = 512
NTC = T // TC
CPR = SEQ // TC
BIG_NEG = -1e30
DEBUG_TAPS = False
SAME_ENGINE = True
L0F_TRUNC = None
ENGINES = ("sync", "tensor", "vector", "scalar", "gpsimd")
DMA_Q = {"sync": "qS", "scalar": "qC", "gpsimd": "qG"}


NDMASEM = 72
NSW = 20


class Prog:
    def __init__(self, nc, stack):
        self.nc = nc
        self.ops = {e: [] for e in ENGINES}
        self.hdl = {}
        for name in ("eT", "eV", "eS", "eG", "eY", "PH"):
            self.hdl[name] = stack.enter_context(nc.semaphore(name))
        for i in range(NDMASEM):
            self.hdl[f"d{i}"] = stack.enter_context(nc.semaphore(f"d{i}"))
        self.esem = {"tensor": "eT", "vector": "eV", "scalar": "eS",
                     "gpsimd": "eG", "sync": "eY"}
        self.last_w = {}
        self.readers = {}
        self.hw_free = [f"d{i}" for i in range(NDMASEM - NSW)]
        self.sw_free = [f"d{i}" for i in range(NDMASEM - NSW, NDMASEM)]
        self.dma_map = {}
        self.dma_cnt = {f"d{i}": 0 for i in range(NDMASEM)}
        self.dma_last = {}
        self.dma_sem_last = {}
        self.sw_used = set()
        self.ph = 0

    def op(self, eng, build, r=(), w=(), deps=(), dma=False):
        idx = len(self.ops[eng])
        dep = set(deps)
        for t in r:
            lw = self.last_w.get(t)
            if lw is not None:
                dep.add(lw)
            self.readers.setdefault(t, []).append((eng, idx))
        for t in w:
            lw = self.last_w.get(t)
            if lw is not None:
                dep.add(lw)
            for rd in self.readers.get(t, ()):
                dep.add(rd)
            self.last_w[t] = (eng, idx)
            self.readers[t] = []
        dep.discard((eng, idx))
        kept = set()
        for e2, i2 in dep:
            if self.ops[e2][i2]["dma"]:
                kept.add((e2, i2))
            elif e2 != eng or (SAME_ENGINE and eng in ("vector", "scalar", "gpsimd")):
                kept.add((e2, i2))
        byeng = {}
        dmadep = set()
        for e2, i2 in kept:
            if self.ops[e2][i2]["dma"]:
                dmadep.add((e2, i2))
            else:
                byeng[e2] = max(byeng.get(e2, -1), i2)
        self.ops[eng].append({"b": build, "dep": byeng, "ddep": dmadep, "dma": dma})
        return (eng, idx)

    def dma(self, queue, build, r=(), w=(), deps=()):
        assert w, "dma op needs a primary write token"
        key = (queue == "gpsimd", w[0])
        deps = list(deps)
        sem = self.dma_map.get(key)
        if sem is None:
            pool = self.sw_free if queue == "gpsimd" else self.hw_free
            assert pool, "out of dma semaphores (add a barrier)"
            sem = pool.pop()
            self.dma_map[key] = sem
            prev = self.dma_sem_last.get(sem)
            if prev is not None:
                deps.append(prev)
        if queue == "gpsimd":
            self.sw_used.add(sem)
        eng, idx = self.op(queue, build, r=r, w=w, deps=deps, dma=True)
        self.dma_cnt[sem] += 16
        o = self.ops[eng][idx]
        o["dsem"] = sem
        o["dcnt"] = self.dma_cnt[sem]
        self.dma_last[key] = (eng, idx)
        self.dma_sem_last[sem] = (eng, idx)
        return (eng, idx)

    def last_real(self, eng):
        ops = self.ops[eng]
        for i in range(len(ops) - 1, -1, -1):
            if ops[i]["b"] is not None:
                return i
        return -1

    def barrier(self):
        snap = {e: self.last_real(e) for e in ENGINES}
        dsnap = set(self.dma_last.values())
        for e in ENGINES:
            deps = {(e2, i2) for e2, i2 in snap.items() if e2 != e and i2 >= 0}
            deps |= set(dsnap)
            self.op(e, None, deps=deps)
        self.dma_map.clear()
        self.dma_last.clear()
        self.hw_free = [f"d{i}" for i in range(NDMASEM - NSW)]
        self.sw_free = [f"d{i}" for i in range(NDMASEM - NSW, NDMASEM)]
        self.last_w.clear()
        self.readers.clear()

    def finalize_and_emit(self, block):
        needed = {e: set() for e in ENGINES}
        for e in ENGINES:
            for o in self.ops[e]:
                for e2, i2 in o["dep"].items():
                    assert self.ops[e2][i2]["b"] is not None
                    assert not self.ops[e2][i2]["dma"]
                    needed[e2].add(i2)
        cum = {}
        for e in ENGINES:
            cn, c = [], 0
            for i, o in enumerate(self.ops[e]):
                if (not o["dma"]) and i in needed[e]:
                    c += 1
                cn.append(c)
            cum[e] = cn
            assert c < 60000, (e, c)
        for s, v in self.dma_cnt.items():
            assert v < 900000, (s, v)

        def runner(e):
            ops, need, esem = self.ops[e], needed[e], self.hdl[self.esem[e]]
            hw = {}

            def f(eng):
                for i, o in enumerate(ops):
                    for s in o.get("hwreset", ()):
                        hw.pop(s, None)
                    waits = [(self.esem[e2], cum[e2][i2])
                             for e2, i2 in o["dep"].items()]
                    waits += [(self.ops[e2][i2]["dsem"], self.ops[e2][i2]["dcnt"])
                              for e2, i2 in o["ddep"]]
                    for s, v in sorted(waits):
                        if v > 0 and hw.get(s, 0) < v:
                            eng.wait_ge(self.hdl[s], v)
                            hw[s] = v
                    if o["b"] is not None:
                        ins = o["b"](eng)
                        if o["dma"]:
                            ins.then_inc(self.hdl[o["dsem"]], 16)
                        elif i in need:
                            ins.then_inc(esem, 1)
            return f

        block.sync(runner("sync"))
        block.tensor(runner("tensor"))
        block.vector(runner("vector"))
        block.scalar(runner("scalar"))
        block.gpsimd(runner("gpsimd"))


class Carver:
    def __init__(self, ap, size):
        self.ap, self.size, self.off = ap, size, 0

    def bf(self, nelem):
        nelem = (nelem + 31) & ~31
        assert self.off + nelem <= self.size, (self.off, nelem)
        ap = self.ap[:, self.off:self.off + nelem]
        self.off += nelem
        return ap

    def f32(self, nelem):
        return self.bf(2 * nelem).bitcast(F32)

    def reset(self):
        self.off = 0


def _cc(a):
    return np.ascontiguousarray(a)


def _col_tiles(b, np_=NP):
    return _cc(np.asarray(b, np.float32).reshape(np_, 128).T)


def prep_inputs(inputs):
    g = {}
    g["ones_col_bf"] = np.ones((128, 1), dtype=BF)
    g["ones_row_f32"] = np.ones((1, 128), np.float32)
    g["ones_row_bf"] = np.ones((1, 128), dtype=BF)
    g["ident_bf"] = np.eye(128, dtype=np.float32).astype(BF)
    g["ident_f32"] = np.eye(128, dtype=np.float32)
    g["l0a_wqT"] = _cc(np.asarray(inputs["L0_attn_Wq"], np.float32).T)
    g["l0a_wkT"] = _cc(np.asarray(inputs["L0_attn_Wk"], np.float32).T)
    g["l0a_wvT"] = _cc(np.asarray(inputs["L0_attn_Wv"], np.float32).T)
    g["l0a_woT"] = _cc(np.asarray(inputs["L0_attn_Wo"], np.float32).T)
    g["l0a_bq"] = _col_tiles(inputs["L0_attn_bq"])
    g["l0a_bo"] = _col_tiles(inputs["L0_attn_bo"])
    g["l0a_bk_row"] = _cc(np.asarray(inputs["L0_attn_bk"], np.float32)[None, :])
    g["l0a_bv_row"] = _cc(np.asarray(inputs["L0_attn_bv"], np.float32)[None, :])
    g["l1a_wgT"] = _cc(np.asarray(inputs["L1_attn_Wg"], np.float32).T)
    g["l1a_wcT"] = _cc(np.asarray(inputs["L1_attn_Wc"], np.float32).T)
    g["l1a_w1T"] = _cc(np.asarray(inputs["L1_attn_w1"], np.float32).T)
    g["l1a_w2T"] = _cc(np.asarray(inputs["L1_attn_w2"], np.float32).T)
    g["l1a_w3T"] = _cc(np.asarray(inputs["L1_attn_w3"], np.float32).T)
    g["l1a_lnw"] = _col_tiles(inputs["L1_attn_ln_w"])
    g["l1a_ffnw"] = _col_tiles(inputs["L1_attn_ffn_w"])
    DH, W0 = 64, 512
    perm = np.empty(DH, dtype=np.int64)
    perm[0::2] = np.arange(32)
    perm[1::2] = np.arange(32, 64)
    qk_perm = np.concatenate([h * DH + perm for h in range(8)])
    half = np.arange(0, DH, 2, dtype=np.float64)
    inv_freq = 1.0 / 10000.0 ** (half / DH)
    t_ = np.arange(SEQ, dtype=np.float64)
    freqs = np.concatenate([t_[:, None] * inv_freq[None, :]] * 2, axis=1)
    base = (half + 0.4 * DH) / (1.4 * DH)
    power = (t_ - SEQ // 2) / (W0 // 2)
    scale = np.concatenate([base[None, :] ** power[:, None]] * 2, axis=1)
    sign = np.ones(DH)
    sign[:32] = -1.0
    A_q = (np.cos(freqs) * scale) * 8.0
    B_q = (np.sin(freqs) * scale) * sign[None, :] * 8.0
    A_k = np.cos(freqs) / scale
    B_k = (np.sin(freqs) / scale) * sign[None, :]
    for nm, M in (("Aq", A_q), ("Bq", B_q), ("Ak", A_k), ("Bk", B_k)):
        g["l0f_" + nm] = _cc(np.tile(M[:, perm].T.astype(np.float32), (2, 1)))
    for qt in range(4):
        qi = (qt * 128 + np.arange(128))[:, None]
        kj = np.arange(512)[None, :]
        mprev = np.where(kj >= qi, 0.0, BIG_NEG)
        mown = np.where(kj <= qi, 0.0, BIG_NEG)
        g[f"l0f_mask{qt}"] = np.concatenate([mprev, mown], axis=1).astype(BF)
    g["l0f_mask_all"] = np.full((128, 512), BIG_NEG, np.float32).astype(BF)
    hm2 = np.zeros((128, 2), np.float32)
    hm2[0:64, 0] = 1.0
    hm2[64:128, 1] = 1.0
    g["l0f_hm2"] = hm2.astype(BF)
    g["l0f_hbc"] = _cc(hm2.T)
    wqkv = np.asarray(inputs["L0_ffn_wqkv"], np.float32)
    colperm = np.concatenate([qk_perm, 512 + qk_perm, np.arange(1024, 1536)])
    HIDP = 2816
    for l in range(2):
        g[f"l0f_wqkv{l}"] = _cc(wqkv[l][:, colperm])
        g[f"l0f_wo{l}"] = _cc(np.asarray(inputs["L0_ffn_wo"], np.float32)[l])
        qs = np.asarray(inputs["L0_ffn_q_scale"], np.float32)[l][perm]
        ks = np.asarray(inputs["L0_ffn_k_scale"], np.float32)[l][perm]
        g[f"l0f_qs{l}"] = _cc(np.tile(qs, 2)[:, None])
        g[f"l0f_ks{l}"] = _cc(np.tile(ks, 2)[:, None])
        g[f"l0f_lng{l}"] = _col_tiles(np.asarray(inputs["L0_ffn_ln_g"], np.float32)[l])
        g[f"l0f_lnb{l}"] = _col_tiles(np.asarray(inputs["L0_ffn_ln_b"], np.float32)[l])
        w1 = np.asarray(inputs["L0_ffn_w1"], np.float32)[l]
        b1 = np.asarray(inputs["L0_ffn_b1"], np.float32)[l]
        w1a = np.zeros((1024, HIDP), np.float32)
        w1g = np.zeros((1024, HIDP), np.float32)
        w1a[:, :2730] = w1[:, :2730]
        w1g[:, :2730] = w1[:, 2730:]
        b1a = np.zeros(HIDP, np.float32)
        b1g = np.zeros(HIDP, np.float32)
        b1a[:2730] = b1[:2730]
        b1g[:2730] = b1[2730:]
        g[f"l0f_w1a{l}"] = w1a
        g[f"l0f_w1g{l}"] = w1g
        g[f"l0f_b1a{l}"] = _col_tiles(b1a, 22)
        g[f"l0f_b1g{l}"] = _col_tiles(b1g, 22)
        w2 = np.asarray(inputs["L0_ffn_w2"], np.float32)[l]
        w2p = np.zeros((HIDP, 1024), np.float32)
        w2p[:2730] = w2
        g[f"l0f_w2{l}"] = w2p
        g[f"l0f_b2{l}"] = _col_tiles(np.asarray(inputs["L0_ffn_b2"], np.float32)[l])
    g["l1f_rwT"] = _cc(np.asarray(inputs["L1_ffn_router_w"], np.float32).T)
    g["l1f_rb"] = _cc(np.asarray(inputs["L1_ffn_router_b"], np.float32)[None, :])
    gw = np.asarray(inputs["L1_ffn_gate_w"], np.float32)
    uw = np.asarray(inputs["L1_ffn_up_w"], np.float32)
    dw = np.asarray(inputs["L1_ffn_down_w"], np.float32)
    g["l1f_gwT"] = _cc(gw.transpose(0, 2, 1).reshape(8 * 1024, 4096))
    g["l1f_uwT"] = _cc(uw.transpose(0, 2, 1).reshape(8 * 1024, 4096))
    g["l1f_dwT"] = _cc(dw.transpose(0, 2, 1).reshape(8 * 4096, 1024))
    x = np.asarray(inputs["L0_attn_x"], np.float32)
    shards = [_cc(x[RPC * c: RPC * c + RPC].transpose(2, 0, 1).reshape(D, T))
              for c in range(NCORES)]
    return g, shards


SLOT_INPUTS = {
    "l0a": ("l0a_wqT", "l0a_wkT", "l0a_wvT", "l0a_woT",
            "l0a_bq", "l0a_bo", "l0a_bk_row", "l0a_bv_row"),
    "l1a": ("l1a_wgT", "l1a_wcT", "l1a_w1T", "l1a_w2T", "l1a_w3T",
            "l1a_lnw", "l1a_ffnw"),
    "l1f": ("l1f_rwT", "l1f_rb", "l1f_gwT", "l1f_uwT", "l1f_dwT"),
    "l0f": tuple(["l0f_Aq", "l0f_Bq", "l0f_Ak", "l0f_Bk", "l0f_mask_all",
                  "l0f_hm2", "l0f_hbc"]
                 + [f"l0f_mask{qt}" for qt in range(4)]
                 + [f"l0f_{nm}{l}" for l in range(2)
                    for nm in ("wqkv", "wo", "qs", "ks", "lng", "lnb",
                               "w1a", "w1g", "b1a", "b1g", "w2", "b2")]),
}
CONST_INPUTS = ("ones_col_bf", "ones_row_f32", "ones_row_bf", "ident_bf", "ident_f32")


class Ctx:
    pass


def setup_ctx(P, nc, stack, io):
    C = Ctx()
    C.P, C.nc, C.io = P, nc, io
    sb = lambda n, s, d: stack.enter_context(nc.sbuf_tensor(n, s, d))
    ps = lambda n, s, d: stack.enter_context(nc.psum_tensor(n, s, d))
    C.pb = [ps(f"PB{i}", [128, 512], F32)[:] for i in range(7)]
    C.pt = ps("PT", [128, 1024], BF16)[:]
    ar = sb("ARENA", [128, 82944], BF16)
    C.cv = Carver(ar[:], 82944)
    C.rbuf = [sb(f"RBUF{i}", [128, NP * TC], F32)[:] for i in range(2)]
    C.rows = {n: sb("row_" + n, [1, TC], F32)[:]
              for n in ("mu", "ms", "tmp", "rstd", "nmr", "x1")}
    C.ones_col_bf = sb("c_ones_col", [128, 1], BF16)[:]
    C.ones_row_f32 = sb("c_ones_row", [1, 128], F32)[:]
    C.ones_row_bf = sb("c_ones_row_bf", [1, 128], BF16)[:]
    C.ident_bf = sb("c_ident_bf", [128, 128], BF16)[:]
    C.ident_f32 = sb("c_ident_f32", [128, 128], F32)[:]
    C.smalls = sb("SMALLS", [128, 64], F32)[:]
    C.smalls_used = 0
    for name, dst in (("ones_col_bf", C.ones_col_bf), ("ones_row_f32", C.ones_row_f32),
                      ("ones_row_bf", C.ones_row_bf),
                      ("ident_bf", C.ident_bf), ("ident_f32", C.ident_f32)):
        P.dma("sync", lambda e, dst=dst, name=name: e.dma_start(out=dst, in_=io[name]),
              w=[("const", name)])
    C.OCB = ("const", "ones_col_bf")
    C.ORF = ("const", "ones_row_f32")
    C.ORB = ("const", "ones_row_bf")
    return C


def small_cols(C, name, ncols, tok):
    o = C.smalls_used
    C.smalls_used += ncols
    assert C.smalls_used <= 64
    dst = C.smalls[:, o:o + ncols]
    C.P.dma("sync", lambda e, dst=dst, name=name: e.dma_start(out=dst, in_=C.io[name]),
            w=[tok])
    return dst


def ln_stats(C, stat_bf, eps, rms_only=False, double_rms=None):
    P, rows = C.P, C.rows
    bsum, bsq = C.pb[6], C.pb[3]
    if not rms_only:
        for p in range(NP):
            P.op("tensor", lambda e, p=p: e.matmul(
                bsum[0:1, 0:TC], C.ones_col_bf, stat_bf[p],
                start=(p == 0), stop=(p == NP - 1), skip_group_check=True),
                r=[("lnsrc", p), C.OCB], w=[("ps", 6)] if p in (0, NP - 1) else [])
    for p in range(NP):
        P.op("gpsimd", lambda e, p=p: e.tensor_mul(stat_bf[p], stat_bf[p], stat_bf[p]),
             r=[("lnsrc", p)], w=[("lnsrc", p)])
        P.op("tensor", lambda e, p=p: e.matmul(
            bsq[0:1, 0:TC], C.ones_col_bf, stat_bf[p],
            start=(p == 0), stop=(p == NP - 1), skip_group_check=True),
            r=[("lnsrc", p), C.OCB],
            w=[("ps", 3)] if p in (0, NP - 1) else [])
    if not rms_only:
        P.op("vector", lambda e: e.tensor_scalar_mul(rows["mu"], bsum[0:1, 0:TC], 1.0 / D),
             r=[("ps", 6)], w=[("row", "mu")])
        P.op("vector", lambda e: e.tensor_scalar_mul(rows["ms"], bsq[0:1, 0:TC], 1.0 / D),
             r=[("ps", 3)], w=[("row", "ms")])
        P.op("vector", lambda e: e.tensor_mul(rows["tmp"], rows["mu"], rows["mu"]),
             r=[("row", "mu")], w=[("row", "tmp")])
        P.op("vector", lambda e: e.tensor_sub(rows["ms"], rows["ms"], rows["tmp"]),
             r=[("row", "tmp")], w=[("row", "ms")])
    else:
        P.op("vector", lambda e: e.tensor_scalar_mul(rows["ms"], bsq[0:1, 0:TC], 1.0 / D),
             r=[("ps", 3)], w=[("row", "ms")])
    if double_rms is not None:
        eo, ei = double_rms
        P.op("vector", lambda e: e.tensor_scalar(
            rows["tmp"], rows["ms"], 1.0 + ei, ei * eo, ALU.mult, ALU.add),
            r=[("row", "ms")], w=[("row", "tmp")])
    else:
        P.op("vector", lambda e: e.tensor_scalar_add(rows["tmp"], rows["ms"], float(eps)),
             r=[("row", "ms")], w=[("row", "tmp")])
    P.op("scalar", lambda e: e.activation(rows["tmp"], rows["tmp"], AF.Sqrt),
         r=[("row", "tmp")], w=[("row", "tmp")])
    P.op("vector", lambda e: e.reciprocal(rows["rstd"], rows["tmp"]),
         r=[("row", "tmp")], w=[("row", "rstd")])
    if not rms_only:
        P.op("vector", lambda e: e.tensor_mul(rows["nmr"], rows["mu"], rows["rstd"]),
             r=[("row", "mu"), ("row", "rstd")], w=[("row", "nmr")])
        P.op("vector", lambda e: e.tensor_scalar_mul(rows["nmr"], rows["nmr"], -1.0),
             w=[("row", "nmr")])


def ln_bcast(C, bankA, bankB=None, rowA="rstd", rowB="nmr"):
    P, rows = C.P, C.rows
    P.op("tensor", lambda e, bankA=bankA, rowA=rowA: e.matmul(
        C.pb[bankA][:, 0:TC], C.ones_row_f32, rows[rowA], start=True, stop=True),
        r=[("row", rowA), C.ORF], w=[("ps", bankA)])
    if rowB is not None:
        P.op("tensor", lambda e, bankB=bankB, rowB=rowB: e.matmul(
            C.pb[bankB][:, 0:TC], C.ones_row_f32, rows[rowB], start=True, stop=True),
            r=[("row", rowB), C.ORF], w=[("ps", bankB)])


def emit_l0a(C, R_in, R_out, sidx):
    P, io, cv = C.P, C.io, C.cv
    cv.reset()
    zb = [cv.bf(T) for _ in range(NP)]
    w0, w1 = cv.bf(8192), cv.bf(8192)
    kvsb = [cv.bf(512) for _ in range(2)]
    ktm = [cv.bf(1024) for _ in range(2)]
    vtm = [cv.bf(1024) for _ in range(2)]
    qbf = [cv.bf(TC) for _ in range(NP)]
    obf = [cv.bf(TC) for _ in range(NP)]
    sqb = [cv.bf(TC) for _ in range(NP)]
    bias_bc = cv.f32(2048)
    scr = [cv.f32(TC) for _ in range(4)]
    sS = [cv.f32(TC) for _ in range(NP)]

    def S(i):
        return scr[i], ("scr", i)

    def wload(dst, name, tok):
        nk = io[name].shape[0] // 128
        src = io[name].rearrange("(k p) f -> p k f", p=128)
        dst3 = dst.rearrange("p (k f) -> p k f", k=nk)
        P.dma("gpsimd", lambda e, dst3=dst3, src=src: e.dma_start(out=dst3, in_=src),
              w=[tok])

    wload(w0, "l0a_wkT", "wk")
    wload(w1, "l0a_wvT", "wv")
    bq_col = small_cols(C, "l0a_bq", NP, "bqcol")
    bo_col = small_cols(C, "l0a_bo", NP, "bocol")
    P.dma("sync", lambda e: e.dma_start(out=bias_bc[0:1, 0:1024], in_=io["l0a_bk_row"]),
          w=["biasbc"])
    P.dma("sync", lambda e: e.dma_start(out=bias_bc[0:1, 1024:2048], in_=io["l0a_bv_row"]),
          w=["biasbc"])
    for h in range(4):
        P.op("tensor", lambda e, h=h: e.matmul(
            C.pb[5][:, 0:512], C.ones_row_f32, bias_bc[0:1, h * 512:(h + 1) * 512],
            start=True, stop=True),
            r=["biasbc", C.ORF], w=[("ps", 5)])
        P.op("vector", lambda e, h=h: e.tensor_copy(
            bias_bc[:, h * 512:(h + 1) * 512], C.pb[5][:, 0:512]),
            r=[("ps", 5)], w=["biasbc"])

    for c in range(NTC):
        rb = C.rbuf[c % 2]
        for p in range(NP):
            P.dma("sync", lambda e, p=p, c=c, rb=rb: e.dma_start(
                out=rb[:, p * TC:(p + 1) * TC],
                in_=R_in[p * 128:(p + 1) * 128, c * TC:(c + 1) * TC]),
                r=[("R", sidx, c)], w=[("rbuf", c % 2, p)])
        for p in range(NP):
            P.op("vector", lambda e, p=p, rb=rb: e.tensor_copy(
                sqb[p], rb[:, p * TC:(p + 1) * TC]),
                r=[("rbuf", c % 2, p)], w=[("lnsrc", p)])
        ln_stats(C, sqb, 1e-5)
        ln_bcast(C, 4, 5)
        rsb, trs = S(2)
        nsb, tns = S(3)
        P.op("vector", lambda e, rsb=rsb: e.tensor_copy(rsb, C.pb[4][:, 0:TC]),
             r=[("ps", 4)], w=[trs])
        P.op("vector", lambda e, nsb=nsb: e.tensor_copy(nsb, C.pb[5][:, 0:TC]),
             r=[("ps", 5)], w=[tns])
        for p in range(NP):
            a, ta = S(p % 2)
            P.op("gpsimd", lambda e, p=p, rb=rb, a=a, rsb=rsb: e.tensor_mul(
                a, rb[:, p * TC:(p + 1) * TC], rsb),
                r=[("rbuf", c % 2, p), trs], w=[ta])
            P.op("vector", lambda e, p=p, c=c, a=a, nsb=nsb: e.tensor_add(
                zb[p][:, c * TC:(c + 1) * TC], a, nsb),
                r=[ta, tns], w=[("zb", p)])

    scnt = 0
    for r in range(RPC):
        kvbank = 2 + r
        nts = SEQ // 128
        for ts in range(nts):
            tok0 = r * SEQ + ts * 128
            kbuf, vbuf = ktm[ts % 2], vtm[ts % 2]
            for half in range(2):
                for which in ("k", "v"):
                    wsb = w0 if which == "k" else w1
                    wtok = "wk" if which == "k" else "wv"
                    bank = (ts * 2 + half) % 2
                    fo = half * 512
                    for k in range(NP):
                        P.op("tensor", lambda e, k=k, fo=fo, wsb=wsb, bank=bank, tok0=tok0: e.matmul(
                            C.pb[bank][:, 0:512],
                            zb[k][:, tok0:tok0 + 128],
                            wsb[:, k * 1024 + fo: k * 1024 + fo + 512],
                            start=(k == 0), stop=(k == NP - 1), skip_group_check=True),
                            r=[("zb", k), wtok],
                            w=[("ps", bank)] if k in (0, NP - 1) else [])
                    if which == "v":
                        P.op("vector", lambda e, vbuf=vbuf, fo=fo, bank=bank, half=half: e.tensor_add(
                            vbuf[:, fo:fo + 512], C.pb[bank][:, 0:512],
                            bias_bc[:, 1024 + fo:1024 + fo + 512]),
                            r=[("ps", bank), "biasbc"],
                            w=[("vtm", ts % 2, half)])
                    else:
                        si = (scnt % 2) * 2
                        scnt += 1
                        a, ta = S(si)
                        b, tb = S(si + 1)
                        P.op("vector", lambda e, a=a, fo=fo, bank=bank: e.tensor_add(
                            a, C.pb[bank][:, 0:512], bias_bc[:, fo:fo + 512]),
                            r=[("ps", bank), "biasbc"], w=[ta])
                        P.op("gpsimd", lambda e, a=a, b=b: e.tensor_scalar_min(b, a, 0.0),
                             r=[ta], w=[tb])
                        P.op("scalar", lambda e, b=b: e.activation(b, b, AF.Exp),
                             r=[tb], w=[tb])
                        P.op("gpsimd", lambda e, a=a: e.tensor_scalar_max(a, a, 0.0),
                             r=[ta], w=[ta])
                        P.op("vector", lambda e, kbuf=kbuf, fo=fo, a=a, b=b: e.tensor_add(
                            kbuf[:, fo:fo + 512], a, b),
                            r=[ta, tb], w=[("ktm", ts % 2, half)])
            for h in range(16):
                pb_ = h % 2
                fs = 64 * (h // 2)
                first = ts == 0 and h in (0, 1)
                last = ts == nts - 1 and h in (14, 15)
                P.op("tensor", lambda e, pb_=pb_, fs=fs, h=h, kbuf=kbuf, vbuf=vbuf, kvbank=kvbank, first=first, last=last: e.matmul(
                    C.pb[kvbank][64 * pb_:64 * pb_ + 64, fs:fs + 64],
                    kbuf[:, 64 * h:64 * h + 64],
                    vbuf[:, 64 * h:64 * h + 64],
                    start=first, stop=last, skip_group_check=True),
                    r=[("ktm", ts % 2, h // 8), ("vtm", ts % 2, h // 8)],
                    w=[("ps", kvbank)] if (first or last) else [])
        P.op("vector", lambda e, r=r, kvbank=kvbank: e.tensor_copy(
            kvsb[r], C.pb[kvbank][:, 0:512]),
            r=[("ps", kvbank)], w=[("kvsb", r)])

    wload(w0, "l0a_wqT", "wk")
    wload(w1, "l0a_woT", "wv")
    for c in range(NTC):
        r = c // CPR
        t0 = c * TC
        rb = C.rbuf[c % 2]
        for p in range(NP):
            P.dma("sync", lambda e, p=p, t0=t0, rb=rb: e.dma_start(
                out=rb[:, p * TC:(p + 1) * TC],
                in_=R_in[p * 128:(p + 1) * 128, t0:t0 + TC]),
                r=[("R", sidx, c)], w=[("rbuf", c % 2, p)])
        for fp in range(NP):
            bank = fp % 2
            for k in range(NP):
                P.op("tensor", lambda e, k=k, fp=fp, bank=bank, t0=t0: e.matmul(
                    C.pb[bank][:, 0:TC],
                    w0[:, k * 1024 + fp * 128: k * 1024 + fp * 128 + 128],
                    zb[k][:, t0:t0 + TC],
                    start=(k == 0), stop=(k == NP - 1), skip_group_check=True),
                    r=[("zb", k), "wk"], w=[("ps", bank)] if k in (0, NP - 1) else [])
            si = (fp % 2) * 2
            a, ta = S(si)
            b, tb = S(si + 1)
            P.op("vector", lambda e, a=a, bank=bank, fp=fp: e.tensor_scalar(
                a, C.pb[bank][:, 0:TC], bq_col[:, fp:fp + 1], 0.0, ALU.add, ALU.min),
                r=[("ps", bank), "bqcol"], w=[ta])
            P.op("scalar", lambda e, a=a: e.activation(a, a, AF.Exp),
                 r=[ta], w=[ta])
            P.op("vector", lambda e, b=b, bank=bank, fp=fp: e.tensor_scalar(
                b, C.pb[bank][:, 0:TC], bq_col[:, fp:fp + 1], 0.0, ALU.add, ALU.max),
                r=[("ps", bank), "bqcol"], w=[tb])
            P.op("vector", lambda e, a=a, b=b, fp=fp: e.tensor_add(qbf[fp], a, b),
                 r=[ta, tb], w=[("qbf", fp)])
        for pair in range(8):
            bank = 4 + pair % 2
            for sub in range(2):
                h = 2 * pair + sub
                P.op("tensor", lambda e, h=h, bank=bank, r=r, sub=sub: e.matmul(
                    C.pb[bank][64 * (h % 2):64 * (h % 2) + 64, 0:TC],
                    kvsb[r][64 * (h % 2):64 * (h % 2) + 64,
                            64 * (h // 2):64 * (h // 2) + 64],
                    qbf[h // 2][64 * (h % 2):64 * (h % 2) + 64, :],
                    start=True, stop=True, skip_group_check=True),
                    r=[("kvsb", r), ("qbf", h // 2)], w=[("ps", bank)])
            P.op("vector", lambda e, pair=pair, bank=bank: e.tensor_copy(
                obf[pair], C.pb[bank][:, 0:TC]),
                r=[("ps", bank)], w=[("obf", pair)])
        for fp in range(NP):
            bank = fp % 2
            for k in range(NP):
                P.op("tensor", lambda e, k=k, fp=fp, bank=bank: e.matmul(
                    C.pb[bank][:, 0:TC],
                    w1[:, k * 1024 + fp * 128: k * 1024 + fp * 128 + 128],
                    obf[k], start=(k == 0), stop=(k == NP - 1), skip_group_check=True),
                    r=[("obf", k), "wv"], w=[("ps", bank)] if k in (0, NP - 1) else [])
            P.op("vector", lambda e, fp=fp, bank=bank: e.tensor_scalar_add(
                sS[fp], C.pb[bank][:, 0:TC], bo_col[:, fp:fp + 1]),
                r=[("ps", bank), "bocol"], w=[("sS", fp)])
        for p in range(NP):
            P.op("gpsimd", lambda e, p=p: e.tensor_copy(sqb[p], sS[p]),
                 r=[("sS", p)], w=[("lnsrc", p)])
        ln_stats(C, sqb, 1e-5)
        ln_bcast(C, 4, 5)
        rsb, trs = S(2)
        nsb, tns = S(3)
        P.op("vector", lambda e, rsb=rsb: e.tensor_copy(rsb, C.pb[4][:, 0:TC]),
             r=[("ps", 4)], w=[trs])
        P.op("vector", lambda e, nsb=nsb: e.tensor_copy(nsb, C.pb[5][:, 0:TC]),
             r=[("ps", 5)], w=[tns])
        for p in range(NP):
            a, ta = S(p % 2)
            P.op("gpsimd", lambda e, p=p, a=a, rsb=rsb: e.tensor_mul(a, sS[p], rsb),
                 r=[("sS", p), trs], w=[ta])
            P.op("gpsimd", lambda e, a=a, nsb=nsb: e.tensor_add(a, a, nsb),
                 r=[ta, tns], w=[ta])
            P.op("vector", lambda e, p=p, rb=rb, a=a: e.tensor_add(
                rb[:, p * TC:(p + 1) * TC], a, rb[:, p * TC:(p + 1) * TC]),
                r=[ta], w=[("rbuf", c % 2, p)])
            P.dma("sync", lambda e, p=p, t0=t0, rb=rb: e.dma_start(
                out=R_out[p * 128:(p + 1) * 128, t0:t0 + TC],
                in_=rb[:, p * TC:(p + 1) * TC]),
                r=[("rbuf", c % 2, p)], w=[("R", sidx + 1, c)])
    if DEBUG_TAPS:
        for p in range(NP):
            P.dma("gpsimd", lambda e, p=p: e.dma_start(
                out=io["dbg_z"][p * 128:(p + 1) * 128, :], in_=zb[p]),
                r=[("zb", p)])
        for r_ in range(2):
            P.dma("gpsimd", lambda e, r_=r_: e.dma_start(
                out=io["dbg_kv"][r_ * 128:(r_ + 1) * 128, :], in_=kvsb[r_]),
                r=[("kvsb", r_)])
        for p in range(NP):
            P.dma("gpsimd", lambda e, p=p: e.dma_start(
                out=io["dbg_q"][p * 128:(p + 1) * 128, :], in_=qbf[p]),
                r=[("qbf", p)])
            P.dma("gpsimd", lambda e, p=p: e.dma_start(
                out=io["dbg_o"][p * 128:(p + 1) * 128, :], in_=obf[p]),
                r=[("obf", p)])
            P.dma("sync", lambda e, p=p: e.dma_start(
                out=io["dbg_s"][p * 128:(p + 1) * 128, :], in_=sS[p]),
                r=[("sS", p)])


def emit_l1a(C, R_in, R_out, sidx):
    P, io, cv = C.P, C.io, C.cv
    Z = io["Zscratch"]
    lnw_col = small_cols(C, "l1a_lnw", NP, "lnwcol")
    ffnw_col = small_cols(C, "l1a_ffnw", NP, "ffnwcol")

    def wload(dst, name, tok):
        nk = io[name].shape[0] // 128
        src_ = io[name].rearrange("(k p) f -> p k f", p=128)
        dst3 = dst.rearrange("p (k f) -> p k f", k=nk)
        P.dma("gpsimd", lambda e, dst3=dst3, src_=src_: e.dma_start(out=dst3, in_=src_),
              w=[tok])

    def S(i):
        return scr[i], ("scr", i)

    cv.reset()
    zst = cv.bf(4096)
    sqb = [cv.bf(TC) for _ in range(NP)]
    scr = [cv.f32(TC) for _ in range(4)]
    Zv = Z.rearrange("(k p) t -> p k t", p=128)
    for c in range(NTC):
        rb = C.rbuf[c % 2]
        for p in range(NP):
            P.dma("sync", lambda e, p=p, c=c, rb=rb: e.dma_start(
                out=rb[:, p * TC:(p + 1) * TC],
                in_=R_in[p * 128:(p + 1) * 128, c * TC:(c + 1) * TC]),
                r=[("R", sidx, c)], w=[("rbuf", c % 2, p)])
        for p in range(NP):
            P.op("vector", lambda e, p=p, rb=rb: e.tensor_copy(
                sqb[p], rb[:, p * TC:(p + 1) * TC]),
                r=[("rbuf", c % 2, p)], w=[("lnsrc", p)])
        ln_stats(C, sqb, 1e-5, double_rms=(1e-5, 1e-6))
        ln_bcast(C, 4, 5)
        rsb, trs = S(2)
        nsb, tns = S(3)
        P.op("vector", lambda e, rsb=rsb: e.tensor_copy(rsb, C.pb[4][:, 0:TC]),
             r=[("ps", 4)], w=[trs])
        P.op("vector", lambda e, nsb=nsb: e.tensor_copy(nsb, C.pb[5][:, 0:TC]),
             r=[("ps", 5)], w=[tns])
        for p in range(NP):
            a, ta = S(p % 2)
            P.op("gpsimd", lambda e, p=p, rb=rb, a=a, rsb=rsb: e.tensor_mul(
                a, rb[:, p * TC:(p + 1) * TC], rsb),
                r=[("rbuf", c % 2, p), trs], w=[ta])
            P.op("gpsimd", lambda e, a=a, nsb=nsb: e.tensor_add(a, a, nsb),
                 r=[ta, tns], w=[ta])
            P.op("vector", lambda e, p=p, a=a: e.tensor_scalar(
                zst[:, p * TC:(p + 1) * TC], a, lnw_col[:, p:p + 1], None, ALU.mult),
                r=[ta, "lnwcol"], w=["zst"])
        P.dma("sync", lambda e, c=c: e.dma_start(
            out=Zv[:, :, c * TC:(c + 1) * TC],
            in_=zst.rearrange("p (k t) -> p k t", k=NP)),
            r=["zst"], w=[("Z", c)])
    P.barrier()

    HX = io["HXscratch"]
    HXv = HX.rearrange("(k p) t -> p k t", p=128)
    cv.reset()
    hxc = [cv.f32(NP * TC) for _ in range(2)]
    wg = cv.bf(8192)
    wc = cv.bf(8192)
    zck = [cv.bf(4096) for _ in range(2)]
    abuf = [cv.f32(TC) for _ in range(2)]
    xbuf = [cv.f32(TC) for _ in range(2)]
    scr1 = [cv.f32(TC) for _ in range(2)]
    wload(wg, "l1a_wgT", "wg")
    wload(wc, "l1a_wcT", "wc")
    for c in range(NTC):
        zc_ = zck[c % 2]
        P.dma("sync", lambda e, c=c, zc_=zc_: e.dma_start(
            out=zc_.rearrange("p (k t) -> p k t", k=NP),
            in_=Zv[:, :, c * TC:(c + 1) * TC]),
            r=[("Z", c)], w=[("zck", c % 2)])
        for p in range(NP):
            bg, bh = p % 2, 2 + p % 2
            for k in range(NP):
                P.op("tensor", lambda e, k=k, p=p, bg=bg, zc_=zc_: e.matmul(
                    C.pb[bg][:, 0:TC],
                    wg[:, k * 1024 + p * 128: k * 1024 + p * 128 + 128],
                    zc_[:, k * TC:(k + 1) * TC],
                    start=(k == 0), stop=(k == NP - 1), skip_group_check=True),
                    r=[("zck", c % 2), "wg"], w=[("ps", bg)] if k in (0, NP - 1) else [])
            for k in range(NP):
                P.op("tensor", lambda e, k=k, p=p, bh=bh, zc_=zc_: e.matmul(
                    C.pb[bh][:, 0:TC],
                    wc[:, k * 1024 + p * 128: k * 1024 + p * 128 + 128],
                    zc_[:, k * TC:(k + 1) * TC],
                    start=(k == 0), stop=(k == NP - 1), skip_group_check=True),
                    r=[("zck", c % 2), "wc"], w=[("ps", bh)] if k in (0, NP - 1) else [])
            ab, tab = abuf[p % 2], ("ab", p % 2)
            xb_, txb = xbuf[p % 2], ("xb", p % 2)
            sc_, tsc = scr1[p % 2], ("scr1", p % 2)
            P.op("scalar", lambda e, ab=ab, bg=bg: e.activation(
                ab, C.pb[bg][:, 0:TC], AF.Sigmoid, scale=-1.0),
                r=[("ps", bg)], w=[tab])
            P.op("scalar", lambda e, sc_=sc_, bg=bg: e.activation(
                sc_, C.pb[bg][:, 0:TC], AF.Sigmoid),
                r=[("ps", bg)], w=[tsc])
            P.op("vector", lambda e, xb_=xb_, sc_=sc_, bh=bh: e.tensor_mul(
                xb_, sc_, C.pb[bh][:, 0:TC]),
                r=[tsc, ("ps", bh)], w=[txb])
            hxt = hxc[c % 2][:, p * TC:(p + 1) * TC]
            if c % CPR == 0:
                P.op("vector", lambda e, hxt=hxt, ab=ab, xb_=xb_: e.tensor_tensor_scan(
                    hxt, ab, xb_, 0.0, ALU.mult, ALU.add),
                    r=[tab, txb], w=[("hxc", c % 2, p)])
            else:
                carry = hxc[(c - 1) % 2][:, p * TC + TC - 1: p * TC + TC]
                P.op("vector", lambda e, hxt=hxt, ab=ab, xb_=xb_, carry=carry: e.tensor_tensor_scan(
                    hxt, ab, xb_, carry, ALU.mult, ALU.add),
                    r=[tab, txb, ("hxc", (c - 1) % 2, p)], w=[("hxc", c % 2, p)])
        P.dma("sync", lambda e, c=c: e.dma_start(
            out=HXv[:, :, c * TC:(c + 1) * TC],
            in_=hxc[c % 2].rearrange("p (k t) -> p k t", k=NP)),
            r=[("hxc", c % 2, p) for p in range(NP)], w=[("HX", c)])
    P.barrier()

    cv.reset()
    hxk = [cv.f32(NP * TC) for _ in range(2)]
    w1 = cv.bf(8192)
    w3 = cv.bf(8192)
    w2 = cv.bf(8192)
    nhb = [cv.bf(TC) for _ in range(NP)]
    hhb = [cv.bf(TC) for _ in range(NP)]
    sqb2 = [cv.bf(TC) for _ in range(NP)]
    scr2 = [cv.f32(TC) for _ in range(4)]

    def S2(i):
        return scr2[i], ("scr2", i)
    wload(w1, "l1a_w1T", "w1")
    wload(w3, "l1a_w3T", "w3")
    wload(w2, "l1a_w2T", "w2")
    for c in range(NTC):
        t0 = c * TC
        rb = C.rbuf[c % 2]
        hx_ = hxk[c % 2]
        P.dma("scalar", lambda e, c=c, hx_=hx_: e.dma_start(
            out=hx_.rearrange("p (k t) -> p k t", k=NP),
            in_=HXv[:, :, c * TC:(c + 1) * TC]),
            r=[("HX", c)], w=[("hxk", c % 2)])
        for p in range(NP):
            P.dma("sync", lambda e, p=p, t0=t0, rb=rb: e.dma_start(
                out=rb[:, p * TC:(p + 1) * TC],
                in_=R_in[p * 128:(p + 1) * 128, t0:t0 + TC]),
                r=[("R", sidx, c)], w=[("rbuf", c % 2, p)])
        for p in range(NP):
            P.op("vector", lambda e, p=p, hx_=hx_: e.tensor_copy(
                sqb2[p], hx_[:, p * TC:(p + 1) * TC]),
                r=[("hxk", c % 2)], w=[("lnsrc", p)])
        ln_stats(C, sqb2, 1e-6, rms_only=True)
        ln_bcast(C, 4, None, rowB=None)
        rsb, trs = S2(2)
        P.op("vector", lambda e, rsb=rsb: e.tensor_copy(rsb, C.pb[4][:, 0:TC]),
             r=[("ps", 4)], w=[trs])
        for p in range(NP):
            a, ta = S2(p % 2)
            P.op("gpsimd", lambda e, p=p, a=a, rsb=rsb, hx_=hx_: e.tensor_mul(
                a, hx_[:, p * TC:(p + 1) * TC], rsb),
                r=[("hxk", c % 2), trs], w=[ta])
            P.op("vector", lambda e, p=p, a=a: e.tensor_scalar(
                nhb[p], a, ffnw_col[:, p:p + 1], None, ALU.mult),
                r=[ta, "ffnwcol"], w=[("nh", p)])
        for p in range(NP):
            b1, b3 = p % 2, 2 + p % 2
            for k in range(NP):
                P.op("tensor", lambda e, k=k, p=p, b1=b1: e.matmul(
                    C.pb[b1][:, 0:TC],
                    w1[:, k * 1024 + p * 128: k * 1024 + p * 128 + 128],
                    nhb[k], start=(k == 0), stop=(k == NP - 1), skip_group_check=True),
                    r=[("nh", k), "w1"], w=[("ps", b1)] if k in (0, NP - 1) else [])
            for k in range(NP):
                P.op("tensor", lambda e, k=k, p=p, b3=b3: e.matmul(
                    C.pb[b3][:, 0:TC],
                    w3[:, k * 1024 + p * 128: k * 1024 + p * 128 + 128],
                    nhb[k], start=(k == 0), stop=(k == NP - 1), skip_group_check=True),
                    r=[("nh", k), "w3"], w=[("ps", b3)] if k in (0, NP - 1) else [])
            a, ta = S2(p % 2)
            P.op("scalar", lambda e, a=a, b1=b1: e.activation(
                a, C.pb[b1][:, 0:TC], AF.Silu),
                r=[("ps", b1)], w=[ta])
            P.op("vector", lambda e, p=p, a=a, b3=b3: e.tensor_mul(
                hhb[p], a, C.pb[b3][:, 0:TC]),
                r=[ta, ("ps", b3)], w=[("hh", p)])
        for p in range(NP):
            b2 = p % 2
            for k in range(NP):
                P.op("tensor", lambda e, k=k, p=p, b2=b2: e.matmul(
                    C.pb[b2][:, 0:TC],
                    w2[:, k * 1024 + p * 128: k * 1024 + p * 128 + 128],
                    hhb[k], start=(k == 0), stop=(k == NP - 1), skip_group_check=True),
                    r=[("hh", k), "w2"], w=[("ps", b2)] if k in (0, NP - 1) else [])
            P.op("vector", lambda e, p=p, b2=b2, hx_=hx_: e.tensor_add(
                hx_[:, p * TC:(p + 1) * TC], hx_[:, p * TC:(p + 1) * TC],
                C.pb[b2][:, 0:TC]),
                r=[("ps", b2), ("hxk", c % 2)], w=[("hxk", c % 2)])
        for p in range(NP):
            P.op("gpsimd", lambda e, p=p, hx_=hx_: e.tensor_copy(
                sqb2[p], hx_[:, p * TC:(p + 1) * TC]),
                r=[("hxk", c % 2)], w=[("lnsrc", p)])
        ln_stats(C, sqb2, 1e-5)
        ln_bcast(C, 4, 5)
        rsb, trs = S2(2)
        nsb, tns = S2(3)
        P.op("vector", lambda e, rsb=rsb: e.tensor_copy(rsb, C.pb[4][:, 0:TC]),
             r=[("ps", 4)], w=[trs])
        P.op("vector", lambda e, nsb=nsb: e.tensor_copy(nsb, C.pb[5][:, 0:TC]),
             r=[("ps", 5)], w=[tns])
        for p in range(NP):
            a, ta = S2(p % 2)
            P.op("gpsimd", lambda e, p=p, a=a, rsb=rsb, hx_=hx_: e.tensor_mul(
                a, hx_[:, p * TC:(p + 1) * TC], rsb),
                r=[("hxk", c % 2), trs], w=[ta])
            P.op("gpsimd", lambda e, a=a, nsb=nsb: e.tensor_add(a, a, nsb),
                 r=[ta, tns], w=[ta])
            P.op("vector", lambda e, p=p, rb=rb, a=a: e.tensor_add(
                rb[:, p * TC:(p + 1) * TC], a, rb[:, p * TC:(p + 1) * TC]),
                r=[ta], w=[("rbuf", c % 2, p)])
            P.dma("sync", lambda e, p=p, t0=t0, rb=rb: e.dma_start(
                out=R_out[p * 128:(p + 1) * 128, t0:t0 + TC],
                in_=rb[:, p * TC:(p + 1) * TC]),
                r=[("rbuf", c % 2, p)], w=[("R", sidx + 1, c)])


def emit_l1f(C, R_in, R_out, sidx):
    P, io, cv = C.P, C.io, C.cv
    Z = io["Zscratch"]
    Zv = Z.rearrange("(k p) t -> p k t", p=128)
    ACC = io["ACCscratch"]
    WG = io["WGATES"]
    NE, F, FH = 8, 4096, 2048

    cv.reset()
    zst = cv.bf(4096)
    zf = cv.f32(4096)
    sqb = [cv.bf(TC) for _ in range(NP)]
    scr = [cv.f32(TC) for _ in range(4)]
    w_em = cv.f32(T)[0:8, :]
    rw_sb = cv.f32(64)
    rbrow = cv.f32(32)[0:1, 0:8]
    rb_sb = cv.f32(32)[0:128, 0:8]
    gm = [cv.f32(32)[0:128, 0:8] for _ in range(2)]
    col = [cv.f32(16)[0:128, 0:1] for _ in range(4)]

    def S(i):
        return scr[i], ("scr", i)

    P.dma("sync", lambda e: e.dma_start(
        out=rw_sb.rearrange("p (k f) -> p k f", k=8),
        in_=io["l1f_rwT"].rearrange("(k p) f -> p k f", p=128)), w=["rw"])
    P.dma("sync", lambda e: e.dma_start(out=rbrow, in_=io["l1f_rb"]), w=["rb"])
    P.op("tensor", lambda e: e.matmul(C.pb[2][:, 0:8], C.ones_row_f32, rbrow,
                                      start=True, stop=True),
         r=["rb", C.ORF], w=[("ps", 2)])
    P.op("vector", lambda e: e.tensor_copy(rb_sb, C.pb[2][:, 0:8]),
         r=[("ps", 2)], w=["rbsb"])

    for c in range(NTC):
        rb = C.rbuf[c % 2]
        for p in range(NP):
            P.dma("sync", lambda e, p=p, c=c, rb=rb: e.dma_start(
                out=rb[:, p * TC:(p + 1) * TC],
                in_=R_in[p * 128:(p + 1) * 128, c * TC:(c + 1) * TC]),
                r=[("R", sidx, c)], w=[("rbuf", c % 2, p)])
        for p in range(NP):
            P.op("vector", lambda e, p=p, rb=rb: e.tensor_copy(
                sqb[p], rb[:, p * TC:(p + 1) * TC]),
                r=[("rbuf", c % 2, p)], w=[("lnsrc", p)])
        ln_stats(C, sqb, 1e-5)
        ln_bcast(C, 4, 5)
        rsb, trs = S(2)
        nsb, tns = S(3)
        P.op("vector", lambda e, rsb=rsb: e.tensor_copy(rsb, C.pb[4][:, 0:TC]),
             r=[("ps", 4)], w=[trs])
        P.op("vector", lambda e, nsb=nsb: e.tensor_copy(nsb, C.pb[5][:, 0:TC]),
             r=[("ps", 5)], w=[tns])
        for p in range(NP):
            a, ta = S(p % 2)
            P.op("gpsimd", lambda e, p=p, rb=rb, a=a, rsb=rsb: e.tensor_mul(
                a, rb[:, p * TC:(p + 1) * TC], rsb),
                r=[("rbuf", c % 2, p), trs], w=[ta])
            P.op("vector", lambda e, p=p, a=a, nsb=nsb: e.tensor_add(
                zf[:, p * TC:(p + 1) * TC], a, nsb),
                r=[ta, tns], w=["zf"])
            P.op("gpsimd", lambda e, p=p: e.tensor_copy(
                zst[:, p * TC:(p + 1) * TC], zf[:, p * TC:(p + 1) * TC]),
                r=["zf"], w=["zst"])
        P.dma("sync", lambda e, c=c: e.dma_start(
            out=Zv[:, :, c * TC:(c + 1) * TC],
            in_=zst.rearrange("p (k t) -> p k t", k=NP)),
            r=["zst"], w=[("Z", c)])
        for tpl in range(4):
            tp = c * 4 + tpl
            for k in range(NP):
                P.op("tensor", lambda e, k=k, tpl=tpl: e.matmul(
                    C.pb[0][:, 0:8],
                    zf[:, k * TC + tpl * 128: k * TC + tpl * 128 + 128],
                    rw_sb[:, k * 8:(k + 1) * 8],
                    start=(k == 0), stop=(k == NP - 1), skip_group_check=True),
                    r=["zf", "rw"], w=[("ps", 0)] if k in (0, NP - 1) else [])
            l_sb, tl = gm[0], ("gm", 0)
            l2, tl2 = gm[1], ("gm", 1)
            nm1, m2c, dcol, ecol = col
            P.op("vector", lambda e, l_sb=l_sb: e.tensor_add(l_sb, C.pb[0][:, 0:8], rb_sb),
                 r=[("ps", 0), "rbsb"], w=[tl])
            P.op("vector", lambda e, nm1=nm1, l_sb=l_sb: e.reduce_max(
                nm1, l_sb, AX.X, negate=True), r=[tl], w=[("col", 0)])
            P.op("vector", lambda e, ecol=ecol, nm1=nm1: e.tensor_scalar_mul(
                ecol, nm1, -1.0), r=[("col", 0)], w=[("col", 3)])
            P.op("vector", lambda e, l2=l2, l_sb=l_sb, ecol=ecol: e.tensor_single_scalar(
                l2, l_sb, ecol, ALU.is_equal), r=[tl, ("col", 3)], w=[tl2])
            P.op("vector", lambda e, l2=l2, l_sb=l_sb: e.tensor_scalar(
                l2, l2, -BIG_NEG, None, ALU.mult), r=[tl2], w=[tl2])
            P.op("vector", lambda e, l2=l2, l_sb=l_sb: e.tensor_sub(l2, l_sb, l2),
                 r=[tl, tl2], w=[tl2])
            P.op("vector", lambda e, m2c=m2c, l2=l2: e.reduce_max(m2c, l2, AX.X),
                 r=[tl2], w=[("col", 1)])
            P.op("scalar", lambda e, dcol=dcol, m2c=m2c, nm1=nm1: e.activation(
                dcol, m2c, AF.Exp, bias=nm1), r=[("col", 0), ("col", 1)], w=[("col", 2)])
            P.op("vector", lambda e, dcol=dcol: e.tensor_scalar_add(dcol, dcol, 1.0),
                 r=[("col", 2)], w=[("col", 2)])
            P.op("vector", lambda e, dcol=dcol: e.reciprocal(dcol, dcol),
                 r=[("col", 2)], w=[("col", 2)])
            P.op("scalar", lambda e, l2=l2, l_sb=l_sb, nm1=nm1: e.activation(
                l2, l_sb, AF.Exp, bias=nm1), r=[tl, ("col", 0)], w=[tl2])
            P.op("vector", lambda e, l_sb=l_sb, m2c=m2c: e.tensor_single_scalar(
                l_sb, l_sb, m2c, ALU.is_ge), r=[tl, ("col", 1)], w=[tl])
            P.op("vector", lambda e, l_sb=l_sb, l2=l2: e.tensor_mul(l_sb, l_sb, l2),
                 r=[tl, tl2], w=[tl])
            P.op("vector", lambda e, l_sb=l_sb, dcol=dcol: e.tensor_single_scalar(
                l_sb, l_sb, dcol, ALU.mult), r=[tl, ("col", 2)], w=[tl])
            P.op("tensor", lambda e, l_sb=l_sb: e.transpose(
                C.pb[1][0:8, 0:128], l_sb, C.ident_f32),
                r=[tl, ("const", "ident_f32")], w=[("ps", 1)])
            P.op("vector", lambda e, tp=tp: e.tensor_copy(
                w_em[:, tp * 128:(tp + 1) * 128], C.pb[1][0:8, 0:128]),
                r=[("ps", 1)], w=["wem"])
    P.dma("sync", lambda e: e.dma_start(out=WG, in_=w_em), r=["wem"], w=["WGd"])
    P.barrier()

    cv.reset()
    gwb = cv.bf(16384)
    uwb = cv.bf(16384)
    dwb = cv.bf(16384)
    hact = cv.bf(8192)
    zcks = [cv.bf(4096) for _ in range(2)]
    wst = cv.bf(T)[0:1, :]
    yout = [cv.f32(TC) for _ in range(2)]
    wev = cv.f32(TC)
    sils = [cv.f32(TC) for _ in range(2)]
    def load_gu(e_, fh):
        r0 = e_ * 1024
        f0 = fh * FH
        for ci in range(4):
            fc = ci * 512
            P.dma("gpsimd", lambda e, r0=r0, f0=f0, fc=fc: e.dma_start(
                out=gwb.rearrange("p (k f) -> p k f", k=8)[:, :, fc:fc + 512],
                in_=io["l1f_gwT"][r0:r0 + 1024, f0 + fc:f0 + fc + 512].rearrange(
                    "(k p) f -> p k f", p=128)), w=[("gw", ci)])
            P.dma("gpsimd", lambda e, r0=r0, f0=f0, fc=fc: e.dma_start(
                out=uwb.rearrange("p (k f) -> p k f", k=8)[:, :, fc:fc + 512],
                in_=io["l1f_uwT"][r0:r0 + 1024, f0 + fc:f0 + fc + 512].rearrange(
                    "(k p) f -> p k f", p=128)), w=[("uw", ci)])

    def load_dw(e_, fh):
        d0 = e_ * 4096 + fh * FH
        P.dma("gpsimd", lambda e, d0=d0: e.dma_start(
            out=dwb.rearrange("p (k f) -> p k f", k=16),
            in_=io["l1f_dwT"][d0:d0 + FH, :].rearrange(
                "(k p) f -> p k f", p=128)), w=["dw"])

    efhs = [(e_, fh) for e_ in range(NE) for fh in range(2)]
    load_gu(0, 0)
    load_dw(0, 0)
    for idx_, (e_, fh) in enumerate(efhs):
        if fh == 0:
            P.dma("gpsimd", lambda e, e_=e_: e.dma_start(out=wst, in_=WG[e_:e_ + 1, :]),
                  r=["WGd"], w=["wst"])
        if True:
            for tc_ in range(NTC):
                t0 = tc_ * TC
                zck = zcks[tc_ % 2]
                P.dma("sync", lambda e, t0=t0, zck=zck: e.dma_start(
                    out=zck.rearrange("p (k t) -> p k t", k=NP),
                    in_=Zv[:, :, t0:t0 + TC]),
                    r=[("Z", tc_)], w=[("zck", tc_ % 2)])
                P.op("tensor", lambda e, t0=t0: e.matmul(
                    C.pb[4][:, 0:TC], C.ones_row_bf, wst[0:1, t0:t0 + TC],
                    start=True, stop=True),
                    r=["wst", C.ORB], w=[("ps", 4)])
                P.op("vector", lambda e: e.tensor_copy(wev, C.pb[4][:, 0:TC]),
                     r=[("ps", 4)], w=["wev"])
                for fp in range(16):
                    bg, bu = fp % 2, 2 + fp % 2
                    ci = fp // 4
                    for k in range(NP):
                        P.op("tensor", lambda e, k=k, fp=fp, bg=bg, zck=zck: e.matmul(
                            C.pb[bg][:, 0:TC],
                            gwb[:, k * FH + fp * 128: k * FH + fp * 128 + 128],
                            zck[:, k * TC:(k + 1) * TC],
                            start=(k == 0), stop=(k == NP - 1), skip_group_check=True),
                            r=[("zck", tc_ % 2), ("gw", ci)],
                            w=[("ps", bg)] if k in (0, NP - 1) else [])
                    for k in range(NP):
                        P.op("tensor", lambda e, k=k, fp=fp, bu=bu, zck=zck: e.matmul(
                            C.pb[bu][:, 0:TC],
                            uwb[:, k * FH + fp * 128: k * FH + fp * 128 + 128],
                            zck[:, k * TC:(k + 1) * TC],
                            start=(k == 0), stop=(k == NP - 1), skip_group_check=True),
                            r=[("zck", tc_ % 2), ("uw", ci)],
                            w=[("ps", bu)] if k in (0, NP - 1) else [])
                    sl_ = sils[fp % 2]
                    P.op("scalar", lambda e, bg=bg, sl_=sl_: e.activation(
                        sl_, C.pb[bg][:, 0:TC], AF.Silu),
                        r=[("ps", bg)], w=[("sil", fp % 2)])
                    P.op("vector", lambda e, fp=fp, bu=bu, sl_=sl_: e.tensor_mul(
                        hact[:, fp * TC:(fp + 1) * TC], sl_, C.pb[bu][:, 0:TC]),
                        r=[("sil", fp % 2), ("ps", bu)], w=[("ha", fp)])
                if tc_ == NTC - 1 and idx_ + 1 < len(efhs):
                    load_gu(*efhs[idx_ + 1])
                for op in range(NP):
                    bo = 5 if op % 2 else 6
                    for k in range(16):
                        P.op("tensor", lambda e, k=k, op=op, bo=bo: e.matmul(
                            C.pb[bo][:, 0:TC],
                            dwb[:, k * 1024 + op * 128: k * 1024 + op * 128 + 128],
                            hact[:, k * TC:(k + 1) * TC],
                            start=(k == 0), stop=(k == 15), skip_group_check=True),
                            r=[("ha", k), "dw"], w=[("ps", bo)] if k in (0, 15) else [])
                    yo = yout[op % 2]
                    P.op("vector", lambda e, yo=yo, bo=bo: e.tensor_mul(
                        yo, C.pb[bo][:, 0:TC], wev),
                        r=[("ps", bo), "wev"], w=[("yo", op % 2)])
                    first_w = e_ == 0 and fh == 0
                    P.dma("gpsimd", lambda e, yo=yo, op=op, t0=t0, first_w=first_w: e.dma_start(
                        out=ACC[op * 128:(op + 1) * 128, t0:t0 + TC], in_=yo,
                        accum_op=(ALU.bypass if first_w else ALU.add)),
                        r=[("yo", op % 2)], w=[("ACC", tc_)])
                if tc_ == NTC - 1 and idx_ + 1 < len(efhs):
                    load_dw(*efhs[idx_ + 1])
    P.barrier()

    cv.reset()
    sqb3 = [cv.bf(TC) for _ in range(NP)]
    scr3 = [cv.f32(TC) for _ in range(4)]
    for c in range(NTC):
        rb = C.rbuf[c % 2]
        ab = C.rbuf[1 - c % 2]
        for p in range(NP):
            P.dma("sync", lambda e, p=p, c=c, rb=rb: e.dma_start(
                out=rb[:, p * TC:(p + 1) * TC],
                in_=R_in[p * 128:(p + 1) * 128, c * TC:(c + 1) * TC]),
                r=[("R", sidx, c)], w=[("rbuf", c % 2, p)])
            P.dma("scalar", lambda e, p=p, c=c, ab=ab: e.dma_start(
                out=ab[:, p * TC:(p + 1) * TC],
                in_=ACC[p * 128:(p + 1) * 128, c * TC:(c + 1) * TC]),
                r=[("ACC", c)], w=[("rbuf", 1 - c % 2, p)])
        for p in range(NP):
            P.op("vector", lambda e, p=p, ab=ab: e.tensor_copy(
                sqb3[p], ab[:, p * TC:(p + 1) * TC]),
                r=[("rbuf", 1 - c % 2, p)], w=[("lnsrc", p)])
        ln_stats(C, sqb3, 1e-5)
        ln_bcast(C, 4, 5)
        rsb, tr3 = scr3[2], ("scr3", 2)
        nsb, tn3 = scr3[3], ("scr3", 3)
        P.op("vector", lambda e, rsb=rsb: e.tensor_copy(rsb, C.pb[4][:, 0:TC]),
             r=[("ps", 4)], w=[tr3])
        P.op("vector", lambda e, nsb=nsb: e.tensor_copy(nsb, C.pb[5][:, 0:TC]),
             r=[("ps", 5)], w=[tn3])
        for p in range(NP):
            a, ta = scr3[p % 2], ("scr3", p % 2)
            P.op("gpsimd", lambda e, p=p, a=a, ab=ab, rsb=rsb: e.tensor_mul(
                a, ab[:, p * TC:(p + 1) * TC], rsb),
                r=[("rbuf", 1 - c % 2, p), tr3], w=[ta])
            P.op("gpsimd", lambda e, a=a, nsb=nsb: e.tensor_add(a, a, nsb),
                 r=[ta, tn3], w=[ta])
            P.op("vector", lambda e, p=p, rb=rb, a=a: e.tensor_add(
                rb[:, p * TC:(p + 1) * TC], a, rb[:, p * TC:(p + 1) * TC]),
                r=[ta], w=[("rbuf", c % 2, p)])
            P.dma("sync", lambda e, p=p, c=c, rb=rb: e.dma_start(
                out=R_out[p * 128:(p + 1) * 128, c * TC:(c + 1) * TC],
                in_=rb[:, p * TC:(p + 1) * TC]),
                r=[("rbuf", c % 2, p)], w=[("R", sidx + 1, c)])


def emit_l0f(C, R_in, R_out, sidx):
    P, io, cv = C.P, C.io, C.cv
    XLH = io["HXscratch"]
    NW = SEQ // 512
    HIDP = 2816
    NK2 = HIDP // 128

    cv.reset()
    sqb = [cv.bf(TC) for _ in range(NP)]
    scr = [cv.f32(TC) for _ in range(4)]
    xf = cv.f32(NP * TC)
    for c in range(NTC):
        rb = C.rbuf[c % 2]
        for p in range(NP):
            P.dma("sync", lambda e, p=p, c=c, rb=rb: e.dma_start(
                out=rb[:, p * TC:(p + 1) * TC],
                in_=R_in[p * 128:(p + 1) * 128, c * TC:(c + 1) * TC]),
                r=[("R", sidx, c)], w=[("rbuf", c % 2, p)])
        for p in range(NP):
            P.op("vector", lambda e, p=p, rb=rb: e.tensor_copy(
                sqb[p], rb[:, p * TC:(p + 1) * TC]),
                r=[("rbuf", c % 2, p)], w=[("lnsrc", p)])
        ln_stats(C, sqb, 1e-5)
        ln_bcast(C, 4, 5)
        rsb, trs = scr[2], ("scr", 2)
        nsb, tns = scr[3], ("scr", 3)
        P.op("vector", lambda e, rsb=rsb: e.tensor_copy(rsb, C.pb[4][:, 0:TC]),
             r=[("ps", 4)], w=[trs])
        P.op("vector", lambda e, nsb=nsb: e.tensor_copy(nsb, C.pb[5][:, 0:TC]),
             r=[("ps", 5)], w=[tns])
        for p in range(NP):
            a, ta = scr[p % 2], ("scr", p % 2)
            P.op("gpsimd", lambda e, p=p, rb=rb, a=a, rsb=rsb: e.tensor_mul(
                a, rb[:, p * TC:(p + 1) * TC], rsb),
                r=[("rbuf", c % 2, p), trs], w=[ta])
            P.op("vector", lambda e, p=p, a=a, nsb=nsb: e.tensor_add(
                xf[:, p * TC:(p + 1) * TC], a, nsb),
                r=[ta, tns], w=["xf"])
        P.dma("sync", lambda e, c=c: e.dma_start(
            out=XLH.rearrange("(k p) t -> p k t", p=128)[:, :, c * TC:(c + 1) * TC],
            in_=xf.rearrange("p (k t) -> p k t", k=NP)),
            r=["xf"], w=[("XLH", c)])
    P.barrier()
    if L0F_TRUNC == "xl":
        return

    for l in range(2):
        cv.reset()
        wqkv = cv.bf(NP * 1536)
        wo_sb = cv.bf(4096)
        tabs = {nm: cv.f32(2048) for nm in ("Aq", "Bq", "Ak", "Bk")}
        mq = [cv.bf(1024) for _ in range(4)]
        mall = cv.bf(512)
        hm2_sb = cv.bf(32)[:, 0:2]
        hbc_sb = cv.f32(128)[0:2, :]
        kb = cv.bf(4096)
        vb = cv.bf(4096)
        qb = cv.bf(2048)
        pn = cv.bf(4096)
        pt_sb = cv.bf(4096)
        ob = cv.bf(2048)
        hb = cv.bf(4096)
        sqfs = [cv.bf(512) for _ in range(2)]
        scr = [cv.f32(TC) for _ in range(4)]
        r2as = [cv.f32(TC)[0:2, :] for _ in range(2)]
        colc = cv.f32(32)
        halo = cv.f32(32)[:, 0:8]
        qs_col = cv.f32(32)[:, 0:1]
        ks_col = cv.f32(32)[:, 0:1]

        def wload3(dst, name, tok, nk):
            P.dma("gpsimd", lambda e, dst=dst, name=name, nk=nk: e.dma_start(
                out=dst.rearrange("p (k f) -> p k f", k=nk),
                in_=io[name].rearrange("(k p) f -> p k f", p=128)), w=[tok])

        wload3(wqkv, f"l0f_wqkv{l}", "wqkv", NP)
        wload3(wo_sb, f"l0f_wo{l}", "wo", 4)
        for nm in ("Aq", "Bq", "Ak", "Bk"):
            P.dma("sync", lambda e, nm=nm: e.dma_start(out=tabs[nm][:, 0:SEQ],
                                                       in_=io["l0f_" + nm]),
                  w=[("tab", nm)])
        for qt in range(4):
            P.dma("sync", lambda e, qt=qt: e.dma_start(out=mq[qt], in_=io[f"l0f_mask{qt}"]),
                  w=[("mq", qt)])
        P.dma("sync", lambda e: e.dma_start(out=mall, in_=io["l0f_mask_all"]), w=["mall"])
        P.dma("sync", lambda e: e.dma_start(out=hm2_sb, in_=io["l0f_hm2"]), w=["hm2"])
        P.dma("sync", lambda e: e.dma_start(out=hbc_sb, in_=io["l0f_hbc"]), w=["hbc"])
        P.dma("sync", lambda e, l=l: e.dma_start(out=qs_col, in_=io[f"l0f_qs{l}"]), w=["qsc"])
        P.dma("sync", lambda e, l=l: e.dma_start(out=ks_col, in_=io[f"l0f_ks{l}"]), w=["ksc"])
        P.op("gpsimd", lambda e: e.memset(kb, 0.0), w=[("kb", 0, 0), ("kb", 0, 1),
                                                       ("kb", 0, 2), ("kb", 0, 3),
                                                       ("kb", 1, 0), ("kb", 1, 1),
                                                       ("kb", 1, 2), ("kb", 1, 3)])
        P.op("gpsimd", lambda e: e.memset(vb, 0.0), w=[("vb", 0, 0), ("vb", 0, 1),
                                                       ("vb", 0, 2), ("vb", 0, 3),
                                                       ("vb", 1, 0), ("vb", 1, 1),
                                                       ("vb", 1, 2), ("vb", 1, 3)])

        def qk_proc(which, fp, bank, slot, woff, tc):
            A = tabs["Aq" if which == "q" else "Ak"]
            B = tabs["Bq" if which == "q" else "Bk"]
            sc_col = qs_col if which == "q" else ks_col
            sct = "qsc" if which == "q" else "ksc"
            if which == "q":
                dst, dtok = qb[:, fp * 512:(fp + 1) * 512], ("qb", fp)
            else:
                dst = kb[:, (slot * 4 + fp) * 512:(slot * 4 + fp + 1) * 512]
                dtok = ("kb", slot, fp)
            par = fp % 2
            sqf_ = sqfs[par]
            r2a_ = r2as[par]
            bn2 = 6 if par == 0 else 3
            bbc = 5 if par == 0 else 4
            P.op("scalar", lambda e, bank=bank, sqf_=sqf_: e.activation(
                sqf_, C.pb[bank][:, 0:TC], AF.Square),
                r=[("ps", bank)], w=[("sqf", par)])
            P.op("tensor", lambda e, bn2=bn2, sqf_=sqf_: e.matmul(
                C.pb[bn2][0:2, 0:TC], hm2_sb, sqf_, start=True, stop=True),
                r=[("sqf", par), "hm2"], w=[("ps", bn2)])
            P.op("scalar", lambda e, bn2=bn2, r2a_=r2a_: e.activation(
                r2a_, C.pb[bn2][0:2, 0:TC], AF.Sqrt),
                r=[("ps", bn2)], w=[("r2a", par)])
            P.op("vector", lambda e, r2a_=r2a_: e.tensor_scalar_max(r2a_, r2a_, 1e-12),
                 r=[("r2a", par)], w=[("r2a", par)])
            P.op("vector", lambda e, r2a_=r2a_: e.reciprocal(r2a_, r2a_),
                 r=[("r2a", par)], w=[("r2a", par)])
            P.op("tensor", lambda e, bbc=bbc, r2a_=r2a_: e.matmul(
                C.pb[bbc][:, 0:TC], hbc_sb, r2a_, start=True, stop=True),
                r=[("r2a", par), "hbc"], w=[("ps", bbc)])
            a, ta = scr[par * 2], ("scr", par * 2)
            b, tb = scr[par * 2 + 1], ("scr", par * 2 + 1)
            P.op("vector", lambda e, bank=bank, sc_col=sc_col, a=a: e.tensor_scalar(
                a, C.pb[bank][:, 0:TC], sc_col, None, ALU.mult),
                r=[("ps", bank), sct], w=[ta])
            mask = []
            for i in range(16):
                mask += [2 * i + 1, 2 * i]
            P.op("vector", lambda e, a=a, b=b, mask=mask: e.stream_shuffle(b, a, mask),
                 r=[ta], w=[tb])
            P.op("gpsimd", lambda e, a=a, A=A, woff=woff: e.tensor_mul(
                a, a, A[:, woff:woff + 512]),
                r=[ta, ("tab", "Aq"), ("tab", "Ak")], w=[ta])
            P.op("gpsimd", lambda e, b=b, B=B, woff=woff: e.tensor_mul(
                b, b, B[:, woff:woff + 512]),
                r=[tb, ("tab", "Bq"), ("tab", "Bk")], w=[tb])
            P.op("vector", lambda e, a=a, b=b: e.tensor_add(a, a, b),
                 r=[ta, tb], w=[ta])
            P.op("vector", lambda e, a=a, dst=dst, bbc=bbc: e.tensor_mul(
                dst, a, C.pb[bbc][:, 0:TC]),
                r=[ta, ("ps", bbc)], w=[dtok])

        for r_ in range(RPC):
            for w_ in range(NW):
                tc = r_ * NW + w_
                slot = tc % 2
                pslot = 1 - slot
                woff = w_ * 512
                tok0 = tc * TC
                rb = C.rbuf[slot]
                for p in range(NP):
                    P.dma("sync", lambda e, p=p, tok0=tok0, rb=rb: e.dma_start(
                        out=rb[:, p * TC:(p + 1) * TC],
                        in_=XLH[p * 128:(p + 1) * 128, tok0:tok0 + TC]),
                        r=[("XLH", tc)], w=[("rbuf", slot, p)])
                if w_ == 0:
                    P.op("vector", lambda e: e.memset(halo, 0.0), w=["halo"])
                for p in range(NP):
                    if p < 4:
                        P.op("gpsimd", lambda e, p=p, rb=rb: e.tensor_copy(
                            hb[:, p * 512:(p + 1) * 512], rb[:, p * TC:(p + 1) * TC]),
                            r=[("rbuf", slot, p)], w=[("hb", p)])
                    else:
                        P.op("vector", lambda e, p=p, rb=rb: e.tensor_copy(
                            hb[:, p * 512 + 1:(p + 1) * 512],
                            rb[:, p * TC:(p + 1) * TC - 1]),
                            r=[("rbuf", slot, p)], w=[("hb", p)])
                        P.op("vector", lambda e, p=p: e.tensor_copy(
                            hb[:, p * 512:p * 512 + 1], halo[:, p:p + 1]),
                            r=["halo"], w=[("hb", p)])
                for p in range(NP):
                    P.op("vector", lambda e, p=p, rb=rb: e.tensor_copy(
                        halo[:, p:p + 1], rb[:, (p + 1) * TC - 1:(p + 1) * TC]),
                        r=[("rbuf", slot, p)], w=["halo"])
                for which, cbase in (("q", 0), ("k", 512)):
                    for fp in range(4):
                        bank = fp % 2
                        for k in range(NP):
                            P.op("tensor", lambda e, k=k, fp=fp, bank=bank, cbase=cbase: e.matmul(
                                C.pb[bank][:, 0:TC],
                                wqkv[:, k * 1536 + cbase + fp * 128:
                                     k * 1536 + cbase + fp * 128 + 128],
                                hb[:, k * 512:(k + 1) * 512],
                                start=(k == 0), stop=(k == NP - 1),
                                skip_group_check=True),
                                r=[("hb", k), "wqkv"],
                                w=[("ps", bank)] if k in (0, NP - 1) else [])
                        qk_proc(which, fp, bank, slot, woff, tc)
                for ts in range(4):
                    bank = 2 + ts % 2
                    for k in range(NP):
                        P.op("tensor", lambda e, k=k, ts=ts, bank=bank: e.matmul(
                            C.pb[bank][:, 0:TC],
                            hb[:, k * 512 + ts * 128: k * 512 + ts * 128 + 128],
                            wqkv[:, k * 1536 + 1024: k * 1536 + 1536],
                            start=(k == 0), stop=(k == NP - 1), skip_group_check=True),
                            r=[("hb", k), "wqkv"],
                            w=[("ps", bank)] if k in (0, NP - 1) else [])
                    P.op("vector", lambda e, ts=ts, bank=bank, slot=slot: e.tensor_copy(
                        vb[:, (slot * 4 + ts) * 512:(slot * 4 + ts + 1) * 512],
                        C.pb[bank][:, 0:TC]),
                        r=[("ps", bank)], w=[("vb", slot, ts)])
                for h in range(8):
                    hp = h // 2
                    hb_ = 64 * (h % 2)
                    for qt in range(4):
                        bp = 2 * (qt % 2)
                        bo = bp + 1
                        prev_mask = mall if w_ == 0 else mq[qt][:, 0:512]
                        prev_mtok = "mall" if w_ == 0 else ("mq", qt)
                        P.op("tensor", lambda e, bp=bp, prev_mask=prev_mask: e.matmul(
                            C.pb[bp][:, 0:TC], C.ident_bf, prev_mask,
                            start=True, stop=False, skip_group_check=True),
                            r=[prev_mtok, ("const", "ident_bf")], w=[("ps", bp)])
                        P.op("tensor", lambda e, bo=bo, qt=qt: e.matmul(
                            C.pb[bo][:, 0:TC], C.ident_bf, mq[qt][:, 512:1024],
                            start=True, stop=False, skip_group_check=True),
                            r=[("mq", qt), ("const", "ident_bf")], w=[("ps", bo)])
                        for sl, bnk in ((pslot, bp), (slot, bo)):
                            P.op("tensor", lambda e, h=h, hp=hp, hb_=hb_, qt=qt, sl=sl, bnk=bnk: e.matmul(
                                C.pb[bnk][:, 0:TC],
                                qb[hb_:hb_ + 64, hp * 512 + qt * 128: hp * 512 + qt * 128 + 128],
                                kb[hb_:hb_ + 64, (sl * 4 + hp) * 512:(sl * 4 + hp + 1) * 512],
                                start=False, stop=True, skip_group_check=True),
                                r=[("qb", hp), ("kb", sl, hp)], w=[("ps", bnk)])
                        nm1 = colc[:, 0:1]
                        nm2 = colc[:, 1:2]
                        s0 = colc[:, 2:3]
                        s1 = colc[:, 3:4]
                        P.op("vector", lambda e, bp=bp, nm1=nm1: e.reduce_max(
                            nm1, C.pb[bp][:, 0:TC], AX.X, negate=True),
                            r=[("ps", bp)], w=[("col", 0)])
                        P.op("vector", lambda e, bo=bo, nm2=nm2: e.reduce_max(
                            nm2, C.pb[bo][:, 0:TC], AX.X, negate=True),
                            r=[("ps", bo)], w=[("col", 1)])
                        P.op("vector", lambda e, nm1=nm1, nm2=nm2: e.tensor_tensor(
                            nm1, nm1, nm2, ALU.min),
                            r=[("col", 0), ("col", 1)], w=[("col", 0)])
                        P.op("scalar", lambda e, bp=bp, qt=qt, nm1=nm1, s0=s0: e.activation(
                            pn[:, qt * 1024: qt * 1024 + 512], C.pb[bp][:, 0:TC],
                            AF.Exp, bias=nm1, accum_out=s0),
                            r=[("ps", bp), ("col", 0)], w=[("pn", qt), ("col", 2)])
                        P.op("scalar", lambda e, bo=bo, qt=qt, nm1=nm1, s1=s1: e.activation(
                            pn[:, qt * 1024 + 512: qt * 1024 + 1024], C.pb[bo][:, 0:TC],
                            AF.Exp, bias=nm1, accum_out=s1),
                            r=[("ps", bo), ("col", 0)], w=[("pn", qt), ("col", 3)])
                        P.op("vector", lambda e, s0=s0, s1=s1: e.tensor_add(s0, s0, s1),
                             r=[("col", 2), ("col", 3)], w=[("col", 2)])
                        P.op("vector", lambda e, s0=s0: e.reciprocal(s0, s0),
                             r=[("col", 2)], w=[("col", 2)])
                        P.op("vector", lambda e, qt=qt, s0=s0: e.tensor_scalar(
                            pn[:, qt * 1024:(qt + 1) * 1024],
                            pn[:, qt * 1024:(qt + 1) * 1024], s0, None, ALU.mult),
                            r=[("pn", qt), ("col", 2)], w=[("pn", qt)])
                    for j in range(8):
                        for qt in range(4):
                            P.op("tensor", lambda e, j=j, qt=qt: e.transpose(
                                C.pt[:, qt * 128:(qt + 1) * 128],
                                pn[:, qt * 1024 + j * 128: qt * 1024 + j * 128 + 128],
                                C.ident_bf),
                                r=[("pn", qt), ("const", "ident_bf")],
                                w=[("pt",)] if qt in (0, 3) else [])
                        P.op("vector", lambda e, j=j: e.tensor_copy(
                            pt_sb[:, j * 512:(j + 1) * 512], C.pt[:, 0:512]),
                            r=[("pt",)], w=[("ptsb", j)])
                    bo2 = 4 + hp % 2
                    for j in range(8):
                        sl = pslot if j < 4 else slot
                        ts = j % 4
                        P.op("tensor", lambda e, j=j, sl=sl, ts=ts, h=h, hb_=hb_, bo2=bo2: e.matmul(
                            C.pb[bo2][hb_:hb_ + 64, 0:TC],
                            vb[:, (sl * 4 + ts) * 512 + 64 * h:
                               (sl * 4 + ts) * 512 + 64 * h + 64],
                            pt_sb[:, j * 512:(j + 1) * 512],
                            start=(j == 0), stop=(j == 7), skip_group_check=True),
                            r=[("vb", sl, ts), ("ptsb", j)],
                            w=[("ps", bo2)] if j in (0, 7) else [])
                    if h % 2 == 1:
                        P.op("vector", lambda e, hp=hp, bo2=bo2: e.tensor_copy(
                            ob[:, hp * 512:(hp + 1) * 512], C.pb[bo2][:, 0:TC]),
                            r=[("ps", bo2)], w=[("ob", hp)])
                for fp in range(NP):
                    bank = fp % 2
                    for k in range(4):
                        P.op("tensor", lambda e, k=k, fp=fp, bank=bank: e.matmul(
                            C.pb[bank][:, 0:TC],
                            wo_sb[:, k * 1024 + fp * 128: k * 1024 + fp * 128 + 128],
                            ob[:, k * 512:(k + 1) * 512],
                            start=(k == 0), stop=(k == 3), skip_group_check=True),
                            r=[("ob", k), "wo"],
                            w=[("ps", bank)] if k in (0, 3) else [])
                    P.op("vector", lambda e, fp=fp, bank=bank, rb=rb: e.tensor_add(
                        rb[:, fp * TC:(fp + 1) * TC], rb[:, fp * TC:(fp + 1) * TC],
                        C.pb[bank][:, 0:TC]),
                        r=[("ps", bank)], w=[("rbuf", slot, fp)])
                    P.dma("sync", lambda e, fp=fp, tok0=tok0, rb=rb: e.dma_start(
                        out=XLH[fp * 128:(fp + 1) * 128, tok0:tok0 + TC],
                        in_=rb[:, fp * TC:(fp + 1) * TC]),
                        r=[("rbuf", slot, fp)], w=[("XLH", tc)])
        if DEBUG_TAPS and L0F_TRUNC == f"att{l}":
            P.barrier()
            for nm, buf in (("dbg_hb", hb), ("dbg_qb", qb), ("dbg_kb", kb),
                            ("dbg_vb", vb), ("dbg_pn", pn), ("dbg_ob", ob),
                            ("dbg_pt", pt_sb)):
                P.dma("gpsimd", lambda e, nm=nm, buf=buf: e.dma_start(
                    out=io[nm], in_=buf), w=[("dbg", nm)])
        P.barrier()
        if L0F_TRUNC == f"att{l}":
            return

        cv.reset()
        w1a_sb = cv.bf(NP * HIDP)
        w1g_sb = cv.bf(NP * HIDP)
        w2c = [cv.bf(HIDP) for _ in range(2)]
        hn = cv.bf(4096)
        hh = cv.bf(NK2 * 512)
        sqb2 = [cv.bf(TC) for _ in range(NP)]
        hfc = cv.f32(NP * TC)
        scr2 = [cv.f32(TC) for _ in range(4)]
        gel = scr2[3]
        lng = cv.f32(16)[:, 0:8]
        lnb = cv.f32(16)[:, 0:8]
        b1a = cv.f32(NK2)[:, 0:NK2]
        b1g = cv.f32(NK2)[:, 0:NK2]
        b2c = cv.f32(16)[:, 0:8]
        halo2 = cv.f32(32)[:, 0:8]
        wload3(w1a_sb, f"l0f_w1a{l}", "w1a", NP)
        wload3(w1g_sb, f"l0f_w1g{l}", "w1g", NP)
        for nm, dst in (("lng", lng), ("lnb", lnb), ("b1a", b1a), ("b1g", b1g),
                        ("b2", b2c)):
            P.dma("sync", lambda e, nm=nm, dst=dst, l=l: e.dma_start(
                out=dst, in_=io[f"l0f_{nm}{l}"]), w=[nm])
        for tc in range(NTC):
            t0 = tc * TC
            rb = C.rbuf[0]
            rb2 = C.rbuf[1]
            for p in range(NP):
                P.dma("sync", lambda e, p=p, t0=t0, rb=rb: e.dma_start(
                    out=rb[:, p * TC:(p + 1) * TC],
                    in_=XLH[p * 128:(p + 1) * 128, t0:t0 + TC]),
                    r=[("XLH", tc)], w=[("rbuf", 0, p)])
                if l == 1:
                    P.dma("scalar", lambda e, p=p, t0=t0, rb2=rb2: e.dma_start(
                        out=rb2[:, p * TC:(p + 1) * TC],
                        in_=R_in[p * 128:(p + 1) * 128, t0:t0 + TC]),
                        r=[("R", sidx, tc)], w=[("rbuf", 1, p)])
            if tc % CPR == 0:
                P.op("vector", lambda e: e.memset(halo2, 0.0), w=["halo2"])
            for p in range(NP):
                if p < 4:
                    P.op("gpsimd", lambda e, p=p, rb=rb: e.tensor_copy(
                        hfc[:, p * TC:(p + 1) * TC], rb[:, p * TC:(p + 1) * TC]),
                        r=[("rbuf", 0, p)], w=[("hfc", p)])
                else:
                    P.op("vector", lambda e, p=p, rb=rb: e.tensor_copy(
                        hfc[:, p * TC + 1:(p + 1) * TC],
                        rb[:, p * TC:(p + 1) * TC - 1]),
                        r=[("rbuf", 0, p)], w=[("hfc", p)])
                    P.op("vector", lambda e, p=p: e.tensor_copy(
                        hfc[:, p * TC:p * TC + 1], halo2[:, p:p + 1]),
                        r=["halo2"], w=[("hfc", p)])
            for p in range(NP):
                P.op("vector", lambda e, p=p, rb=rb: e.tensor_copy(
                    halo2[:, p:p + 1], rb[:, (p + 1) * TC - 1:(p + 1) * TC]),
                    r=[("rbuf", 0, p)], w=["halo2"])
            for p in range(NP):
                P.op("gpsimd", lambda e, p=p: e.tensor_copy(
                    sqb2[p], hfc[:, p * TC:(p + 1) * TC]),
                    r=[("hfc", p)], w=[("lnsrc", p)])
            ln_stats(C, sqb2, 1e-5)
            ln_bcast(C, 4, 5)
            rsb, trs = scr2[2], ("scr2", 2)
            nsb, tns = scr2[3], ("scr2", 3)
            P.op("vector", lambda e, rsb=rsb: e.tensor_copy(rsb, C.pb[4][:, 0:TC]),
                 r=[("ps", 4)], w=[trs])
            P.op("vector", lambda e, nsb=nsb: e.tensor_copy(nsb, C.pb[5][:, 0:TC]),
                 r=[("ps", 5)], w=[tns])
            for p in range(NP):
                a, ta = scr2[p % 2], ("scr2", p % 2)
                P.op("gpsimd", lambda e, p=p, a=a, rsb=rsb: e.tensor_mul(
                    a, hfc[:, p * TC:(p + 1) * TC], rsb),
                    r=[("hfc", p), trs], w=[ta])
                P.op("gpsimd", lambda e, a=a, nsb=nsb: e.tensor_add(a, a, nsb),
                     r=[ta, tns], w=[ta])
                P.op("vector", lambda e, p=p, a=a: e.tensor_scalar(
                    hn[:, p * TC:(p + 1) * TC], a, lng[:, p:p + 1], lnb[:, p:p + 1],
                    ALU.mult, ALU.add),
                    r=[ta, "lng", "lnb"], w=[("hn", p)])
            for fp in range(NK2):
                ba, bg = fp % 2, 2 + fp % 2
                for k in range(NP):
                    P.op("tensor", lambda e, k=k, fp=fp, ba=ba: e.matmul(
                        C.pb[ba][:, 0:TC],
                        w1a_sb[:, k * HIDP + fp * 128: k * HIDP + fp * 128 + 128],
                        hn[:, k * TC:(k + 1) * TC],
                        start=(k == 0), stop=(k == NP - 1), skip_group_check=True),
                        r=[("hn", k), "w1a"], w=[("ps", ba)] if k in (0, NP - 1) else [])
                for k in range(NP):
                    P.op("tensor", lambda e, k=k, fp=fp, bg=bg: e.matmul(
                        C.pb[bg][:, 0:TC],
                        w1g_sb[:, k * HIDP + fp * 128: k * HIDP + fp * 128 + 128],
                        hn[:, k * TC:(k + 1) * TC],
                        start=(k == 0), stop=(k == NP - 1), skip_group_check=True),
                        r=[("hn", k), "w1g"], w=[("ps", bg)] if k in (0, NP - 1) else [])
                P.op("scalar", lambda e, fp=fp, bg=bg: e.activation(
                    gel, C.pb[bg][:, 0:TC], AF.Gelu, bias=b1g[:, fp:fp + 1]),
                    r=[("ps", bg), "b1g"], w=[("scr2", 3)])
                P.op("vector", lambda e, fp=fp, ba=ba: e.scalar_tensor_tensor(
                    hh[:, fp * 512:(fp + 1) * 512], C.pb[ba][:, 0:TC],
                    b1a[:, fp:fp + 1], gel, ALU.add, ALU.mult),
                    r=[("ps", ba), "b1a", ("scr2", 3)], w=[("hh", fp)])
            for op in range(NP):
                b2k = 4 + op % 2
                wc_ = w2c[op % 2]
                P.dma("gpsimd", lambda e, op=op, wc_=wc_, l=l: e.dma_start(
                    out=wc_.rearrange("p (k f) -> p k f", k=NK2),
                    in_=io[f"l0f_w2{l}"][:, op * 128:(op + 1) * 128].rearrange(
                        "(k p) f -> p k f", p=128)),
                    w=[("w2c", op % 2)])
                for k in range(NK2):
                    P.op("tensor", lambda e, k=k, op=op, b2k=b2k, wc_=wc_: e.matmul(
                        C.pb[b2k][:, 0:TC],
                        wc_[:, k * 128:(k + 1) * 128],
                        hh[:, k * 512:(k + 1) * 512],
                        start=(k == 0), stop=(k == NK2 - 1), skip_group_check=True),
                        r=[("hh", k), ("w2c", op % 2)],
                        w=[("ps", b2k)] if k in (0, NK2 - 1) else [])
                P.op("vector", lambda e, op=op, b2k=b2k, rb=rb: e.scalar_tensor_tensor(
                    rb[:, op * TC:(op + 1) * TC], C.pb[b2k][:, 0:TC],
                    b2c[:, op:op + 1], rb[:, op * TC:(op + 1) * TC],
                    ALU.add, ALU.add),
                    r=[("ps", b2k), "b2", ("rbuf", 0, op)], w=[("rbuf", 0, op)])
                if l == 0:
                    P.dma("sync", lambda e, op=op, t0=t0, rb=rb: e.dma_start(
                        out=XLH[op * 128:(op + 1) * 128, t0:t0 + TC],
                        in_=rb[:, op * TC:(op + 1) * TC]),
                        r=[("rbuf", 0, op)], w=[("XLH", tc)])
            if l == 1:
                for p in range(NP):
                    P.op("gpsimd", lambda e, p=p, rb=rb: e.tensor_copy(
                        sqb2[p], rb[:, p * TC:(p + 1) * TC]),
                        r=[("rbuf", 0, p)], w=[("lnsrc", p)])
                ln_stats(C, sqb2, 1e-5)
                ln_bcast(C, 4, 5)
                P.op("vector", lambda e, rsb=rsb: e.tensor_copy(rsb, C.pb[4][:, 0:TC]),
                     r=[("ps", 4)], w=[trs])
                P.op("vector", lambda e, nsb=nsb: e.tensor_copy(nsb, C.pb[5][:, 0:TC]),
                     r=[("ps", 5)], w=[tns])
                for p in range(NP):
                    a, ta = scr2[p % 2], ("scr2", p % 2)
                    P.op("gpsimd", lambda e, p=p, a=a, rsb=rsb, rb=rb: e.tensor_mul(
                        a, rb[:, p * TC:(p + 1) * TC], rsb),
                        r=[("rbuf", 0, p), trs], w=[ta])
                    P.op("gpsimd", lambda e, a=a, nsb=nsb: e.tensor_add(a, a, nsb),
                         r=[ta, tns], w=[ta])
                    P.op("vector", lambda e, p=p, a=a, rb2=rb2: e.tensor_add(
                        rb2[:, p * TC:(p + 1) * TC], a, rb2[:, p * TC:(p + 1) * TC]),
                        r=[ta], w=[("rbuf", 1, p)])
                    P.dma("sync", lambda e, p=p, t0=t0, rb2=rb2: e.dma_start(
                        out=R_out[p * 128:(p + 1) * 128, t0:t0 + TC],
                        in_=rb2[:, p * TC:(p + 1) * TC]),
                        r=[("rbuf", 1, p)], w=[("R", sidx + 1, tc)])
        P.barrier()
        if L0F_TRUNC == f"ffn{l}":
            return


SLOT_EMITTERS = {"l0a": emit_l0a, "l1a": emit_l1a, "l1f": emit_l1f,
                 "l0f": emit_l0f}


def build_program(g, slots):
    nc = bass.Bass("TRN2", target_bir_lowering=False)
    stack = ExitStack()
    io = {}
    io["x"] = nc.dram_tensor("x", [D, T], F32, kind="ExternalInput").ap()
    io["out"] = nc.dram_tensor("out", [D, T], F32, kind="ExternalOutput").ap()
    if DEBUG_TAPS:
        for nm, shp in (("dbg_hb", 4096), ("dbg_qb", 2048), ("dbg_kb", 4096),
                        ("dbg_vb", 4096), ("dbg_pn", 4096), ("dbg_ob", 2048),
                        ("dbg_pt", 4096)):
            io[nm] = nc.dram_tensor(nm, [128, shp], BF16, kind="ExternalOutput").ap()
        io["dbg_z"] = nc.dram_tensor("dbg_z", [D, T], BF16, kind="ExternalOutput").ap()
        io["dbg_kv"] = nc.dram_tensor("dbg_kv", [256, 512], BF16, kind="ExternalOutput").ap()
        io["dbg_q"] = nc.dram_tensor("dbg_q", [D, TC], BF16, kind="ExternalOutput").ap()
        io["dbg_o"] = nc.dram_tensor("dbg_o", [D, TC], BF16, kind="ExternalOutput").ap()
        io["dbg_s"] = nc.dram_tensor("dbg_s", [D, TC], F32, kind="ExternalOutput").ap()
    names = list(CONST_INPUTS)
    for s in slots:
        names += list(SLOT_INPUTS[s])
    for n in names:
        a = np.asarray(g[n])
        dt = BF16 if a.dtype == BF else F32
        io[n] = nc.dram_tensor(n, list(a.shape), dt, kind="ExternalInput").ap()
    R = [io["x"]]
    for i in range(len(slots) - 1):
        R.append(nc.dram_tensor(f"Rtmp{i}", [D, T], F32, kind="Internal").ap())
    R.append(io["out"])
    io["Zscratch"] = nc.dram_tensor("Zscratch", [D, T], BF16, kind="Internal").ap()
    io["HXscratch"] = nc.dram_tensor("HXscratch", [D, T], F32, kind="Internal").ap()
    io["ACCscratch"] = nc.dram_tensor("ACCscratch", [D, T], F32, kind="Internal").ap()
    io["WGATES"] = nc.dram_tensor("WGATES", [8, T], F32, kind="Internal").ap()

    P = Prog(nc, stack)
    C = setup_ctx(P, nc, stack, io)
    for i, s in enumerate(slots):
        SLOT_EMITTERS[s](C, R[i], R[i + 1], i)
        P.barrier()
    with nc.Block() as block:
        P.finalize_and_emit(block)
    stack.close()
    return nc, names


def run_slots(g, shards, slot_groups):
    cur = shards
    for grp in slot_groups:
        nc, names = build_program(g, grp)
        in_maps = []
        for c in range(NCORES):
            m = {n: np.asarray(g[n]) for n in names}
            m["x"] = cur[c]
            in_maps.append(m)
        res = run_bass_kernel_spmd(nc, in_maps, core_ids=list(range(NCORES)))
        if DEBUG_TAPS:
            globals()["LAST_RESULTS"] = res.results
        cur = [np.asarray(res.results[c]["out"]) for c in range(NCORES)]
    return cur


def assemble(cur):
    out = np.empty((NCORES * RPC, SEQ, D), np.float32)
    for c in range(NCORES):
        out[RPC * c:RPC * c + RPC] = np.asarray(cur[c], np.float32).reshape(
            D, RPC, SEQ).transpose(1, 2, 0)
    return out


def kernel(**inputs):
    g, shards = prep_inputs(inputs)
    cur = run_slots(g, shards, [["l0a", "l0f", "l1a", "l1f"]])
    return assemble(cur)


# version 04
# speedup vs baseline: 1.1264x; 1.1044x over previous
import numpy as np
import ml_dtypes
import concourse.bass as bass
import concourse.mybir as mybir
from concourse.bass_utils import run_bass_kernel_spmd
from contextlib import ExitStack

F32 = mybir.dt.float32
BF16 = mybir.dt.bfloat16
AF = mybir.ActivationFunctionType
ALU = mybir.AluOpType
AX = mybir.AxisListType
BF = ml_dtypes.bfloat16

NCORES = 8
D = 1024
NP = 8
SEQ = 2048
RPC = 2
T = RPC * SEQ
TC = 512
NTC = T // TC
CPR = SEQ // TC
BIG_NEG = -1e30
DEBUG_TAPS = False
SAME_ENGINE = True
L0F_TRUNC = None
ENGINES = ("sync", "tensor", "vector", "scalar", "gpsimd")
DMA_Q = {"sync": "qS", "scalar": "qC", "gpsimd": "qG"}


NDMASEM = 72
NSW = 20


class Prog:
    def __init__(self, nc, stack):
        self.nc = nc
        self.ops = {e: [] for e in ENGINES}
        self.hdl = {}
        for name in ("eT", "eV", "eS", "eG", "eY", "PH"):
            self.hdl[name] = stack.enter_context(nc.semaphore(name))
        for i in range(NDMASEM):
            self.hdl[f"d{i}"] = stack.enter_context(nc.semaphore(f"d{i}"))
        self.esem = {"tensor": "eT", "vector": "eV", "scalar": "eS",
                     "gpsimd": "eG", "sync": "eY"}
        self.last_w = {}
        self.readers = {}
        self.hw_free = [f"d{i}" for i in range(NDMASEM - NSW)]
        self.sw_free = [f"d{i}" for i in range(NDMASEM - NSW, NDMASEM)]
        self.dma_map = {}
        self.dma_cnt = {f"d{i}": 0 for i in range(NDMASEM)}
        self.dma_last = {}
        self.dma_sem_last = {}
        self.sw_used = set()
        self.ph = 0

    def op(self, eng, build, r=(), w=(), deps=(), dma=False):
        idx = len(self.ops[eng])
        dep = set(deps)
        for t in r:
            lw = self.last_w.get(t)
            if lw is not None:
                dep.add(lw)
            self.readers.setdefault(t, []).append((eng, idx))
        for t in w:
            lw = self.last_w.get(t)
            if lw is not None:
                dep.add(lw)
            for rd in self.readers.get(t, ()):
                dep.add(rd)
            self.last_w[t] = (eng, idx)
            self.readers[t] = []
        dep.discard((eng, idx))
        kept = set()
        for e2, i2 in dep:
            if self.ops[e2][i2]["dma"]:
                kept.add((e2, i2))
            elif e2 != eng or (SAME_ENGINE and eng in ("vector", "scalar", "gpsimd")):
                kept.add((e2, i2))
        byeng = {}
        dmadep = set()
        for e2, i2 in kept:
            if self.ops[e2][i2]["dma"]:
                dmadep.add((e2, i2))
            else:
                byeng[e2] = max(byeng.get(e2, -1), i2)
        self.ops[eng].append({"b": build, "dep": byeng, "ddep": dmadep, "dma": dma})
        return (eng, idx)

    def dma(self, queue, build, r=(), w=(), deps=()):
        assert w, "dma op needs a primary write token"
        key = (queue == "gpsimd", w[0])
        deps = list(deps)
        sem = self.dma_map.get(key)
        if sem is None:
            pool = self.sw_free if queue == "gpsimd" else self.hw_free
            assert pool, "out of dma semaphores (add a barrier)"
            sem = pool.pop()
            self.dma_map[key] = sem
            prev = self.dma_sem_last.get(sem)
            if prev is not None:
                deps.append(prev)
        if queue == "gpsimd":
            self.sw_used.add(sem)
        eng, idx = self.op(queue, build, r=r, w=w, deps=deps, dma=True)
        self.dma_cnt[sem] += 16
        o = self.ops[eng][idx]
        o["dsem"] = sem
        o["dcnt"] = self.dma_cnt[sem]
        self.dma_last[key] = (eng, idx)
        self.dma_sem_last[sem] = (eng, idx)
        return (eng, idx)

    def last_real(self, eng):
        ops = self.ops[eng]
        for i in range(len(ops) - 1, -1, -1):
            if ops[i]["b"] is not None:
                return i
        return -1

    def barrier(self):
        snap = {e: self.last_real(e) for e in ENGINES}
        dsnap = set(self.dma_last.values())
        for e in ENGINES:
            deps = {(e2, i2) for e2, i2 in snap.items() if e2 != e and i2 >= 0}
            deps |= set(dsnap)
            self.op(e, None, deps=deps)
        self.dma_map.clear()
        self.dma_last.clear()
        self.hw_free = [f"d{i}" for i in range(NDMASEM - NSW)]
        self.sw_free = [f"d{i}" for i in range(NDMASEM - NSW, NDMASEM)]
        self.last_w.clear()
        self.readers.clear()

    def finalize_and_emit(self, block):
        needed = {e: set() for e in ENGINES}
        for e in ENGINES:
            for o in self.ops[e]:
                for e2, i2 in o["dep"].items():
                    assert self.ops[e2][i2]["b"] is not None
                    assert not self.ops[e2][i2]["dma"]
                    needed[e2].add(i2)
        cum = {}
        for e in ENGINES:
            cn, c = [], 0
            for i, o in enumerate(self.ops[e]):
                if (not o["dma"]) and i in needed[e]:
                    c += 1
                cn.append(c)
            cum[e] = cn
            assert c < 60000, (e, c)
        for s, v in self.dma_cnt.items():
            assert v < 900000, (s, v)

        def runner(e):
            ops, need, esem = self.ops[e], needed[e], self.hdl[self.esem[e]]
            hw = {}

            def f(eng):
                for i, o in enumerate(ops):
                    for s in o.get("hwreset", ()):
                        hw.pop(s, None)
                    waits = [(self.esem[e2], cum[e2][i2])
                             for e2, i2 in o["dep"].items()]
                    waits += [(self.ops[e2][i2]["dsem"], self.ops[e2][i2]["dcnt"])
                              for e2, i2 in o["ddep"]]
                    for s, v in sorted(waits):
                        if v > 0 and hw.get(s, 0) < v:
                            eng.wait_ge(self.hdl[s], v)
                            hw[s] = v
                    if o["b"] is not None:
                        ins = o["b"](eng)
                        if o["dma"]:
                            ins.then_inc(self.hdl[o["dsem"]], 16)
                        elif i in need:
                            ins.then_inc(esem, 1)
            return f

        block.sync(runner("sync"))
        block.tensor(runner("tensor"))
        block.vector(runner("vector"))
        block.scalar(runner("scalar"))
        block.gpsimd(runner("gpsimd"))


class Carver:
    def __init__(self, ap, size):
        self.ap, self.size, self.off = ap, size, 0

    def bf(self, nelem):
        nelem = (nelem + 31) & ~31
        assert self.off + nelem <= self.size, (self.off, nelem)
        ap = self.ap[:, self.off:self.off + nelem]
        self.off += nelem
        return ap

    def f32(self, nelem):
        return self.bf(2 * nelem).bitcast(F32)

    def reset(self):
        self.off = 0


def _cc(a):
    return np.ascontiguousarray(a)


def _col_tiles(b, np_=NP):
    return _cc(np.asarray(b, np.float32).reshape(np_, 128).T)


def prep_inputs(inputs):
    g = {}
    g["ones_col_bf"] = np.ones((128, 1), dtype=BF)
    g["ones_row_f32"] = np.ones((1, 128), np.float32)
    g["ones_row_bf"] = np.ones((1, 128), dtype=BF)
    g["ident_bf"] = np.eye(128, dtype=np.float32).astype(BF)
    g["ident_f32"] = np.eye(128, dtype=np.float32)
    g["l0a_wqT"] = _cc(np.asarray(inputs["L0_attn_Wq"], np.float32).T)
    g["l0a_wkT"] = _cc(np.asarray(inputs["L0_attn_Wk"], np.float32).T)
    g["l0a_wvT"] = _cc(np.asarray(inputs["L0_attn_Wv"], np.float32).T)
    g["l0a_woT"] = _cc(np.asarray(inputs["L0_attn_Wo"], np.float32).T)
    g["l0a_bq"] = _col_tiles(inputs["L0_attn_bq"])
    g["l0a_bo"] = _col_tiles(inputs["L0_attn_bo"])
    g["l0a_bk_row"] = _cc(np.asarray(inputs["L0_attn_bk"], np.float32)[None, :])
    g["l0a_bv_row"] = _cc(np.asarray(inputs["L0_attn_bv"], np.float32)[None, :])
    g["l1a_wgT"] = _cc(np.asarray(inputs["L1_attn_Wg"], np.float32).T)
    g["l1a_wcT"] = _cc(np.asarray(inputs["L1_attn_Wc"], np.float32).T)
    g["l1a_w1T"] = _cc(np.asarray(inputs["L1_attn_w1"], np.float32).T)
    g["l1a_w2T"] = _cc(np.asarray(inputs["L1_attn_w2"], np.float32).T)
    g["l1a_w3T"] = _cc(np.asarray(inputs["L1_attn_w3"], np.float32).T)
    g["l1a_lnw"] = _col_tiles(inputs["L1_attn_ln_w"])
    g["l1a_ffnw"] = _col_tiles(inputs["L1_attn_ffn_w"])
    DH, W0 = 64, 512
    perm = np.empty(DH, dtype=np.int64)
    perm[0::2] = np.arange(32)
    perm[1::2] = np.arange(32, 64)
    qk_perm = np.concatenate([h * DH + perm for h in range(8)])
    half = np.arange(0, DH, 2, dtype=np.float64)
    inv_freq = 1.0 / 10000.0 ** (half / DH)
    t_ = np.arange(SEQ, dtype=np.float64)
    freqs = np.concatenate([t_[:, None] * inv_freq[None, :]] * 2, axis=1)
    base = (half + 0.4 * DH) / (1.4 * DH)
    power = (t_ - SEQ // 2) / (W0 // 2)
    scale = np.concatenate([base[None, :] ** power[:, None]] * 2, axis=1)
    sign = np.ones(DH)
    sign[:32] = -1.0
    A_q = (np.cos(freqs) * scale) * 8.0
    B_q = (np.sin(freqs) * scale) * sign[None, :] * 8.0
    A_k = np.cos(freqs) / scale
    B_k = (np.sin(freqs) / scale) * sign[None, :]
    for nm, M in (("Aq", A_q), ("Bq", B_q), ("Ak", A_k), ("Bk", B_k)):
        g["l0f_" + nm] = _cc(np.tile(M[:, perm].T.astype(np.float32), (2, 1)))
    for qt in range(4):
        qi = (qt * 128 + np.arange(128))[:, None]
        kj = np.arange(512)[None, :]
        mprev = np.where(kj >= qi, 0.0, BIG_NEG)
        mown = np.where(kj <= qi, 0.0, BIG_NEG)
        g[f"l0f_mask{qt}"] = np.concatenate([mprev, mown], axis=1).astype(BF)
    g["l0f_mask_all"] = np.full((128, 512), BIG_NEG, np.float32).astype(BF)
    hm2 = np.zeros((128, 2), np.float32)
    hm2[0:64, 0] = 1.0
    hm2[64:128, 1] = 1.0
    g["l0f_hm2"] = hm2.astype(BF)
    g["l0f_hbc"] = _cc(hm2.T)
    wqkv = np.asarray(inputs["L0_ffn_wqkv"], np.float32)
    colperm = np.concatenate([qk_perm, 512 + qk_perm, np.arange(1024, 1536)])
    HIDP = 2816
    for l in range(2):
        g[f"l0f_wqkv{l}"] = _cc(wqkv[l][:, colperm])
        g[f"l0f_wo{l}"] = _cc(np.asarray(inputs["L0_ffn_wo"], np.float32)[l])
        qs = np.asarray(inputs["L0_ffn_q_scale"], np.float32)[l][perm]
        ks = np.asarray(inputs["L0_ffn_k_scale"], np.float32)[l][perm]
        g[f"l0f_qs{l}"] = _cc(np.tile(qs, 2)[:, None])
        g[f"l0f_ks{l}"] = _cc(np.tile(ks, 2)[:, None])
        g[f"l0f_lng{l}"] = _col_tiles(np.asarray(inputs["L0_ffn_ln_g"], np.float32)[l])
        g[f"l0f_lnb{l}"] = _col_tiles(np.asarray(inputs["L0_ffn_ln_b"], np.float32)[l])
        w1 = np.asarray(inputs["L0_ffn_w1"], np.float32)[l]
        b1 = np.asarray(inputs["L0_ffn_b1"], np.float32)[l]
        w1a = np.zeros((1024, HIDP), np.float32)
        w1g = np.zeros((1024, HIDP), np.float32)
        w1a[:, :2730] = w1[:, :2730]
        w1g[:, :2730] = w1[:, 2730:]
        b1a = np.zeros(HIDP, np.float32)
        b1g = np.zeros(HIDP, np.float32)
        b1a[:2730] = b1[:2730]
        b1g[:2730] = b1[2730:]
        g[f"l0f_w1a{l}"] = w1a
        g[f"l0f_w1g{l}"] = w1g
        g[f"l0f_b1a{l}"] = _col_tiles(b1a, 22)
        g[f"l0f_b1g{l}"] = _col_tiles(b1g, 22)
        w2 = np.asarray(inputs["L0_ffn_w2"], np.float32)[l]
        w2p = np.zeros((HIDP, 1024), np.float32)
        w2p[:2730] = w2
        g[f"l0f_w2{l}"] = w2p
        g[f"l0f_b2{l}"] = _col_tiles(np.asarray(inputs["L0_ffn_b2"], np.float32)[l])
    g["l1f_rwT"] = _cc(np.asarray(inputs["L1_ffn_router_w"], np.float32).T)
    g["l1f_rb"] = _cc(np.asarray(inputs["L1_ffn_router_b"], np.float32)[None, :])
    gw = np.asarray(inputs["L1_ffn_gate_w"], np.float32)
    uw = np.asarray(inputs["L1_ffn_up_w"], np.float32)
    dw = np.asarray(inputs["L1_ffn_down_w"], np.float32)
    g["l1f_gwT"] = _cc(gw.transpose(0, 2, 1).reshape(8 * 1024, 4096))
    g["l1f_uwT"] = _cc(uw.transpose(0, 2, 1).reshape(8 * 1024, 4096))
    g["l1f_dwT"] = _cc(dw.transpose(0, 2, 1).reshape(8 * 4096, 1024))
    x = np.asarray(inputs["L0_attn_x"], np.float32)
    shards = [_cc(x[RPC * c: RPC * c + RPC].transpose(2, 0, 1).reshape(D, T))
              for c in range(NCORES)]
    return g, shards


SLOT_INPUTS = {
    "l0a": ("l0a_wqT", "l0a_wkT", "l0a_wvT", "l0a_woT",
            "l0a_bq", "l0a_bo", "l0a_bk_row", "l0a_bv_row"),
    "l1a": ("l1a_wgT", "l1a_wcT", "l1a_w1T", "l1a_w2T", "l1a_w3T",
            "l1a_lnw", "l1a_ffnw"),
    "l1f": ("l1f_rwT", "l1f_rb", "l1f_gwT", "l1f_uwT", "l1f_dwT"),
    "l0f": tuple(["l0f_Aq", "l0f_Bq", "l0f_Ak", "l0f_Bk", "l0f_mask_all",
                  "l0f_hm2", "l0f_hbc"]
                 + [f"l0f_mask{qt}" for qt in range(4)]
                 + [f"l0f_{nm}{l}" for l in range(2)
                    for nm in ("wqkv", "wo", "qs", "ks", "lng", "lnb",
                               "w1a", "w1g", "b1a", "b1g", "w2", "b2")]),
}
CONST_INPUTS = ("ones_col_bf", "ones_row_f32", "ones_row_bf", "ident_bf", "ident_f32")


class Ctx:
    pass


def setup_ctx(P, nc, stack, io):
    C = Ctx()
    C.P, C.nc, C.io = P, nc, io
    sb = lambda n, s, d: stack.enter_context(nc.sbuf_tensor(n, s, d))
    ps = lambda n, s, d: stack.enter_context(nc.psum_tensor(n, s, d))
    C.pb = [ps(f"PB{i}", [128, 512], F32)[:] for i in range(7)]
    C.pt = ps("PT", [128, 1024], BF16)[:]
    ar = sb("ARENA", [128, 82944], BF16)
    C.cv = Carver(ar[:], 82944)
    C.rbuf = [sb(f"RBUF{i}", [128, NP * TC], F32)[:] for i in range(2)]
    C.rows = {n: sb("row_" + n, [1, TC], F32)[:]
              for n in ("mu", "ms", "tmp", "rstd", "nmr", "x1")}
    C.ones_col_bf = sb("c_ones_col", [128, 1], BF16)[:]
    C.ones_row_f32 = sb("c_ones_row", [1, 128], F32)[:]
    C.ones_row_bf = sb("c_ones_row_bf", [1, 128], BF16)[:]
    C.ident_bf = sb("c_ident_bf", [128, 128], BF16)[:]
    C.ident_f32 = sb("c_ident_f32", [128, 128], F32)[:]
    C.smalls = sb("SMALLS", [128, 64], F32)[:]
    C.smalls_used = 0
    for name, dst in (("ones_col_bf", C.ones_col_bf), ("ones_row_f32", C.ones_row_f32),
                      ("ones_row_bf", C.ones_row_bf),
                      ("ident_bf", C.ident_bf), ("ident_f32", C.ident_f32)):
        P.dma("sync", lambda e, dst=dst, name=name: e.dma_start(out=dst, in_=io[name]),
              w=[("const", name)])
    C.OCB = ("const", "ones_col_bf")
    C.ORF = ("const", "ones_row_f32")
    C.ORB = ("const", "ones_row_bf")
    return C


def small_cols(C, name, ncols, tok):
    o = C.smalls_used
    C.smalls_used += ncols
    assert C.smalls_used <= 64
    dst = C.smalls[:, o:o + ncols]
    C.P.dma("sync", lambda e, dst=dst, name=name: e.dma_start(out=dst, in_=C.io[name]),
            w=[tok])
    return dst


def ln_stats(C, stat_bf, eps, rms_only=False, double_rms=None):
    P, rows = C.P, C.rows
    bsum, bsq = C.pb[6], C.pb[3]
    if not rms_only:
        for p in range(NP):
            P.op("tensor", lambda e, p=p: e.matmul(
                bsum[0:1, 0:TC], C.ones_col_bf, stat_bf[p],
                start=(p == 0), stop=(p == NP - 1), skip_group_check=True),
                r=[("lnsrc", p), C.OCB], w=[("ps", 6)] if p in (0, NP - 1) else [])
    for p in range(NP):
        P.op("scalar", lambda e, p=p: e.activation(stat_bf[p], stat_bf[p], AF.Square),
             r=[("lnsrc", p)], w=[("lnsrc", p)])
        P.op("tensor", lambda e, p=p: e.matmul(
            bsq[0:1, 0:TC], C.ones_col_bf, stat_bf[p],
            start=(p == 0), stop=(p == NP - 1), skip_group_check=True),
            r=[("lnsrc", p), C.OCB],
            w=[("ps", 3)] if p in (0, NP - 1) else [])
    if not rms_only:
        P.op("vector", lambda e: e.tensor_scalar_mul(rows["mu"], bsum[0:1, 0:TC], 1.0 / D),
             r=[("ps", 6)], w=[("row", "mu")])
        P.op("vector", lambda e: e.tensor_scalar_mul(rows["ms"], bsq[0:1, 0:TC], 1.0 / D),
             r=[("ps", 3)], w=[("row", "ms")])
        P.op("vector", lambda e: e.tensor_mul(rows["tmp"], rows["mu"], rows["mu"]),
             r=[("row", "mu")], w=[("row", "tmp")])
        P.op("vector", lambda e: e.tensor_sub(rows["ms"], rows["ms"], rows["tmp"]),
             r=[("row", "tmp")], w=[("row", "ms")])
    else:
        P.op("vector", lambda e: e.tensor_scalar_mul(rows["ms"], bsq[0:1, 0:TC], 1.0 / D),
             r=[("ps", 3)], w=[("row", "ms")])
    if double_rms is not None:
        eo, ei = double_rms
        P.op("vector", lambda e: e.tensor_scalar(
            rows["tmp"], rows["ms"], 1.0 + ei, ei * eo, ALU.mult, ALU.add),
            r=[("row", "ms")], w=[("row", "tmp")])
    else:
        P.op("vector", lambda e: e.tensor_scalar_add(rows["tmp"], rows["ms"], float(eps)),
             r=[("row", "ms")], w=[("row", "tmp")])
    P.op("scalar", lambda e: e.activation(rows["tmp"], rows["tmp"], AF.Sqrt),
         r=[("row", "tmp")], w=[("row", "tmp")])
    P.op("vector", lambda e: e.reciprocal(rows["rstd"], rows["tmp"]),
         r=[("row", "tmp")], w=[("row", "rstd")])
    if not rms_only:
        P.op("vector", lambda e: e.tensor_mul(rows["nmr"], rows["mu"], rows["rstd"]),
             r=[("row", "mu"), ("row", "rstd")], w=[("row", "nmr")])
        P.op("vector", lambda e: e.tensor_scalar_mul(rows["nmr"], rows["nmr"], -1.0),
             w=[("row", "nmr")])


def ln_bcast(C, bankA, bankB=None, rowA="rstd", rowB="nmr"):
    P, rows = C.P, C.rows
    P.op("tensor", lambda e, bankA=bankA, rowA=rowA: e.matmul(
        C.pb[bankA][:, 0:TC], C.ones_row_f32, rows[rowA], start=True, stop=True),
        r=[("row", rowA), C.ORF], w=[("ps", bankA)])
    if rowB is not None:
        P.op("tensor", lambda e, bankB=bankB, rowB=rowB: e.matmul(
            C.pb[bankB][:, 0:TC], C.ones_row_f32, rows[rowB], start=True, stop=True),
            r=[("row", rowB), C.ORF], w=[("ps", bankB)])


def emit_l0a(C, R_in, R_out, sidx):
    P, io, cv = C.P, C.io, C.cv
    cv.reset()
    zb = [cv.bf(T) for _ in range(NP)]
    w0, w1 = cv.bf(8192), cv.bf(8192)
    kvsb = [cv.bf(512) for _ in range(2)]
    ktm = [cv.bf(1024) for _ in range(2)]
    vtm = [cv.bf(1024) for _ in range(2)]
    qbf = [cv.bf(TC) for _ in range(NP)]
    obf = [cv.bf(TC) for _ in range(NP)]
    sqb = [cv.bf(TC) for _ in range(NP)]
    bias_bc = cv.f32(2048)
    scr = [cv.f32(TC) for _ in range(4)]
    sS = [cv.f32(TC) for _ in range(NP)]

    def S(i):
        return scr[i], ("scr", i)

    def wload(dst, name, tok):
        nk = io[name].shape[0] // 128
        src = io[name].rearrange("(k p) f -> p k f", p=128)
        dst3 = dst.rearrange("p (k f) -> p k f", k=nk)
        P.dma("gpsimd", lambda e, dst3=dst3, src=src: e.dma_start(out=dst3, in_=src),
              w=[tok])

    wload(w0, "l0a_wkT", "wk")
    wload(w1, "l0a_wvT", "wv")
    bq_col = small_cols(C, "l0a_bq", NP, "bqcol")
    bo_col = small_cols(C, "l0a_bo", NP, "bocol")
    P.dma("sync", lambda e: e.dma_start(out=bias_bc[0:1, 0:1024], in_=io["l0a_bk_row"]),
          w=["biasbc"])
    P.dma("sync", lambda e: e.dma_start(out=bias_bc[0:1, 1024:2048], in_=io["l0a_bv_row"]),
          w=["biasbc"])
    for h in range(4):
        P.op("tensor", lambda e, h=h: e.matmul(
            C.pb[5][:, 0:512], C.ones_row_f32, bias_bc[0:1, h * 512:(h + 1) * 512],
            start=True, stop=True),
            r=["biasbc", C.ORF], w=[("ps", 5)])
        P.op("vector", lambda e, h=h: e.tensor_copy(
            bias_bc[:, h * 512:(h + 1) * 512], C.pb[5][:, 0:512]),
            r=[("ps", 5)], w=["biasbc"])

    for c in range(NTC):
        rb = C.rbuf[c % 2]
        for p in range(NP):
            P.dma("sync", lambda e, p=p, c=c, rb=rb: e.dma_start(
                out=rb[:, p * TC:(p + 1) * TC],
                in_=R_in[p * 128:(p + 1) * 128, c * TC:(c + 1) * TC]),
                r=[("R", sidx, c)], w=[("rbuf", c % 2, p)])
        for p in range(NP):
            P.op("vector", lambda e, p=p, rb=rb: e.tensor_copy(
                sqb[p], rb[:, p * TC:(p + 1) * TC]),
                r=[("rbuf", c % 2, p)], w=[("lnsrc", p)])
        ln_stats(C, sqb, 1e-5)
        ln_bcast(C, 4, 5)
        rsb, trs = S(2)
        nsb, tns = S(3)
        P.op("vector", lambda e, rsb=rsb: e.tensor_copy(rsb, C.pb[4][:, 0:TC]),
             r=[("ps", 4)], w=[trs])
        P.op("vector", lambda e, nsb=nsb: e.tensor_copy(nsb, C.pb[5][:, 0:TC]),
             r=[("ps", 5)], w=[tns])
        for p in range(NP):
            a, ta = S(p % 2)
            P.op("vector", lambda e, p=p, rb=rb, a=a, rsb=rsb: e.tensor_mul(
                a, rb[:, p * TC:(p + 1) * TC], rsb),
                r=[("rbuf", c % 2, p), trs], w=[ta])
            P.op("vector", lambda e, p=p, c=c, a=a, nsb=nsb: e.tensor_add(
                zb[p][:, c * TC:(c + 1) * TC], a, nsb),
                r=[ta, tns], w=[("zb", p)])

    scnt = 0
    for r in range(RPC):
        kvbank = 2 + r
        nts = SEQ // 128
        for ts in range(nts):
            tok0 = r * SEQ + ts * 128
            kbuf, vbuf = ktm[ts % 2], vtm[ts % 2]
            for half in range(2):
                for which in ("k", "v"):
                    wsb = w0 if which == "k" else w1
                    wtok = "wk" if which == "k" else "wv"
                    bank = (ts * 2 + half) % 2
                    fo = half * 512
                    for k in range(NP):
                        P.op("tensor", lambda e, k=k, fo=fo, wsb=wsb, bank=bank, tok0=tok0: e.matmul(
                            C.pb[bank][:, 0:512],
                            zb[k][:, tok0:tok0 + 128],
                            wsb[:, k * 1024 + fo: k * 1024 + fo + 512],
                            start=(k == 0), stop=(k == NP - 1), skip_group_check=True),
                            r=[("zb", k), wtok],
                            w=[("ps", bank)] if k in (0, NP - 1) else [])
                    if which == "v":
                        P.op("vector", lambda e, vbuf=vbuf, fo=fo, bank=bank, half=half: e.tensor_add(
                            vbuf[:, fo:fo + 512], C.pb[bank][:, 0:512],
                            bias_bc[:, 1024 + fo:1024 + fo + 512]),
                            r=[("ps", bank), "biasbc"],
                            w=[("vtm", ts % 2, half)])
                    else:
                        si = (scnt % 2) * 2
                        scnt += 1
                        a, ta = S(si)
                        b, tb = S(si + 1)
                        P.op("vector", lambda e, a=a, fo=fo, bank=bank: e.tensor_add(
                            a, C.pb[bank][:, 0:512], bias_bc[:, fo:fo + 512]),
                            r=[("ps", bank), "biasbc"], w=[ta])
                        P.op("vector", lambda e, a=a, b=b: e.tensor_scalar_min(b, a, 0.0),
                             r=[ta], w=[tb])
                        P.op("scalar", lambda e, b=b: e.activation(b, b, AF.Exp),
                             r=[tb], w=[tb])
                        P.op("vector", lambda e, a=a: e.tensor_scalar_max(a, a, 0.0),
                             r=[ta], w=[ta])
                        P.op("vector", lambda e, kbuf=kbuf, fo=fo, a=a, b=b: e.tensor_add(
                            kbuf[:, fo:fo + 512], a, b),
                            r=[ta, tb], w=[("ktm", ts % 2, half)])
            for h in range(16):
                pb_ = h % 2
                fs = 64 * (h // 2)
                first = ts == 0 and h in (0, 1)
                last = ts == nts - 1 and h in (14, 15)
                P.op("tensor", lambda e, pb_=pb_, fs=fs, h=h, kbuf=kbuf, vbuf=vbuf, kvbank=kvbank, first=first, last=last: e.matmul(
                    C.pb[kvbank][64 * pb_:64 * pb_ + 64, fs:fs + 64],
                    kbuf[:, 64 * h:64 * h + 64],
                    vbuf[:, 64 * h:64 * h + 64],
                    start=first, stop=last, skip_group_check=True),
                    r=[("ktm", ts % 2, h // 8), ("vtm", ts % 2, h // 8)],
                    w=[("ps", kvbank)] if (first or last) else [])
        P.op("vector", lambda e, r=r, kvbank=kvbank: e.tensor_copy(
            kvsb[r], C.pb[kvbank][:, 0:512]),
            r=[("ps", kvbank)], w=[("kvsb", r)])

    wload(w0, "l0a_wqT", "wk")
    wload(w1, "l0a_woT", "wv")
    for c in range(NTC):
        r = c // CPR
        t0 = c * TC
        rb = C.rbuf[c % 2]
        for p in range(NP):
            P.dma("sync", lambda e, p=p, t0=t0, rb=rb: e.dma_start(
                out=rb[:, p * TC:(p + 1) * TC],
                in_=R_in[p * 128:(p + 1) * 128, t0:t0 + TC]),
                r=[("R", sidx, c)], w=[("rbuf", c % 2, p)])
        for fp in range(NP):
            bank = fp % 2
            for k in range(NP):
                P.op("tensor", lambda e, k=k, fp=fp, bank=bank, t0=t0: e.matmul(
                    C.pb[bank][:, 0:TC],
                    w0[:, k * 1024 + fp * 128: k * 1024 + fp * 128 + 128],
                    zb[k][:, t0:t0 + TC],
                    start=(k == 0), stop=(k == NP - 1), skip_group_check=True),
                    r=[("zb", k), "wk"], w=[("ps", bank)] if k in (0, NP - 1) else [])
            si = (fp % 2) * 2
            a, ta = S(si)
            b, tb = S(si + 1)
            P.op("vector", lambda e, a=a, bank=bank, fp=fp: e.tensor_scalar(
                a, C.pb[bank][:, 0:TC], bq_col[:, fp:fp + 1], 0.0, ALU.add, ALU.min),
                r=[("ps", bank), "bqcol"], w=[ta])
            P.op("scalar", lambda e, a=a: e.activation(a, a, AF.Exp),
                 r=[ta], w=[ta])
            P.op("vector", lambda e, b=b, bank=bank, fp=fp: e.tensor_scalar(
                b, C.pb[bank][:, 0:TC], bq_col[:, fp:fp + 1], 0.0, ALU.add, ALU.max),
                r=[("ps", bank), "bqcol"], w=[tb])
            P.op("vector", lambda e, a=a, b=b, fp=fp: e.tensor_add(qbf[fp], a, b),
                 r=[ta, tb], w=[("qbf", fp)])
        for pair in range(8):
            bank = 4 + pair % 2
            for sub in range(2):
                h = 2 * pair + sub
                P.op("tensor", lambda e, h=h, bank=bank, r=r, sub=sub: e.matmul(
                    C.pb[bank][64 * (h % 2):64 * (h % 2) + 64, 0:TC],
                    kvsb[r][64 * (h % 2):64 * (h % 2) + 64,
                            64 * (h // 2):64 * (h // 2) + 64],
                    qbf[h // 2][64 * (h % 2):64 * (h % 2) + 64, :],
                    start=True, stop=True, skip_group_check=True),
                    r=[("kvsb", r), ("qbf", h // 2)], w=[("ps", bank)])
            P.op("vector", lambda e, pair=pair, bank=bank: e.tensor_copy(
                obf[pair], C.pb[bank][:, 0:TC]),
                r=[("ps", bank)], w=[("obf", pair)])
        for fp in range(NP):
            bank = fp % 2
            for k in range(NP):
                P.op("tensor", lambda e, k=k, fp=fp, bank=bank: e.matmul(
                    C.pb[bank][:, 0:TC],
                    w1[:, k * 1024 + fp * 128: k * 1024 + fp * 128 + 128],
                    obf[k], start=(k == 0), stop=(k == NP - 1), skip_group_check=True),
                    r=[("obf", k), "wv"], w=[("ps", bank)] if k in (0, NP - 1) else [])
            P.op("vector", lambda e, fp=fp, bank=bank: e.tensor_scalar_add(
                sS[fp], C.pb[bank][:, 0:TC], bo_col[:, fp:fp + 1]),
                r=[("ps", bank), "bocol"], w=[("sS", fp)])
        for p in range(NP):
            P.op("scalar", lambda e, p=p: e.copy(sqb[p], sS[p]),
                 r=[("sS", p)], w=[("lnsrc", p)])
        ln_stats(C, sqb, 1e-5)
        ln_bcast(C, 4, 5)
        rsb, trs = S(2)
        nsb, tns = S(3)
        P.op("vector", lambda e, rsb=rsb: e.tensor_copy(rsb, C.pb[4][:, 0:TC]),
             r=[("ps", 4)], w=[trs])
        P.op("vector", lambda e, nsb=nsb: e.tensor_copy(nsb, C.pb[5][:, 0:TC]),
             r=[("ps", 5)], w=[tns])
        for p in range(NP):
            a, ta = S(p % 2)
            P.op("vector", lambda e, p=p, a=a, rsb=rsb: e.tensor_mul(a, sS[p], rsb),
                 r=[("sS", p), trs], w=[ta])
            P.op("vector", lambda e, a=a, nsb=nsb: e.tensor_add(a, a, nsb),
                 r=[ta, tns], w=[ta])
            P.op("vector", lambda e, p=p, rb=rb, a=a: e.tensor_add(
                rb[:, p * TC:(p + 1) * TC], a, rb[:, p * TC:(p + 1) * TC]),
                r=[ta], w=[("rbuf", c % 2, p)])
            P.dma("sync", lambda e, p=p, t0=t0, rb=rb: e.dma_start(
                out=R_out[p * 128:(p + 1) * 128, t0:t0 + TC],
                in_=rb[:, p * TC:(p + 1) * TC]),
                r=[("rbuf", c % 2, p)], w=[("R", sidx + 1, c)])
    if DEBUG_TAPS:
        for p in range(NP):
            P.dma("gpsimd", lambda e, p=p: e.dma_start(
                out=io["dbg_z"][p * 128:(p + 1) * 128, :], in_=zb[p]),
                r=[("zb", p)])
        for r_ in range(2):
            P.dma("gpsimd", lambda e, r_=r_: e.dma_start(
                out=io["dbg_kv"][r_ * 128:(r_ + 1) * 128, :], in_=kvsb[r_]),
                r=[("kvsb", r_)])
        for p in range(NP):
            P.dma("gpsimd", lambda e, p=p: e.dma_start(
                out=io["dbg_q"][p * 128:(p + 1) * 128, :], in_=qbf[p]),
                r=[("qbf", p)])
            P.dma("gpsimd", lambda e, p=p: e.dma_start(
                out=io["dbg_o"][p * 128:(p + 1) * 128, :], in_=obf[p]),
                r=[("obf", p)])
            P.dma("sync", lambda e, p=p: e.dma_start(
                out=io["dbg_s"][p * 128:(p + 1) * 128, :], in_=sS[p]),
                r=[("sS", p)])


def emit_l1a(C, R_in, R_out, sidx):
    P, io, cv = C.P, C.io, C.cv
    Z = io["Zscratch"]
    lnw_col = small_cols(C, "l1a_lnw", NP, "lnwcol")
    ffnw_col = small_cols(C, "l1a_ffnw", NP, "ffnwcol")

    def wload(dst, name, tok):
        nk = io[name].shape[0] // 128
        src_ = io[name].rearrange("(k p) f -> p k f", p=128)
        dst3 = dst.rearrange("p (k f) -> p k f", k=nk)
        P.dma("gpsimd", lambda e, dst3=dst3, src_=src_: e.dma_start(out=dst3, in_=src_),
              w=[tok])

    def S(i):
        return scr[i], ("scr", i)

    cv.reset()
    zst = cv.bf(4096)
    sqb = [cv.bf(TC) for _ in range(NP)]
    scr = [cv.f32(TC) for _ in range(4)]
    Zv = Z.rearrange("(k p) t -> p k t", p=128)
    for c in range(NTC):
        rb = C.rbuf[c % 2]
        for p in range(NP):
            P.dma("sync", lambda e, p=p, c=c, rb=rb: e.dma_start(
                out=rb[:, p * TC:(p + 1) * TC],
                in_=R_in[p * 128:(p + 1) * 128, c * TC:(c + 1) * TC]),
                r=[("R", sidx, c)], w=[("rbuf", c % 2, p)])
        for p in range(NP):
            P.op("vector", lambda e, p=p, rb=rb: e.tensor_copy(
                sqb[p], rb[:, p * TC:(p + 1) * TC]),
                r=[("rbuf", c % 2, p)], w=[("lnsrc", p)])
        ln_stats(C, sqb, 1e-5, double_rms=(1e-5, 1e-6))
        ln_bcast(C, 4, 5)
        rsb, trs = S(2)
        nsb, tns = S(3)
        P.op("vector", lambda e, rsb=rsb: e.tensor_copy(rsb, C.pb[4][:, 0:TC]),
             r=[("ps", 4)], w=[trs])
        P.op("vector", lambda e, nsb=nsb: e.tensor_copy(nsb, C.pb[5][:, 0:TC]),
             r=[("ps", 5)], w=[tns])
        for p in range(NP):
            a, ta = S(p % 2)
            P.op("vector", lambda e, p=p, rb=rb, a=a, rsb=rsb: e.tensor_mul(
                a, rb[:, p * TC:(p + 1) * TC], rsb),
                r=[("rbuf", c % 2, p), trs], w=[ta])
            P.op("vector", lambda e, a=a, nsb=nsb: e.tensor_add(a, a, nsb),
                 r=[ta, tns], w=[ta])
            P.op("vector", lambda e, p=p, a=a: e.tensor_scalar(
                zst[:, p * TC:(p + 1) * TC], a, lnw_col[:, p:p + 1], None, ALU.mult),
                r=[ta, "lnwcol"], w=["zst"])
        P.dma("sync", lambda e, c=c: e.dma_start(
            out=Zv[:, :, c * TC:(c + 1) * TC],
            in_=zst.rearrange("p (k t) -> p k t", k=NP)),
            r=["zst"], w=[("Z", c)])
    P.barrier()

    HX = io["HXscratch"]
    HXv = HX.rearrange("(k p) t -> p k t", p=128)
    cv.reset()
    hxc = [cv.f32(NP * TC) for _ in range(2)]
    wg = cv.bf(8192)
    wc = cv.bf(8192)
    zck = [cv.bf(4096) for _ in range(2)]
    abuf = [cv.f32(TC) for _ in range(2)]
    xbuf = [cv.f32(TC) for _ in range(2)]
    scr1 = [cv.f32(TC) for _ in range(2)]
    wload(wg, "l1a_wgT", "wg")
    wload(wc, "l1a_wcT", "wc")
    for c in range(NTC):
        zc_ = zck[c % 2]
        P.dma("sync", lambda e, c=c, zc_=zc_: e.dma_start(
            out=zc_.rearrange("p (k t) -> p k t", k=NP),
            in_=Zv[:, :, c * TC:(c + 1) * TC]),
            r=[("Z", c)], w=[("zck", c % 2)])
        for p in range(NP):
            bg, bh = p % 2, 2 + p % 2
            for k in range(NP):
                P.op("tensor", lambda e, k=k, p=p, bg=bg, zc_=zc_: e.matmul(
                    C.pb[bg][:, 0:TC],
                    wg[:, k * 1024 + p * 128: k * 1024 + p * 128 + 128],
                    zc_[:, k * TC:(k + 1) * TC],
                    start=(k == 0), stop=(k == NP - 1), skip_group_check=True),
                    r=[("zck", c % 2), "wg"], w=[("ps", bg)] if k in (0, NP - 1) else [])
            for k in range(NP):
                P.op("tensor", lambda e, k=k, p=p, bh=bh, zc_=zc_: e.matmul(
                    C.pb[bh][:, 0:TC],
                    wc[:, k * 1024 + p * 128: k * 1024 + p * 128 + 128],
                    zc_[:, k * TC:(k + 1) * TC],
                    start=(k == 0), stop=(k == NP - 1), skip_group_check=True),
                    r=[("zck", c % 2), "wc"], w=[("ps", bh)] if k in (0, NP - 1) else [])
            ab, tab = abuf[p % 2], ("ab", p % 2)
            xb_, txb = xbuf[p % 2], ("xb", p % 2)
            sc_, tsc = scr1[p % 2], ("scr1", p % 2)
            P.op("scalar", lambda e, ab=ab, bg=bg: e.activation(
                ab, C.pb[bg][:, 0:TC], AF.Sigmoid, scale=-1.0),
                r=[("ps", bg)], w=[tab])
            P.op("scalar", lambda e, sc_=sc_, bg=bg: e.activation(
                sc_, C.pb[bg][:, 0:TC], AF.Sigmoid),
                r=[("ps", bg)], w=[tsc])
            P.op("vector", lambda e, xb_=xb_, sc_=sc_, bh=bh: e.tensor_mul(
                xb_, sc_, C.pb[bh][:, 0:TC]),
                r=[tsc, ("ps", bh)], w=[txb])
            hxt = hxc[c % 2][:, p * TC:(p + 1) * TC]
            if c % CPR == 0:
                P.op("vector", lambda e, hxt=hxt, ab=ab, xb_=xb_: e.tensor_tensor_scan(
                    hxt, ab, xb_, 0.0, ALU.mult, ALU.add),
                    r=[tab, txb], w=[("hxc", c % 2, p)])
            else:
                carry = hxc[(c - 1) % 2][:, p * TC + TC - 1: p * TC + TC]
                P.op("vector", lambda e, hxt=hxt, ab=ab, xb_=xb_, carry=carry: e.tensor_tensor_scan(
                    hxt, ab, xb_, carry, ALU.mult, ALU.add),
                    r=[tab, txb, ("hxc", (c - 1) % 2, p)], w=[("hxc", c % 2, p)])
        P.dma("sync", lambda e, c=c: e.dma_start(
            out=HXv[:, :, c * TC:(c + 1) * TC],
            in_=hxc[c % 2].rearrange("p (k t) -> p k t", k=NP)),
            r=[("hxc", c % 2, p) for p in range(NP)], w=[("HX", c)])
    P.barrier()

    cv.reset()
    hxk = [cv.f32(NP * TC) for _ in range(2)]
    w1 = cv.bf(8192)
    w3 = cv.bf(8192)
    w2 = cv.bf(8192)
    nhb = [cv.bf(TC) for _ in range(NP)]
    hhb = [cv.bf(TC) for _ in range(NP)]
    sqb2 = [cv.bf(TC) for _ in range(NP)]
    scr2 = [cv.f32(TC) for _ in range(4)]

    def S2(i):
        return scr2[i], ("scr2", i)
    wload(w1, "l1a_w1T", "w1")
    wload(w3, "l1a_w3T", "w3")
    wload(w2, "l1a_w2T", "w2")
    for c in range(NTC):
        t0 = c * TC
        rb = C.rbuf[c % 2]
        hx_ = hxk[c % 2]
        P.dma("scalar", lambda e, c=c, hx_=hx_: e.dma_start(
            out=hx_.rearrange("p (k t) -> p k t", k=NP),
            in_=HXv[:, :, c * TC:(c + 1) * TC]),
            r=[("HX", c)], w=[("hxk", c % 2)])
        for p in range(NP):
            P.dma("sync", lambda e, p=p, t0=t0, rb=rb: e.dma_start(
                out=rb[:, p * TC:(p + 1) * TC],
                in_=R_in[p * 128:(p + 1) * 128, t0:t0 + TC]),
                r=[("R", sidx, c)], w=[("rbuf", c % 2, p)])
        for p in range(NP):
            P.op("vector", lambda e, p=p, hx_=hx_: e.tensor_copy(
                sqb2[p], hx_[:, p * TC:(p + 1) * TC]),
                r=[("hxk", c % 2)], w=[("lnsrc", p)])
        ln_stats(C, sqb2, 1e-6, rms_only=True)
        ln_bcast(C, 4, None, rowB=None)
        rsb, trs = S2(2)
        P.op("vector", lambda e, rsb=rsb: e.tensor_copy(rsb, C.pb[4][:, 0:TC]),
             r=[("ps", 4)], w=[trs])
        for p in range(NP):
            a, ta = S2(p % 2)
            P.op("vector", lambda e, p=p, a=a, rsb=rsb, hx_=hx_: e.tensor_mul(
                a, hx_[:, p * TC:(p + 1) * TC], rsb),
                r=[("hxk", c % 2), trs], w=[ta])
            P.op("vector", lambda e, p=p, a=a: e.tensor_scalar(
                nhb[p], a, ffnw_col[:, p:p + 1], None, ALU.mult),
                r=[ta, "ffnwcol"], w=[("nh", p)])
        for p in range(NP):
            b1, b3 = p % 2, 2 + p % 2
            for k in range(NP):
                P.op("tensor", lambda e, k=k, p=p, b1=b1: e.matmul(
                    C.pb[b1][:, 0:TC],
                    w1[:, k * 1024 + p * 128: k * 1024 + p * 128 + 128],
                    nhb[k], start=(k == 0), stop=(k == NP - 1), skip_group_check=True),
                    r=[("nh", k), "w1"], w=[("ps", b1)] if k in (0, NP - 1) else [])
            for k in range(NP):
                P.op("tensor", lambda e, k=k, p=p, b3=b3: e.matmul(
                    C.pb[b3][:, 0:TC],
                    w3[:, k * 1024 + p * 128: k * 1024 + p * 128 + 128],
                    nhb[k], start=(k == 0), stop=(k == NP - 1), skip_group_check=True),
                    r=[("nh", k), "w3"], w=[("ps", b3)] if k in (0, NP - 1) else [])
            a, ta = S2(p % 2)
            P.op("scalar", lambda e, a=a, b1=b1: e.activation(
                a, C.pb[b1][:, 0:TC], AF.Silu),
                r=[("ps", b1)], w=[ta])
            P.op("vector", lambda e, p=p, a=a, b3=b3: e.tensor_mul(
                hhb[p], a, C.pb[b3][:, 0:TC]),
                r=[ta, ("ps", b3)], w=[("hh", p)])
        for p in range(NP):
            b2 = p % 2
            for k in range(NP):
                P.op("tensor", lambda e, k=k, p=p, b2=b2: e.matmul(
                    C.pb[b2][:, 0:TC],
                    w2[:, k * 1024 + p * 128: k * 1024 + p * 128 + 128],
                    hhb[k], start=(k == 0), stop=(k == NP - 1), skip_group_check=True),
                    r=[("hh", k), "w2"], w=[("ps", b2)] if k in (0, NP - 1) else [])
            P.op("vector", lambda e, p=p, b2=b2, hx_=hx_: e.tensor_add(
                hx_[:, p * TC:(p + 1) * TC], hx_[:, p * TC:(p + 1) * TC],
                C.pb[b2][:, 0:TC]),
                r=[("ps", b2), ("hxk", c % 2)], w=[("hxk", c % 2)])
        for p in range(NP):
            P.op("scalar", lambda e, p=p, hx_=hx_: e.copy(
                sqb2[p], hx_[:, p * TC:(p + 1) * TC]),
                r=[("hxk", c % 2)], w=[("lnsrc", p)])
        ln_stats(C, sqb2, 1e-5)
        ln_bcast(C, 4, 5)
        rsb, trs = S2(2)
        nsb, tns = S2(3)
        P.op("vector", lambda e, rsb=rsb: e.tensor_copy(rsb, C.pb[4][:, 0:TC]),
             r=[("ps", 4)], w=[trs])
        P.op("vector", lambda e, nsb=nsb: e.tensor_copy(nsb, C.pb[5][:, 0:TC]),
             r=[("ps", 5)], w=[tns])
        for p in range(NP):
            a, ta = S2(p % 2)
            P.op("vector", lambda e, p=p, a=a, rsb=rsb, hx_=hx_: e.tensor_mul(
                a, hx_[:, p * TC:(p + 1) * TC], rsb),
                r=[("hxk", c % 2), trs], w=[ta])
            P.op("vector", lambda e, a=a, nsb=nsb: e.tensor_add(a, a, nsb),
                 r=[ta, tns], w=[ta])
            P.op("vector", lambda e, p=p, rb=rb, a=a: e.tensor_add(
                rb[:, p * TC:(p + 1) * TC], a, rb[:, p * TC:(p + 1) * TC]),
                r=[ta], w=[("rbuf", c % 2, p)])
            P.dma("sync", lambda e, p=p, t0=t0, rb=rb: e.dma_start(
                out=R_out[p * 128:(p + 1) * 128, t0:t0 + TC],
                in_=rb[:, p * TC:(p + 1) * TC]),
                r=[("rbuf", c % 2, p)], w=[("R", sidx + 1, c)])


def emit_l1f(C, R_in, R_out, sidx):
    P, io, cv = C.P, C.io, C.cv
    Z = io["Zscratch"]
    Zv = Z.rearrange("(k p) t -> p k t", p=128)
    ACC = io["ACCscratch"]
    WG = io["WGATES"]
    NE, F, FH = 8, 4096, 2048

    cv.reset()
    zst = cv.bf(4096)
    zf = cv.f32(4096)
    sqb = [cv.bf(TC) for _ in range(NP)]
    scr = [cv.f32(TC) for _ in range(4)]
    w_em = cv.f32(T)[0:8, :]
    rw_sb = cv.f32(64)
    rbrow = cv.f32(32)[0:1, 0:8]
    rb_sb = cv.f32(32)[0:128, 0:8]
    gm = [cv.f32(32)[0:128, 0:8] for _ in range(2)]
    col = [cv.f32(16)[0:128, 0:1] for _ in range(4)]

    def S(i):
        return scr[i], ("scr", i)

    P.dma("sync", lambda e: e.dma_start(
        out=rw_sb.rearrange("p (k f) -> p k f", k=8),
        in_=io["l1f_rwT"].rearrange("(k p) f -> p k f", p=128)), w=["rw"])
    P.dma("sync", lambda e: e.dma_start(out=rbrow, in_=io["l1f_rb"]), w=["rb"])
    P.op("tensor", lambda e: e.matmul(C.pb[2][:, 0:8], C.ones_row_f32, rbrow,
                                      start=True, stop=True),
         r=["rb", C.ORF], w=[("ps", 2)])
    P.op("vector", lambda e: e.tensor_copy(rb_sb, C.pb[2][:, 0:8]),
         r=[("ps", 2)], w=["rbsb"])

    for c in range(NTC):
        rb = C.rbuf[c % 2]
        for p in range(NP):
            P.dma("sync", lambda e, p=p, c=c, rb=rb: e.dma_start(
                out=rb[:, p * TC:(p + 1) * TC],
                in_=R_in[p * 128:(p + 1) * 128, c * TC:(c + 1) * TC]),
                r=[("R", sidx, c)], w=[("rbuf", c % 2, p)])
        for p in range(NP):
            P.op("vector", lambda e, p=p, rb=rb: e.tensor_copy(
                sqb[p], rb[:, p * TC:(p + 1) * TC]),
                r=[("rbuf", c % 2, p)], w=[("lnsrc", p)])
        ln_stats(C, sqb, 1e-5)
        ln_bcast(C, 4, 5)
        rsb, trs = S(2)
        nsb, tns = S(3)
        P.op("vector", lambda e, rsb=rsb: e.tensor_copy(rsb, C.pb[4][:, 0:TC]),
             r=[("ps", 4)], w=[trs])
        P.op("vector", lambda e, nsb=nsb: e.tensor_copy(nsb, C.pb[5][:, 0:TC]),
             r=[("ps", 5)], w=[tns])
        for p in range(NP):
            a, ta = S(p % 2)
            P.op("vector", lambda e, p=p, rb=rb, a=a, rsb=rsb: e.tensor_mul(
                a, rb[:, p * TC:(p + 1) * TC], rsb),
                r=[("rbuf", c % 2, p), trs], w=[ta])
            P.op("vector", lambda e, p=p, a=a, nsb=nsb: e.tensor_add(
                zf[:, p * TC:(p + 1) * TC], a, nsb),
                r=[ta, tns], w=["zf"])
            P.op("scalar", lambda e, p=p: e.copy(
                zst[:, p * TC:(p + 1) * TC], zf[:, p * TC:(p + 1) * TC]),
                r=["zf"], w=["zst"])
        P.dma("sync", lambda e, c=c: e.dma_start(
            out=Zv[:, :, c * TC:(c + 1) * TC],
            in_=zst.rearrange("p (k t) -> p k t", k=NP)),
            r=["zst"], w=[("Z", c)])
        for tpl in range(4):
            tp = c * 4 + tpl
            for k in range(NP):
                P.op("tensor", lambda e, k=k, tpl=tpl: e.matmul(
                    C.pb[0][:, 0:8],
                    zf[:, k * TC + tpl * 128: k * TC + tpl * 128 + 128],
                    rw_sb[:, k * 8:(k + 1) * 8],
                    start=(k == 0), stop=(k == NP - 1), skip_group_check=True),
                    r=["zf", "rw"], w=[("ps", 0)] if k in (0, NP - 1) else [])
            l_sb, tl = gm[0], ("gm", 0)
            l2, tl2 = gm[1], ("gm", 1)
            nm1, m2c, dcol, ecol = col
            P.op("vector", lambda e, l_sb=l_sb: e.tensor_add(l_sb, C.pb[0][:, 0:8], rb_sb),
                 r=[("ps", 0), "rbsb"], w=[tl])
            P.op("vector", lambda e, nm1=nm1, l_sb=l_sb: e.reduce_max(
                nm1, l_sb, AX.X, negate=True), r=[tl], w=[("col", 0)])
            P.op("vector", lambda e, ecol=ecol, nm1=nm1: e.tensor_scalar_mul(
                ecol, nm1, -1.0), r=[("col", 0)], w=[("col", 3)])
            P.op("vector", lambda e, l2=l2, l_sb=l_sb, ecol=ecol: e.tensor_single_scalar(
                l2, l_sb, ecol, ALU.is_equal), r=[tl, ("col", 3)], w=[tl2])
            P.op("vector", lambda e, l2=l2, l_sb=l_sb: e.tensor_scalar(
                l2, l2, -BIG_NEG, None, ALU.mult), r=[tl2], w=[tl2])
            P.op("vector", lambda e, l2=l2, l_sb=l_sb: e.tensor_sub(l2, l_sb, l2),
                 r=[tl, tl2], w=[tl2])
            P.op("vector", lambda e, m2c=m2c, l2=l2: e.reduce_max(m2c, l2, AX.X),
                 r=[tl2], w=[("col", 1)])
            P.op("scalar", lambda e, dcol=dcol, m2c=m2c, nm1=nm1: e.activation(
                dcol, m2c, AF.Exp, bias=nm1), r=[("col", 0), ("col", 1)], w=[("col", 2)])
            P.op("vector", lambda e, dcol=dcol: e.tensor_scalar_add(dcol, dcol, 1.0),
                 r=[("col", 2)], w=[("col", 2)])
            P.op("vector", lambda e, dcol=dcol: e.reciprocal(dcol, dcol),
                 r=[("col", 2)], w=[("col", 2)])
            P.op("scalar", lambda e, l2=l2, l_sb=l_sb, nm1=nm1: e.activation(
                l2, l_sb, AF.Exp, bias=nm1), r=[tl, ("col", 0)], w=[tl2])
            P.op("vector", lambda e, l_sb=l_sb, m2c=m2c: e.tensor_single_scalar(
                l_sb, l_sb, m2c, ALU.is_ge), r=[tl, ("col", 1)], w=[tl])
            P.op("vector", lambda e, l_sb=l_sb, l2=l2: e.tensor_mul(l_sb, l_sb, l2),
                 r=[tl, tl2], w=[tl])
            P.op("vector", lambda e, l_sb=l_sb, dcol=dcol: e.tensor_single_scalar(
                l_sb, l_sb, dcol, ALU.mult), r=[tl, ("col", 2)], w=[tl])
            P.op("tensor", lambda e, l_sb=l_sb: e.transpose(
                C.pb[1][0:8, 0:128], l_sb, C.ident_f32),
                r=[tl, ("const", "ident_f32")], w=[("ps", 1)])
            P.op("vector", lambda e, tp=tp: e.tensor_copy(
                w_em[:, tp * 128:(tp + 1) * 128], C.pb[1][0:8, 0:128]),
                r=[("ps", 1)], w=["wem"])
    P.dma("sync", lambda e: e.dma_start(out=WG, in_=w_em), r=["wem"], w=["WGd"])
    P.barrier()

    cv.reset()
    gwb = cv.bf(16384)
    uwb = cv.bf(16384)
    dwb = cv.bf(16384)
    hact = cv.bf(8192)
    zcks = [cv.bf(4096) for _ in range(2)]
    wst = cv.bf(T)[0:1, :]
    yout = [cv.f32(TC) for _ in range(2)]
    wev = cv.f32(TC)
    sils = [cv.f32(TC) for _ in range(2)]
    def load_gu(e_, fh):
        r0 = e_ * 1024
        f0 = fh * FH
        for ci in range(4):
            fc = ci * 512
            P.dma("gpsimd", lambda e, r0=r0, f0=f0, fc=fc: e.dma_start(
                out=gwb.rearrange("p (k f) -> p k f", k=8)[:, :, fc:fc + 512],
                in_=io["l1f_gwT"][r0:r0 + 1024, f0 + fc:f0 + fc + 512].rearrange(
                    "(k p) f -> p k f", p=128)), w=[("gw", ci)])
            P.dma("gpsimd", lambda e, r0=r0, f0=f0, fc=fc: e.dma_start(
                out=uwb.rearrange("p (k f) -> p k f", k=8)[:, :, fc:fc + 512],
                in_=io["l1f_uwT"][r0:r0 + 1024, f0 + fc:f0 + fc + 512].rearrange(
                    "(k p) f -> p k f", p=128)), w=[("uw", ci)])

    def load_dw(e_, fh):
        d0 = e_ * 4096 + fh * FH
        P.dma("gpsimd", lambda e, d0=d0: e.dma_start(
            out=dwb.rearrange("p (k f) -> p k f", k=16),
            in_=io["l1f_dwT"][d0:d0 + FH, :].rearrange(
                "(k p) f -> p k f", p=128)), w=["dw"])

    efhs = [(e_, fh) for e_ in range(NE) for fh in range(2)]
    load_gu(0, 0)
    load_dw(0, 0)
    for idx_, (e_, fh) in enumerate(efhs):
        if fh == 0:
            P.dma("gpsimd", lambda e, e_=e_: e.dma_start(out=wst, in_=WG[e_:e_ + 1, :]),
                  r=["WGd"], w=["wst"])
        if True:
            for tc_ in range(NTC):
                t0 = tc_ * TC
                zck = zcks[tc_ % 2]
                P.dma("sync", lambda e, t0=t0, zck=zck: e.dma_start(
                    out=zck.rearrange("p (k t) -> p k t", k=NP),
                    in_=Zv[:, :, t0:t0 + TC]),
                    r=[("Z", tc_)], w=[("zck", tc_ % 2)])
                P.op("tensor", lambda e, t0=t0: e.matmul(
                    C.pb[4][:, 0:TC], C.ones_row_bf, wst[0:1, t0:t0 + TC],
                    start=True, stop=True),
                    r=["wst", C.ORB], w=[("ps", 4)])
                P.op("vector", lambda e: e.tensor_copy(wev, C.pb[4][:, 0:TC]),
                     r=[("ps", 4)], w=["wev"])
                for fp in range(16):
                    bg, bu = fp % 2, 2 + fp % 2
                    ci = fp // 4
                    for k in range(NP):
                        P.op("tensor", lambda e, k=k, fp=fp, bg=bg, zck=zck: e.matmul(
                            C.pb[bg][:, 0:TC],
                            gwb[:, k * FH + fp * 128: k * FH + fp * 128 + 128],
                            zck[:, k * TC:(k + 1) * TC],
                            start=(k == 0), stop=(k == NP - 1), skip_group_check=True),
                            r=[("zck", tc_ % 2), ("gw", ci)],
                            w=[("ps", bg)] if k in (0, NP - 1) else [])
                    for k in range(NP):
                        P.op("tensor", lambda e, k=k, fp=fp, bu=bu, zck=zck: e.matmul(
                            C.pb[bu][:, 0:TC],
                            uwb[:, k * FH + fp * 128: k * FH + fp * 128 + 128],
                            zck[:, k * TC:(k + 1) * TC],
                            start=(k == 0), stop=(k == NP - 1), skip_group_check=True),
                            r=[("zck", tc_ % 2), ("uw", ci)],
                            w=[("ps", bu)] if k in (0, NP - 1) else [])
                    sl_ = sils[fp % 2]
                    P.op("scalar", lambda e, bg=bg, sl_=sl_: e.activation(
                        sl_, C.pb[bg][:, 0:TC], AF.Silu),
                        r=[("ps", bg)], w=[("sil", fp % 2)])
                    P.op("vector", lambda e, fp=fp, bu=bu, sl_=sl_: e.tensor_mul(
                        hact[:, fp * TC:(fp + 1) * TC], sl_, C.pb[bu][:, 0:TC]),
                        r=[("sil", fp % 2), ("ps", bu)], w=[("ha", fp)])
                if tc_ == NTC - 1 and idx_ + 1 < len(efhs):
                    load_gu(*efhs[idx_ + 1])
                for op in range(NP):
                    bo = 5 if op % 2 else 6
                    for k in range(16):
                        P.op("tensor", lambda e, k=k, op=op, bo=bo: e.matmul(
                            C.pb[bo][:, 0:TC],
                            dwb[:, k * 1024 + op * 128: k * 1024 + op * 128 + 128],
                            hact[:, k * TC:(k + 1) * TC],
                            start=(k == 0), stop=(k == 15), skip_group_check=True),
                            r=[("ha", k), "dw"], w=[("ps", bo)] if k in (0, 15) else [])
                    yo = yout[op % 2]
                    P.op("vector", lambda e, yo=yo, bo=bo: e.tensor_mul(
                        yo, C.pb[bo][:, 0:TC], wev),
                        r=[("ps", bo), "wev"], w=[("yo", op % 2)])
                    first_w = e_ == 0 and fh == 0
                    P.dma("gpsimd", lambda e, yo=yo, op=op, t0=t0, first_w=first_w: e.dma_start(
                        out=ACC[op * 128:(op + 1) * 128, t0:t0 + TC], in_=yo,
                        accum_op=(ALU.bypass if first_w else ALU.add)),
                        r=[("yo", op % 2)], w=[("ACC", tc_)])
                if tc_ == NTC - 1 and idx_ + 1 < len(efhs):
                    load_dw(*efhs[idx_ + 1])
    P.barrier()

    cv.reset()
    sqb3 = [cv.bf(TC) for _ in range(NP)]
    scr3 = [cv.f32(TC) for _ in range(4)]
    for c in range(NTC):
        rb = C.rbuf[c % 2]
        ab = C.rbuf[1 - c % 2]
        for p in range(NP):
            P.dma("sync", lambda e, p=p, c=c, rb=rb: e.dma_start(
                out=rb[:, p * TC:(p + 1) * TC],
                in_=R_in[p * 128:(p + 1) * 128, c * TC:(c + 1) * TC]),
                r=[("R", sidx, c)], w=[("rbuf", c % 2, p)])
            P.dma("scalar", lambda e, p=p, c=c, ab=ab: e.dma_start(
                out=ab[:, p * TC:(p + 1) * TC],
                in_=ACC[p * 128:(p + 1) * 128, c * TC:(c + 1) * TC]),
                r=[("ACC", c)], w=[("rbuf", 1 - c % 2, p)])
        for p in range(NP):
            P.op("vector", lambda e, p=p, ab=ab: e.tensor_copy(
                sqb3[p], ab[:, p * TC:(p + 1) * TC]),
                r=[("rbuf", 1 - c % 2, p)], w=[("lnsrc", p)])
        ln_stats(C, sqb3, 1e-5)
        ln_bcast(C, 4, 5)
        rsb, tr3 = scr3[2], ("scr3", 2)
        nsb, tn3 = scr3[3], ("scr3", 3)
        P.op("vector", lambda e, rsb=rsb: e.tensor_copy(rsb, C.pb[4][:, 0:TC]),
             r=[("ps", 4)], w=[tr3])
        P.op("vector", lambda e, nsb=nsb: e.tensor_copy(nsb, C.pb[5][:, 0:TC]),
             r=[("ps", 5)], w=[tn3])
        for p in range(NP):
            a, ta = scr3[p % 2], ("scr3", p % 2)
            P.op("vector", lambda e, p=p, a=a, ab=ab, rsb=rsb: e.tensor_mul(
                a, ab[:, p * TC:(p + 1) * TC], rsb),
                r=[("rbuf", 1 - c % 2, p), tr3], w=[ta])
            P.op("vector", lambda e, a=a, nsb=nsb: e.tensor_add(a, a, nsb),
                 r=[ta, tn3], w=[ta])
            P.op("vector", lambda e, p=p, rb=rb, a=a: e.tensor_add(
                rb[:, p * TC:(p + 1) * TC], a, rb[:, p * TC:(p + 1) * TC]),
                r=[ta], w=[("rbuf", c % 2, p)])
            P.dma("sync", lambda e, p=p, c=c, rb=rb: e.dma_start(
                out=R_out[p * 128:(p + 1) * 128, c * TC:(c + 1) * TC],
                in_=rb[:, p * TC:(p + 1) * TC]),
                r=[("rbuf", c % 2, p)], w=[("R", sidx + 1, c)])


def emit_l0f(C, R_in, R_out, sidx):
    P, io, cv = C.P, C.io, C.cv
    XLH = io["HXscratch"]
    NW = SEQ // 512
    HIDP = 2816
    NK2 = HIDP // 128

    cv.reset()
    sqb = [cv.bf(TC) for _ in range(NP)]
    scr = [cv.f32(TC) for _ in range(4)]
    xf = cv.f32(NP * TC)
    for c in range(NTC):
        rb = C.rbuf[c % 2]
        for p in range(NP):
            P.dma("sync", lambda e, p=p, c=c, rb=rb: e.dma_start(
                out=rb[:, p * TC:(p + 1) * TC],
                in_=R_in[p * 128:(p + 1) * 128, c * TC:(c + 1) * TC]),
                r=[("R", sidx, c)], w=[("rbuf", c % 2, p)])
        for p in range(NP):
            P.op("vector", lambda e, p=p, rb=rb: e.tensor_copy(
                sqb[p], rb[:, p * TC:(p + 1) * TC]),
                r=[("rbuf", c % 2, p)], w=[("lnsrc", p)])
        ln_stats(C, sqb, 1e-5)
        ln_bcast(C, 4, 5)
        rsb, trs = scr[2], ("scr", 2)
        nsb, tns = scr[3], ("scr", 3)
        P.op("vector", lambda e, rsb=rsb: e.tensor_copy(rsb, C.pb[4][:, 0:TC]),
             r=[("ps", 4)], w=[trs])
        P.op("vector", lambda e, nsb=nsb: e.tensor_copy(nsb, C.pb[5][:, 0:TC]),
             r=[("ps", 5)], w=[tns])
        for p in range(NP):
            a, ta = scr[p % 2], ("scr", p % 2)
            P.op("vector", lambda e, p=p, rb=rb, a=a, rsb=rsb: e.tensor_mul(
                a, rb[:, p * TC:(p + 1) * TC], rsb),
                r=[("rbuf", c % 2, p), trs], w=[ta])
            P.op("vector", lambda e, p=p, a=a, nsb=nsb: e.tensor_add(
                xf[:, p * TC:(p + 1) * TC], a, nsb),
                r=[ta, tns], w=["xf"])
        P.dma("sync", lambda e, c=c: e.dma_start(
            out=XLH.rearrange("(k p) t -> p k t", p=128)[:, :, c * TC:(c + 1) * TC],
            in_=xf.rearrange("p (k t) -> p k t", k=NP)),
            r=["xf"], w=[("XLH", c)])
    P.barrier()
    if L0F_TRUNC == "xl":
        return

    for l in range(2):
        cv.reset()
        wqkv = cv.bf(NP * 1536)
        wo_sb = cv.bf(4096)
        tabs = {nm: cv.f32(2048) for nm in ("Aq", "Bq", "Ak", "Bk")}
        mq = [cv.bf(1024) for _ in range(4)]
        mall = cv.bf(512)
        hm2_sb = cv.bf(32)[:, 0:2]
        hbc_sb = cv.f32(128)[0:2, :]
        kb = cv.bf(4096)
        vb = cv.bf(4096)
        qb = cv.bf(2048)
        pn = cv.bf(4096)
        pt_sb = cv.bf(4096)
        ob = cv.bf(2048)
        hb = cv.bf(4096)
        sqfs = [cv.bf(512) for _ in range(2)]
        scr = [cv.f32(TC) for _ in range(4)]
        r2as = [cv.f32(TC)[0:2, :] for _ in range(2)]
        colc = cv.f32(32)
        halo = cv.f32(32)[:, 0:8]
        qs_col = cv.f32(32)[:, 0:1]
        ks_col = cv.f32(32)[:, 0:1]

        def wload3(dst, name, tok, nk):
            P.dma("gpsimd", lambda e, dst=dst, name=name, nk=nk: e.dma_start(
                out=dst.rearrange("p (k f) -> p k f", k=nk),
                in_=io[name].rearrange("(k p) f -> p k f", p=128)), w=[tok])

        wload3(wqkv, f"l0f_wqkv{l}", "wqkv", NP)
        wload3(wo_sb, f"l0f_wo{l}", "wo", 4)
        for nm in ("Aq", "Bq", "Ak", "Bk"):
            P.dma("sync", lambda e, nm=nm: e.dma_start(out=tabs[nm][:, 0:SEQ],
                                                       in_=io["l0f_" + nm]),
                  w=[("tab", nm)])
        for qt in range(4):
            P.dma("sync", lambda e, qt=qt: e.dma_start(out=mq[qt], in_=io[f"l0f_mask{qt}"]),
                  w=[("mq", qt)])
        P.dma("sync", lambda e: e.dma_start(out=mall, in_=io["l0f_mask_all"]), w=["mall"])
        P.dma("sync", lambda e: e.dma_start(out=hm2_sb, in_=io["l0f_hm2"]), w=["hm2"])
        P.dma("sync", lambda e: e.dma_start(out=hbc_sb, in_=io["l0f_hbc"]), w=["hbc"])
        P.dma("sync", lambda e, l=l: e.dma_start(out=qs_col, in_=io[f"l0f_qs{l}"]), w=["qsc"])
        P.dma("sync", lambda e, l=l: e.dma_start(out=ks_col, in_=io[f"l0f_ks{l}"]), w=["ksc"])
        P.op("gpsimd", lambda e: e.memset(kb, 0.0), w=[("kb", 0, 0), ("kb", 0, 1),
                                                       ("kb", 0, 2), ("kb", 0, 3),
                                                       ("kb", 1, 0), ("kb", 1, 1),
                                                       ("kb", 1, 2), ("kb", 1, 3)])
        P.op("gpsimd", lambda e: e.memset(vb, 0.0), w=[("vb", 0, 0), ("vb", 0, 1),
                                                       ("vb", 0, 2), ("vb", 0, 3),
                                                       ("vb", 1, 0), ("vb", 1, 1),
                                                       ("vb", 1, 2), ("vb", 1, 3)])

        def qk_proc(which, fp, bank, slot, woff, tc):
            A = tabs["Aq" if which == "q" else "Ak"]
            B = tabs["Bq" if which == "q" else "Bk"]
            sc_col = qs_col if which == "q" else ks_col
            sct = "qsc" if which == "q" else "ksc"
            if which == "q":
                dst, dtok = qb[:, fp * 512:(fp + 1) * 512], ("qb", fp)
            else:
                dst = kb[:, (slot * 4 + fp) * 512:(slot * 4 + fp + 1) * 512]
                dtok = ("kb", slot, fp)
            par = fp % 2
            sqf_ = sqfs[par]
            r2a_ = r2as[par]
            bn2 = 6 if par == 0 else 3
            bbc = 5 if par == 0 else 4
            P.op("scalar", lambda e, bank=bank, sqf_=sqf_: e.activation(
                sqf_, C.pb[bank][:, 0:TC], AF.Square),
                r=[("ps", bank)], w=[("sqf", par)])
            P.op("tensor", lambda e, bn2=bn2, sqf_=sqf_: e.matmul(
                C.pb[bn2][0:2, 0:TC], hm2_sb, sqf_, start=True, stop=True),
                r=[("sqf", par), "hm2"], w=[("ps", bn2)])
            P.op("scalar", lambda e, bn2=bn2, r2a_=r2a_: e.activation(
                r2a_, C.pb[bn2][0:2, 0:TC], AF.Sqrt),
                r=[("ps", bn2)], w=[("r2a", par)])
            P.op("vector", lambda e, r2a_=r2a_: e.tensor_scalar_max(r2a_, r2a_, 1e-12),
                 r=[("r2a", par)], w=[("r2a", par)])
            P.op("vector", lambda e, r2a_=r2a_: e.reciprocal(r2a_, r2a_),
                 r=[("r2a", par)], w=[("r2a", par)])
            P.op("tensor", lambda e, bbc=bbc, r2a_=r2a_: e.matmul(
                C.pb[bbc][:, 0:TC], hbc_sb, r2a_, start=True, stop=True),
                r=[("r2a", par), "hbc"], w=[("ps", bbc)])
            a, ta = scr[par * 2], ("scr", par * 2)
            b, tb = scr[par * 2 + 1], ("scr", par * 2 + 1)
            P.op("vector", lambda e, bank=bank, sc_col=sc_col, a=a: e.tensor_scalar(
                a, C.pb[bank][:, 0:TC], sc_col, None, ALU.mult),
                r=[("ps", bank), sct], w=[ta])
            mask = []
            for i in range(16):
                mask += [2 * i + 1, 2 * i]
            P.op("vector", lambda e, a=a, b=b, mask=mask: e.stream_shuffle(b, a, mask),
                 r=[ta], w=[tb])
            P.op("vector", lambda e, a=a, A=A, woff=woff: e.tensor_mul(
                a, a, A[:, woff:woff + 512]),
                r=[ta, ("tab", "Aq"), ("tab", "Ak")], w=[ta])
            P.op("vector", lambda e, b=b, B=B, woff=woff: e.tensor_mul(
                b, b, B[:, woff:woff + 512]),
                r=[tb, ("tab", "Bq"), ("tab", "Bk")], w=[tb])
            P.op("vector", lambda e, a=a, b=b: e.tensor_add(a, a, b),
                 r=[ta, tb], w=[ta])
            P.op("vector", lambda e, a=a, dst=dst, bbc=bbc: e.tensor_mul(
                dst, a, C.pb[bbc][:, 0:TC]),
                r=[ta, ("ps", bbc)], w=[dtok])

        for r_ in range(RPC):
            for w_ in range(NW):
                tc = r_ * NW + w_
                slot = tc % 2
                pslot = 1 - slot
                woff = w_ * 512
                tok0 = tc * TC
                rb = C.rbuf[slot]
                for p in range(NP):
                    P.dma("sync", lambda e, p=p, tok0=tok0, rb=rb: e.dma_start(
                        out=rb[:, p * TC:(p + 1) * TC],
                        in_=XLH[p * 128:(p + 1) * 128, tok0:tok0 + TC]),
                        r=[("XLH", tc)], w=[("rbuf", slot, p)])
                if w_ == 0:
                    P.op("vector", lambda e: e.memset(halo, 0.0), w=["halo"])
                for p in range(NP):
                    if p < 4:
                        P.op("scalar", lambda e, p=p, rb=rb: e.copy(
                            hb[:, p * 512:(p + 1) * 512], rb[:, p * TC:(p + 1) * TC]),
                            r=[("rbuf", slot, p)], w=[("hb", p)])
                    else:
                        P.op("vector", lambda e, p=p, rb=rb: e.tensor_copy(
                            hb[:, p * 512 + 1:(p + 1) * 512],
                            rb[:, p * TC:(p + 1) * TC - 1]),
                            r=[("rbuf", slot, p)], w=[("hb", p)])
                        P.op("vector", lambda e, p=p: e.tensor_copy(
                            hb[:, p * 512:p * 512 + 1], halo[:, p:p + 1]),
                            r=["halo"], w=[("hb", p)])
                for p in range(NP):
                    P.op("vector", lambda e, p=p, rb=rb: e.tensor_copy(
                        halo[:, p:p + 1], rb[:, (p + 1) * TC - 1:(p + 1) * TC]),
                        r=[("rbuf", slot, p)], w=["halo"])
                for which, cbase in (("q", 0), ("k", 512)):
                    for fp in range(4):
                        bank = fp % 2
                        for k in range(NP):
                            P.op("tensor", lambda e, k=k, fp=fp, bank=bank, cbase=cbase: e.matmul(
                                C.pb[bank][:, 0:TC],
                                wqkv[:, k * 1536 + cbase + fp * 128:
                                     k * 1536 + cbase + fp * 128 + 128],
                                hb[:, k * 512:(k + 1) * 512],
                                start=(k == 0), stop=(k == NP - 1),
                                skip_group_check=True),
                                r=[("hb", k), "wqkv"],
                                w=[("ps", bank)] if k in (0, NP - 1) else [])
                        qk_proc(which, fp, bank, slot, woff, tc)
                for ts in range(4):
                    bank = 2 + ts % 2
                    for k in range(NP):
                        P.op("tensor", lambda e, k=k, ts=ts, bank=bank: e.matmul(
                            C.pb[bank][:, 0:TC],
                            hb[:, k * 512 + ts * 128: k * 512 + ts * 128 + 128],
                            wqkv[:, k * 1536 + 1024: k * 1536 + 1536],
                            start=(k == 0), stop=(k == NP - 1), skip_group_check=True),
                            r=[("hb", k), "wqkv"],
                            w=[("ps", bank)] if k in (0, NP - 1) else [])
                    P.op("vector", lambda e, ts=ts, bank=bank, slot=slot: e.tensor_copy(
                        vb[:, (slot * 4 + ts) * 512:(slot * 4 + ts + 1) * 512],
                        C.pb[bank][:, 0:TC]),
                        r=[("ps", bank)], w=[("vb", slot, ts)])
                for h in range(8):
                    hp = h // 2
                    hb_ = 64 * (h % 2)
                    for qt in range(4):
                        bp = 2 * (qt % 2)
                        bo = bp + 1
                        prev_mask = mall if w_ == 0 else mq[qt][:, 0:512]
                        prev_mtok = "mall" if w_ == 0 else ("mq", qt)
                        P.op("tensor", lambda e, bp=bp, prev_mask=prev_mask: e.matmul(
                            C.pb[bp][:, 0:TC], C.ident_bf, prev_mask,
                            start=True, stop=False, skip_group_check=True),
                            r=[prev_mtok, ("const", "ident_bf")], w=[("ps", bp)])
                        P.op("tensor", lambda e, bo=bo, qt=qt: e.matmul(
                            C.pb[bo][:, 0:TC], C.ident_bf, mq[qt][:, 512:1024],
                            start=True, stop=False, skip_group_check=True),
                            r=[("mq", qt), ("const", "ident_bf")], w=[("ps", bo)])
                        for sl, bnk in ((pslot, bp), (slot, bo)):
                            P.op("tensor", lambda e, h=h, hp=hp, hb_=hb_, qt=qt, sl=sl, bnk=bnk: e.matmul(
                                C.pb[bnk][:, 0:TC],
                                qb[hb_:hb_ + 64, hp * 512 + qt * 128: hp * 512 + qt * 128 + 128],
                                kb[hb_:hb_ + 64, (sl * 4 + hp) * 512:(sl * 4 + hp + 1) * 512],
                                start=False, stop=True, skip_group_check=True),
                                r=[("qb", hp), ("kb", sl, hp)], w=[("ps", bnk)])
                        nm1 = colc[:, 0:1]
                        nm2 = colc[:, 1:2]
                        s0 = colc[:, 2:3]
                        s1 = colc[:, 3:4]
                        P.op("vector", lambda e, bp=bp, nm1=nm1: e.reduce_max(
                            nm1, C.pb[bp][:, 0:TC], AX.X, negate=True),
                            r=[("ps", bp)], w=[("col", 0)])
                        P.op("vector", lambda e, bo=bo, nm2=nm2: e.reduce_max(
                            nm2, C.pb[bo][:, 0:TC], AX.X, negate=True),
                            r=[("ps", bo)], w=[("col", 1)])
                        P.op("vector", lambda e, nm1=nm1, nm2=nm2: e.tensor_tensor(
                            nm1, nm1, nm2, ALU.min),
                            r=[("col", 0), ("col", 1)], w=[("col", 0)])
                        P.op("scalar", lambda e, bp=bp, qt=qt, nm1=nm1, s0=s0: e.activation(
                            pn[:, qt * 1024: qt * 1024 + 512], C.pb[bp][:, 0:TC],
                            AF.Exp, bias=nm1, accum_out=s0),
                            r=[("ps", bp), ("col", 0)], w=[("pn", qt), ("col", 2)])
                        P.op("scalar", lambda e, bo=bo, qt=qt, nm1=nm1, s1=s1: e.activation(
                            pn[:, qt * 1024 + 512: qt * 1024 + 1024], C.pb[bo][:, 0:TC],
                            AF.Exp, bias=nm1, accum_out=s1),
                            r=[("ps", bo), ("col", 0)], w=[("pn", qt), ("col", 3)])
                        P.op("vector", lambda e, s0=s0, s1=s1: e.tensor_add(s0, s0, s1),
                             r=[("col", 2), ("col", 3)], w=[("col", 2)])
                        P.op("vector", lambda e, s0=s0: e.reciprocal(s0, s0),
                             r=[("col", 2)], w=[("col", 2)])
                        P.op("vector", lambda e, qt=qt, s0=s0: e.tensor_scalar(
                            pn[:, qt * 1024:(qt + 1) * 1024],
                            pn[:, qt * 1024:(qt + 1) * 1024], s0, None, ALU.mult),
                            r=[("pn", qt), ("col", 2)], w=[("pn", qt)])
                    for j in range(8):
                        for qt in range(4):
                            P.op("tensor", lambda e, j=j, qt=qt: e.transpose(
                                C.pt[:, qt * 128:(qt + 1) * 128],
                                pn[:, qt * 1024 + j * 128: qt * 1024 + j * 128 + 128],
                                C.ident_bf),
                                r=[("pn", qt), ("const", "ident_bf")],
                                w=[("pt",)] if qt in (0, 3) else [])
                        P.op("vector", lambda e, j=j: e.tensor_copy(
                            pt_sb[:, j * 512:(j + 1) * 512], C.pt[:, 0:512]),
                            r=[("pt",)], w=[("ptsb", j)])
                    bo2 = 4 + hp % 2
                    for j in range(8):
                        sl = pslot if j < 4 else slot
                        ts = j % 4
                        P.op("tensor", lambda e, j=j, sl=sl, ts=ts, h=h, hb_=hb_, bo2=bo2: e.matmul(
                            C.pb[bo2][hb_:hb_ + 64, 0:TC],
                            vb[:, (sl * 4 + ts) * 512 + 64 * h:
                               (sl * 4 + ts) * 512 + 64 * h + 64],
                            pt_sb[:, j * 512:(j + 1) * 512],
                            start=(j == 0), stop=(j == 7), skip_group_check=True),
                            r=[("vb", sl, ts), ("ptsb", j)],
                            w=[("ps", bo2)] if j in (0, 7) else [])
                    if h % 2 == 1:
                        P.op("vector", lambda e, hp=hp, bo2=bo2: e.tensor_copy(
                            ob[:, hp * 512:(hp + 1) * 512], C.pb[bo2][:, 0:TC]),
                            r=[("ps", bo2)], w=[("ob", hp)])
                for fp in range(NP):
                    bank = fp % 2
                    for k in range(4):
                        P.op("tensor", lambda e, k=k, fp=fp, bank=bank: e.matmul(
                            C.pb[bank][:, 0:TC],
                            wo_sb[:, k * 1024 + fp * 128: k * 1024 + fp * 128 + 128],
                            ob[:, k * 512:(k + 1) * 512],
                            start=(k == 0), stop=(k == 3), skip_group_check=True),
                            r=[("ob", k), "wo"],
                            w=[("ps", bank)] if k in (0, 3) else [])
                    P.op("vector", lambda e, fp=fp, bank=bank, rb=rb: e.tensor_add(
                        rb[:, fp * TC:(fp + 1) * TC], rb[:, fp * TC:(fp + 1) * TC],
                        C.pb[bank][:, 0:TC]),
                        r=[("ps", bank)], w=[("rbuf", slot, fp)])
                    P.dma("sync", lambda e, fp=fp, tok0=tok0, rb=rb: e.dma_start(
                        out=XLH[fp * 128:(fp + 1) * 128, tok0:tok0 + TC],
                        in_=rb[:, fp * TC:(fp + 1) * TC]),
                        r=[("rbuf", slot, fp)], w=[("XLH", tc)])
        if DEBUG_TAPS and L0F_TRUNC == f"att{l}":
            P.barrier()
            for nm, buf in (("dbg_hb", hb), ("dbg_qb", qb), ("dbg_kb", kb),
                            ("dbg_vb", vb), ("dbg_pn", pn), ("dbg_ob", ob),
                            ("dbg_pt", pt_sb)):
                P.dma("gpsimd", lambda e, nm=nm, buf=buf: e.dma_start(
                    out=io[nm], in_=buf), w=[("dbg", nm)])
        P.barrier()
        if L0F_TRUNC == f"att{l}":
            return

        cv.reset()
        w1a_sb = cv.bf(NP * HIDP)
        w1g_sb = cv.bf(NP * HIDP)
        w2c = [cv.bf(HIDP) for _ in range(2)]
        hn = cv.bf(4096)
        hh = cv.bf(NK2 * 512)
        sqb2 = [cv.bf(TC) for _ in range(NP)]
        hfc = cv.f32(NP * TC)
        scr2 = [cv.f32(TC) for _ in range(4)]
        gel = scr2[3]
        lng = cv.f32(16)[:, 0:8]
        lnb = cv.f32(16)[:, 0:8]
        b1a = cv.f32(NK2)[:, 0:NK2]
        b1g = cv.f32(NK2)[:, 0:NK2]
        b2c = cv.f32(16)[:, 0:8]
        halo2 = cv.f32(32)[:, 0:8]
        wload3(w1a_sb, f"l0f_w1a{l}", "w1a", NP)
        wload3(w1g_sb, f"l0f_w1g{l}", "w1g", NP)
        for nm, dst in (("lng", lng), ("lnb", lnb), ("b1a", b1a), ("b1g", b1g),
                        ("b2", b2c)):
            P.dma("sync", lambda e, nm=nm, dst=dst, l=l: e.dma_start(
                out=dst, in_=io[f"l0f_{nm}{l}"]), w=[nm])
        for tc in range(NTC):
            t0 = tc * TC
            rb = C.rbuf[0]
            rb2 = C.rbuf[1]
            for p in range(NP):
                P.dma("sync", lambda e, p=p, t0=t0, rb=rb: e.dma_start(
                    out=rb[:, p * TC:(p + 1) * TC],
                    in_=XLH[p * 128:(p + 1) * 128, t0:t0 + TC]),
                    r=[("XLH", tc)], w=[("rbuf", 0, p)])
                if l == 1:
                    P.dma("scalar", lambda e, p=p, t0=t0, rb2=rb2: e.dma_start(
                        out=rb2[:, p * TC:(p + 1) * TC],
                        in_=R_in[p * 128:(p + 1) * 128, t0:t0 + TC]),
                        r=[("R", sidx, tc)], w=[("rbuf", 1, p)])
            if tc % CPR == 0:
                P.op("vector", lambda e: e.memset(halo2, 0.0), w=["halo2"])
            for p in range(NP):
                if p < 4:
                    P.op("scalar", lambda e, p=p, rb=rb: e.copy(
                        hfc[:, p * TC:(p + 1) * TC], rb[:, p * TC:(p + 1) * TC]),
                        r=[("rbuf", 0, p)], w=[("hfc", p)])
                else:
                    P.op("vector", lambda e, p=p, rb=rb: e.tensor_copy(
                        hfc[:, p * TC + 1:(p + 1) * TC],
                        rb[:, p * TC:(p + 1) * TC - 1]),
                        r=[("rbuf", 0, p)], w=[("hfc", p)])
                    P.op("vector", lambda e, p=p: e.tensor_copy(
                        hfc[:, p * TC:p * TC + 1], halo2[:, p:p + 1]),
                        r=["halo2"], w=[("hfc", p)])
            for p in range(NP):
                P.op("vector", lambda e, p=p, rb=rb: e.tensor_copy(
                    halo2[:, p:p + 1], rb[:, (p + 1) * TC - 1:(p + 1) * TC]),
                    r=[("rbuf", 0, p)], w=["halo2"])
            for p in range(NP):
                P.op("scalar", lambda e, p=p: e.copy(
                    sqb2[p], hfc[:, p * TC:(p + 1) * TC]),
                    r=[("hfc", p)], w=[("lnsrc", p)])
            ln_stats(C, sqb2, 1e-5)
            ln_bcast(C, 4, 5)
            rsb, trs = scr2[2], ("scr2", 2)
            nsb, tns = scr2[3], ("scr2", 3)
            P.op("vector", lambda e, rsb=rsb: e.tensor_copy(rsb, C.pb[4][:, 0:TC]),
                 r=[("ps", 4)], w=[trs])
            P.op("vector", lambda e, nsb=nsb: e.tensor_copy(nsb, C.pb[5][:, 0:TC]),
                 r=[("ps", 5)], w=[tns])
            for p in range(NP):
                a, ta = scr2[p % 2], ("scr2", p % 2)
                P.op("vector", lambda e, p=p, a=a, rsb=rsb: e.tensor_mul(
                    a, hfc[:, p * TC:(p + 1) * TC], rsb),
                    r=[("hfc", p), trs], w=[ta])
                P.op("vector", lambda e, a=a, nsb=nsb: e.tensor_add(a, a, nsb),
                     r=[ta, tns], w=[ta])
                P.op("vector", lambda e, p=p, a=a: e.tensor_scalar(
                    hn[:, p * TC:(p + 1) * TC], a, lng[:, p:p + 1], lnb[:, p:p + 1],
                    ALU.mult, ALU.add),
                    r=[ta, "lng", "lnb"], w=[("hn", p)])
            for fp in range(NK2):
                ba, bg = fp % 2, 2 + fp % 2
                for k in range(NP):
                    P.op("tensor", lambda e, k=k, fp=fp, ba=ba: e.matmul(
                        C.pb[ba][:, 0:TC],
                        w1a_sb[:, k * HIDP + fp * 128: k * HIDP + fp * 128 + 128],
                        hn[:, k * TC:(k + 1) * TC],
                        start=(k == 0), stop=(k == NP - 1), skip_group_check=True),
                        r=[("hn", k), "w1a"], w=[("ps", ba)] if k in (0, NP - 1) else [])
                for k in range(NP):
                    P.op("tensor", lambda e, k=k, fp=fp, bg=bg: e.matmul(
                        C.pb[bg][:, 0:TC],
                        w1g_sb[:, k * HIDP + fp * 128: k * HIDP + fp * 128 + 128],
                        hn[:, k * TC:(k + 1) * TC],
                        start=(k == 0), stop=(k == NP - 1), skip_group_check=True),
                        r=[("hn", k), "w1g"], w=[("ps", bg)] if k in (0, NP - 1) else [])
                P.op("scalar", lambda e, fp=fp, bg=bg: e.activation(
                    gel, C.pb[bg][:, 0:TC], AF.Gelu, bias=b1g[:, fp:fp + 1]),
                    r=[("ps", bg), "b1g"], w=[("scr2", 3)])
                P.op("vector", lambda e, fp=fp, ba=ba: e.scalar_tensor_tensor(
                    hh[:, fp * 512:(fp + 1) * 512], C.pb[ba][:, 0:TC],
                    b1a[:, fp:fp + 1], gel, ALU.add, ALU.mult),
                    r=[("ps", ba), "b1a", ("scr2", 3)], w=[("hh", fp)])
            for op in range(NP):
                b2k = 4 + op % 2
                wc_ = w2c[op % 2]
                P.dma("gpsimd", lambda e, op=op, wc_=wc_, l=l: e.dma_start(
                    out=wc_.rearrange("p (k f) -> p k f", k=NK2),
                    in_=io[f"l0f_w2{l}"][:, op * 128:(op + 1) * 128].rearrange(
                        "(k p) f -> p k f", p=128)),
                    w=[("w2c", op % 2)])
                for k in range(NK2):
                    P.op("tensor", lambda e, k=k, op=op, b2k=b2k, wc_=wc_: e.matmul(
                        C.pb[b2k][:, 0:TC],
                        wc_[:, k * 128:(k + 1) * 128],
                        hh[:, k * 512:(k + 1) * 512],
                        start=(k == 0), stop=(k == NK2 - 1), skip_group_check=True),
                        r=[("hh", k), ("w2c", op % 2)],
                        w=[("ps", b2k)] if k in (0, NK2 - 1) else [])
                P.op("vector", lambda e, op=op, b2k=b2k, rb=rb: e.scalar_tensor_tensor(
                    rb[:, op * TC:(op + 1) * TC], C.pb[b2k][:, 0:TC],
                    b2c[:, op:op + 1], rb[:, op * TC:(op + 1) * TC],
                    ALU.add, ALU.add),
                    r=[("ps", b2k), "b2", ("rbuf", 0, op)], w=[("rbuf", 0, op)])
                if l == 0:
                    P.dma("sync", lambda e, op=op, t0=t0, rb=rb: e.dma_start(
                        out=XLH[op * 128:(op + 1) * 128, t0:t0 + TC],
                        in_=rb[:, op * TC:(op + 1) * TC]),
                        r=[("rbuf", 0, op)], w=[("XLH", tc)])
            if l == 1:
                for p in range(NP):
                    P.op("scalar", lambda e, p=p, rb=rb: e.copy(
                        sqb2[p], rb[:, p * TC:(p + 1) * TC]),
                        r=[("rbuf", 0, p)], w=[("lnsrc", p)])
                ln_stats(C, sqb2, 1e-5)
                ln_bcast(C, 4, 5)
                P.op("vector", lambda e, rsb=rsb: e.tensor_copy(rsb, C.pb[4][:, 0:TC]),
                     r=[("ps", 4)], w=[trs])
                P.op("vector", lambda e, nsb=nsb: e.tensor_copy(nsb, C.pb[5][:, 0:TC]),
                     r=[("ps", 5)], w=[tns])
                for p in range(NP):
                    a, ta = scr2[p % 2], ("scr2", p % 2)
                    P.op("vector", lambda e, p=p, a=a, rsb=rsb, rb=rb: e.tensor_mul(
                        a, rb[:, p * TC:(p + 1) * TC], rsb),
                        r=[("rbuf", 0, p), trs], w=[ta])
                    P.op("vector", lambda e, a=a, nsb=nsb: e.tensor_add(a, a, nsb),
                         r=[ta, tns], w=[ta])
                    P.op("vector", lambda e, p=p, a=a, rb2=rb2: e.tensor_add(
                        rb2[:, p * TC:(p + 1) * TC], a, rb2[:, p * TC:(p + 1) * TC]),
                        r=[ta], w=[("rbuf", 1, p)])
                    P.dma("sync", lambda e, p=p, t0=t0, rb2=rb2: e.dma_start(
                        out=R_out[p * 128:(p + 1) * 128, t0:t0 + TC],
                        in_=rb2[:, p * TC:(p + 1) * TC]),
                        r=[("rbuf", 1, p)], w=[("R", sidx + 1, tc)])
        P.barrier()
        if L0F_TRUNC == f"ffn{l}":
            return


SLOT_EMITTERS = {"l0a": emit_l0a, "l1a": emit_l1a, "l1f": emit_l1f,
                 "l0f": emit_l0f}


def build_program(g, slots):
    nc = bass.Bass("TRN2", target_bir_lowering=False)
    stack = ExitStack()
    io = {}
    io["x"] = nc.dram_tensor("x", [D, T], F32, kind="ExternalInput").ap()
    io["out"] = nc.dram_tensor("out", [D, T], F32, kind="ExternalOutput").ap()
    if DEBUG_TAPS:
        for nm, shp in (("dbg_hb", 4096), ("dbg_qb", 2048), ("dbg_kb", 4096),
                        ("dbg_vb", 4096), ("dbg_pn", 4096), ("dbg_ob", 2048),
                        ("dbg_pt", 4096)):
            io[nm] = nc.dram_tensor(nm, [128, shp], BF16, kind="ExternalOutput").ap()
        io["dbg_z"] = nc.dram_tensor("dbg_z", [D, T], BF16, kind="ExternalOutput").ap()
        io["dbg_kv"] = nc.dram_tensor("dbg_kv", [256, 512], BF16, kind="ExternalOutput").ap()
        io["dbg_q"] = nc.dram_tensor("dbg_q", [D, TC], BF16, kind="ExternalOutput").ap()
        io["dbg_o"] = nc.dram_tensor("dbg_o", [D, TC], BF16, kind="ExternalOutput").ap()
        io["dbg_s"] = nc.dram_tensor("dbg_s", [D, TC], F32, kind="ExternalOutput").ap()
    names = list(CONST_INPUTS)
    for s in slots:
        names += list(SLOT_INPUTS[s])
    for n in names:
        a = np.asarray(g[n])
        dt = BF16 if a.dtype == BF else F32
        io[n] = nc.dram_tensor(n, list(a.shape), dt, kind="ExternalInput").ap()
    R = [io["x"]]
    for i in range(len(slots) - 1):
        R.append(nc.dram_tensor(f"Rtmp{i}", [D, T], F32, kind="Internal").ap())
    R.append(io["out"])
    io["Zscratch"] = nc.dram_tensor("Zscratch", [D, T], BF16, kind="Internal").ap()
    io["HXscratch"] = nc.dram_tensor("HXscratch", [D, T], F32, kind="Internal").ap()
    io["ACCscratch"] = nc.dram_tensor("ACCscratch", [D, T], F32, kind="Internal").ap()
    io["WGATES"] = nc.dram_tensor("WGATES", [8, T], F32, kind="Internal").ap()

    P = Prog(nc, stack)
    C = setup_ctx(P, nc, stack, io)
    for i, s in enumerate(slots):
        SLOT_EMITTERS[s](C, R[i], R[i + 1], i)
        P.barrier()
    with nc.Block() as block:
        P.finalize_and_emit(block)
    stack.close()
    return nc, names


def run_slots(g, shards, slot_groups):
    cur = shards
    for grp in slot_groups:
        nc, names = build_program(g, grp)
        in_maps = []
        for c in range(NCORES):
            m = {n: np.asarray(g[n]) for n in names}
            m["x"] = cur[c]
            in_maps.append(m)
        res = run_bass_kernel_spmd(nc, in_maps, core_ids=list(range(NCORES)))
        if DEBUG_TAPS:
            globals()["LAST_RESULTS"] = res.results
        cur = [np.asarray(res.results[c]["out"]) for c in range(NCORES)]
    return cur


def assemble(cur):
    out = np.empty((NCORES * RPC, SEQ, D), np.float32)
    for c in range(NCORES):
        out[RPC * c:RPC * c + RPC] = np.asarray(cur[c], np.float32).reshape(
            D, RPC, SEQ).transpose(1, 2, 0)
    return out


def kernel(**inputs):
    g, shards = prep_inputs(inputs)
    cur = run_slots(g, shards, [["l0a", "l0f", "l1a", "l1f"]])
    return assemble(cur)
